# version 09
# baseline (speedup 1.0000x reference)
import numpy as np
import concourse.bass as bass
import concourse.mybir as mybir
from concourse.bass_utils import run_bass_kernel_spmd

F32 = mybir.dt.float32
BF16 = mybir.dt.bfloat16
AF = mybir.ActivationFunctionType
ALU = mybir.AluOpType
AX = mybir.AxisListType

T_OWN, HALO = 1024, 256
T_ALL = T_OWN + HALO
D, KD, F, CF, E, R, NH, DH = 2048, 16, 5632, 44, 8, 16, 32, 64
NT, NQ, NSLOT = T_ALL // 128, T_OWN // 128, 3
M_AL = 2.0 ** (-0.25)
BLK1 = [(0, 512), (512, 512), (1024, 256)]
BLK2 = [(256, 512), (768, 512)]
NEGM = -30000.0




class _InstRec:
    __slots__ = ("chain",)

    def __init__(self):
        self.chain = []

    def then_inc(self, sem, val):
        self.chain.append(("then_inc", (sem, val)))
        return self


class _EngRec:
    __slots__ = ("calls",)

    def __init__(self):
        self.calls = []

    def __getattr__(self, name):
        def _call(*a, **k):
            inst = _InstRec()
            self.calls.append((name, a, k, inst))
            return inst
        return _call

    def replay(self, eng):
        for name, a, k, inst in self.calls:
            real = getattr(eng, name)(*a, **k)
            for cn, ca in inst.chain:
                real = getattr(real, cn)(*ca)


class Led:
    def __init__(self):
        self.c = {}

    def inc(self, s, n=1):
        self.c[s.name] = self.c.get(s.name, 0) + n
        return self.c[s.name]

    def val(self, s):
        return self.c.get(s.name, 0)


def build_program():
    nc = bass.Bass("TRN2", target_bir_lowering=False, debug=False)
    led = Led()

    def inp(name, shape):
        return nc.dram_tensor(name, shape, F32, kind="ExternalInput").ap()

    x0 = inp("x0", [D, T_ALL])
    w13 = inp("w13", [CF, 128, 4096])
    b13 = inp("b13", [CF, 16, 2048])
    a13 = inp("a13", [128, 4096])
    a2pi = inp("a2pi", [128, CF * 128])
    b2pi = inp("b2pi", [16, KD * E * 128])
    gwt = inp("gwt", [128, KD * E])
    wq = inp("wq", [48, 128, 2048])
    wp = inp("wp", [KD, 128, 2048])
    w2r = inp("w2r", [KD, 128, F])
    battn = inp("battn", [128, 48])
    bproj = inp("bproj", [128, KD])
    stb = inp("stb", [128, NQ * NSLOT])
    qaug = inp("qaug", [2, NQ * 128])
    mkrow = inp("mkrow", [2, T_ALL])
    ident = inp("ident", [128, 128])
    tri = inp("tri", [128, 128])
    trit = inp("trit", [128, 128])
    onesf = inp("onesf", [1, 128])

    y = nc.dram_tensor("y", [D, T_OWN], F32, kind="ExternalOutput").ap()

    xres = nc.dram_tensor("xres", [D, T_ALL], F32).ap()
    w13b = nc.dram_tensor("w13b", [CF, 128, 4096], BF16).ap()
    b13bd = nc.dram_tensor("b13bd", [CF, 16, 2048], BF16).ap()
    wqb = nc.dram_tensor("wqb", [48, 128, 2048], BF16).ap()
    wpb = nc.dram_tensor("wpb", [KD, 128, 2048], BF16).ap()
    w2rb = nc.dram_tensor("w2rb", [KD, 128, F], BF16).ap()
    b2pb = nc.dram_tensor("b2pb", [16, KD * E * 128], BF16).ap()
    qd = nc.dram_tensor("qd", [D, T_ALL], BF16).ap()
    kdr = nc.dram_tensor("kdr", [66, NH * T_ALL], BF16).ap()
    vdr = nc.dram_tensor("vdr", [T_ALL, NH * DH], BF16).ap()
    amx = nc.dram_tensor("amx", [CF, 128, T_ALL], BF16).ap()

    import contextlib
    with contextlib.ExitStack() as _st:
        xn_all = _st.enter_context(nc.sbuf_tensor("xn_all", [128, KD * T_ALL], BF16))
        REGA = _st.enter_context(nc.sbuf_tensor("REGA", [128, 2 * E * T_ALL], BF16))
        REGB = _st.enter_context(nc.sbuf_tensor("REGB", [128, E * T_ALL], BF16))
        REGC = _st.enter_context(nc.sbuf_tensor("REGC", [128, KD * T_OWN], BF16))
        REGD = _st.enter_context(nc.sbuf_tensor("REGD", [128, 17408], BF16))
        REGF = _st.enter_context(nc.sbuf_tensor("REGF", [128, 4096], F32))
        a13b = _st.enter_context(nc.sbuf_tensor("a13b", [128, 4096], BF16))
        a2pb = _st.enter_context(nc.sbuf_tensor("a2pb", [128, CF * 128], BF16))
        gwb = _st.enter_context(nc.sbuf_tensor("gwb", [128, KD * E], BF16))
        wfull = _st.enter_context(nc.sbuf_tensor("wfull", [128, E * NT], F32))
        battn_s = _st.enter_context(nc.sbuf_tensor("battn_s", [128, 48], F32))
        battn_q = _st.enter_context(nc.sbuf_tensor("battn_q", [128, 16], F32))
        bproj_s = _st.enter_context(nc.sbuf_tensor("bproj_s", [128, KD], F32))
        stb_s = _st.enter_context(nc.sbuf_tensor("stb_s", [128, NQ * NSLOT], F32))
        qaug_b = _st.enter_context(nc.sbuf_tensor("qaug_b", [2, NQ * 128], BF16))
        id_b = _st.enter_context(nc.sbuf_tensor("id_b", [128, 128], BF16))
        tri_s = _st.enter_context(nc.sbuf_tensor("tri_s", [128, 128], F32))
        trit_s = _st.enter_context(nc.sbuf_tensor("trit_s", [128, 128], F32))
        ones_f = _st.enter_context(nc.sbuf_tensor("ones_f", [1, 128], F32))
        onescol_b = _st.enter_context(nc.sbuf_tensor("onescol_b", [128, 1], BF16))
        one_b = _st.enter_context(nc.sbuf_tensor("one_b", [1, 1], BF16))
        grow = _st.enter_context(nc.sbuf_tensor("grow", [128, 40], F32))
        wf_b = _st.enter_context(nc.sbuf_tensor("wf_b", [128, 8], BF16))
        nmx_b = _st.enter_context(nc.sbuf_tensor("nmx_b", [128, 1], BF16))
        nmx2 = _st.enter_context(nc.sbuf_tensor("nmx2", [128, 2], BF16))
        eps_s = _st.enter_context(nc.sbuf_tensor("eps_s", [128, 1], F32))
        onesrow_b = _st.enter_context(nc.sbuf_tensor("onesrow_b", [1, 128], BF16))
        qa2 = _st.enter_context(nc.sbuf_tensor("qa2", [65, 128], BF16))
        tm_b = _st.enter_context(nc.sbuf_tensor("tm_b", [128, 4 * 64], BF16))
        ph1A = _st.enter_context(nc.psum_tensor("ph1A", [128, 512], F32))
        ph3A = _st.enter_context(nc.psum_tensor("ph3A", [128, 512], F32))
        ph1B = _st.enter_context(nc.psum_tensor("ph1B", [128, 512], F32))
        ph3B = _st.enter_context(nc.psum_tensor("ph3B", [128, 512], F32))
        pamix = _st.enter_context(nc.psum_tensor("pamix", [128, 512], F32))
        pu2a = _st.enter_context(nc.psum_tensor("pu2a", [128, 512], F32))
        pu2b = _st.enter_context(nc.psum_tensor("pu2b", [128, 512], F32))
        pu2c = _st.enter_context(nc.psum_tensor("pu2c", [128, 512], F32))
        s_ld = _st.enter_context(nc.semaphore("s_ld"))
        s_gd = _st.enter_context(nc.semaphore("s_gd"))
        s_xa = _st.enter_context(nc.semaphore("s_xa"))
        s_xc = _st.enter_context(nc.semaphore("s_xc"))
        s_xb = _st.enter_context(nc.semaphore("s_xb"))
        s_wa = _st.enter_context(nc.semaphore("s_wa"))
        s_wb = _st.enter_context(nc.semaphore("s_wb"))
        s_oa = _st.enter_context(nc.semaphore("s_oa"))
        s_ob = _st.enter_context(nc.semaphore("s_ob"))
        s_wd = _st.enter_context(nc.semaphore("s_wd"))
        s_xd = _st.enter_context(nc.semaphore("s_xd"))
        s_od = _st.enter_context(nc.semaphore("s_od"))
        s_pe = _st.enter_context(nc.semaphore("s_pe"))
        s_pe2 = _st.enter_context(nc.semaphore("s_pe2"))
        s_act = _st.enter_context(nc.semaphore("s_act"))
        s_dve = _st.enter_context(nc.semaphore("s_dve"))
        s_pool = _st.enter_context(nc.semaphore("s_pool"))
        s_bar = _st.enter_context(nc.semaphore("s_bar"))
        B = _st.enter_context(nc.Block())
        uN = REGA[64:80, :]
        u2T = REGA[32:48, 0:E * T_ALL]
        kring = REGA[0:66, 0:4 * NH * 128]
        wb_sb = REGB[:, :]
        vring = REGB[:, 0:4 * NH * 65]
        s_bf = REGC[:, 0:1536]
        act_bf = REGC[:, 1536:3072]
        actw_bf = REGC[:, 3072:7168]
        amx_st = REGC[:, 7168:8192]
        u2s = REGC[:, 8192:8704]
        silu_sb = REGC[:, 8704:10240]
        tmp_sb = REGC[:, 10240:11264]
        b13A = REGD[64:80, 8192:10240]
        b13B = REGD[64:80, 10240:12288]
        bsREG = [REGC[:, 12800 + i * 512:12800 + (i + 1) * 512] for i in range(4)]
        qkst = REGC[:, 14848:15872]
        xt16 = REGC[:, 0:2048]
        attn_sb = REGC[:, :]
        h1s = REGF[:, 1024:2560]
        h3s = REGF[:, 2560:4096]
        ws = REGD[:, 0:8192]
        w2sl = REGD[:, 0:11264]
        b2dg = REGD[32:48, 11264:13312]
        adg = REGD[:, 13312:17408]
        wtrow = REGD[0:1, 0:E * NT * 128]
        wqs = REGD[:, 0:4096]
        q_sb = REGD[0:64, 4096:12288]
        pt_bf = REGD[:, 12288:12288 + 4 * NSLOT * 128]
        qaug64 = REGD[64:66, 4096:4096 + NQ * 128]
        qr64 = REGD[64:65, 13056:13056 + 256]
        xln = REGF[:, 0:512]
        xln2 = REGF[:, 512:1024]
        xfp = REGF[:, 1024:1536]
        rows = REGF[0:1, 1536:3584]
        xio = REGF[:, 0:2048]
        xout = REGF[:, 2048:4096]
        xpj = REGF[:, 1536:2560]
        amixf = REGF[:, 0:1024]

        PU2 = [pu2a, pu2b, pu2c]
        R_pe, R_sc, R_ve, R_gp, R_sy = (_EngRec() for _ in range(5))
        pe = lambda f: f(R_pe)
        sc = lambda f: f(R_sc)
        ve = lambda f: f(R_ve)
        gp = lambda f: f(R_gp)
        sy = lambda f: f(R_sy)
        nbar = [0]

        def barrier():
            nbar[0] += 1
            tgt = 5 * nbar[0]
            for em in (pe, sc, ve, gp, sy):
                def _f(e, tgt=tgt):
                    e.drain().then_inc(s_bar, 1)
                    e.wait_ge(s_bar, tgt)
                em(_f)
            led.c[s_bar.name] = tgt

        def _p0g(g):
            for t_in, t_out, n in ((w13, w13b, CF), (b13, b13bd, CF),
                                   (wq, wqb, 48), (wp, wpb, KD), (w2r, w2rb, KD)):
                step = max(1, (n + 3) // 4)
                i = 0
                while i < n:
                    j = min(n, i + step)
                    g.dma_start(out=t_out[i:j], in_=t_in[i:j]).then_inc(s_gd, 16)
                    led.inc(s_gd, 16)
                    i = j
            g.dma_start(out=b2pb[:, :], in_=b2pi).then_inc(s_gd, 16)
            led.inc(s_gd, 16)
            for t_in, t_out in ((a13, a13b[:, :]), (a2pi, a2pb[:, :]),
                                (gwt, gwb[:, :]), (ident, id_b[:, :]),
                                (qaug, qaug_b[:, :])):
                g.dma_start(out=t_out, in_=t_in).then_inc(s_gd, 16)
                led.inc(s_gd, 16)
            for h in range(NH):
                g.dma_start(out=kdr[64:66, h * T_ALL:(h + 1) * T_ALL],
                            in_=mkrow).then_inc(s_gd, 16)
                led.inc(s_gd, 16)
        gp(_p0g)

        def _p0s(s):
            for t_in, t_out in ((battn, battn_s[:, :]), (bproj, bproj_s[:, :]),
                                (stb, stb_s[:, :]), (tri, tri_s[:, :]),
                                (trit, trit_s[:, :]), (onesf, ones_f[:, :])):
                s.dma_start(out=t_out, in_=t_in).then_inc(s_ld, 16)
                led.inc(s_ld, 16)
        sy(_p0s)
        ldt = led.val(s_ld)
        gdt = led.val(s_gd)

        def _p0v(v, ldt=ldt, gdt=gdt):
            v.wait_ge(s_ld, ldt)
            v.memset(onescol_b[:, :], 1.0)
            v.memset(one_b[:, :], 1.0)
            v.memset(eps_s[:, :], 1e-5)
            v.memset(onesrow_b[:, :], 1.0)
        ve(_p0v)

        def _p0sc(s, ldt=ldt, gdt=gdt):
            s.wait_ge(s_ld, ldt)
            s.activation(battn_q[:, :], battn_s[:, 0:16], AF.Identity, scale=0.125)
        sc(_p0sc)
        barrier()

        def ln_phase(src_ap, blocks):
            for (boff, w) in blocks:
                dve0 = led.val(s_dve)
                pe0 = led.val(s_pe)
                for k in range(KD):
                    sem = s_xa if k % 2 == 0 else s_xb

                    def _ld(s2, k=k, boff=boff, w=w, dve0=dve0, sem=sem):
                        s2.wait_ge(s_dve, dve0 + k - 1 if k >= 2 else dve0)
                        dst = xln if k % 2 == 0 else xln2
                        s2.dma_start(out=dst[:, 0:w],
                                     in_=src_ap[k * 128:(k + 1) * 128,
                                                boff:boff + w]).then_inc(sem, 16)
                        led.inc(sem, 16)
                    sy(_ld)
                    xdt = led.val(sem)

                    def _cast(v, k=k, w=w, xdt=xdt, sem=sem, pe0=pe0):
                        v.wait_ge(sem, xdt)
                        if k >= 2:
                            v.wait_ge(s_pe, pe0 + k - 1)
                        src = xln if k % 2 == 0 else xln2
                        xb = xt16[:, (k % 2) * 1024:(k % 2) * 1024 + w]
                        xq = xt16[:, (k % 2) * 1024 + 512:(k % 2) * 1024 + 512 + w]
                        v.tensor_copy(xb, src[:, 0:w])
                        v.tensor_mul(xq, src[:, 0:w], src[:, 0:w]).then_inc(s_dve, 1)
                        led.inc(s_dve, 1)
                    ve(_cast)
                    dvt = led.val(s_dve)

                    def _mm(t, k=k, w=w, dvt=dvt):
                        t.wait_ge(s_dve, dvt)
                        xb = xt16[:, (k % 2) * 1024:(k % 2) * 1024 + w]
                        xq = xt16[:, (k % 2) * 1024 + 512:(k % 2) * 1024 + 512 + w]
                        t.matmul(pamix[0:1, 0:w], onescol_b[:, :], xb,
                                 start=(k == 0), stop=(k == KD - 1))
                        t.matmul(ph1B[0:1, 0:w], onescol_b[:, :], xq,
                                 start=(k == 0),
                                 stop=(k == KD - 1)).then_inc(s_pe, 1)
                        led.inc(s_pe, 1)
                    pe(_mm)
                pet = led.val(s_pe)

                def _st1(s, w=w, pet=pet):
                    s.wait_ge(s_pe, pet)
                    s.activation(rows[:, 0:w], pamix[0:1, 0:w], AF.Identity,
                                 scale=1.0 / D)
                    s.activation(rows[:, 512:512 + w], ph1B[0:1, 0:w],
                                 AF.Identity, scale=1.0 / D).then_inc(s_act, 1)
                    led.inc(s_act, 1)
                sc(_st1)
                at = led.val(s_act)

                def _st2(v, w=w, at=at):
                    v.wait_ge(s_act, at)
                    v.tensor_mul(rows[:, 1024:1024 + w], rows[:, 0:w],
                                 rows[:, 0:w]).then_inc(s_dve, 1)
                    v.wait_ge(s_dve, led.inc(s_dve, 1))
                    v.tensor_sub(rows[:, 512:512 + w], rows[:, 512:512 + w],
                                 rows[:, 1024:1024 + w]).then_inc(s_dve, 1)
                    led.inc(s_dve, 1)
                ve(_st2)
                dvt2 = led.val(s_dve)

                def _st3(s, w=w, dvt2=dvt2):
                    s.wait_ge(s_dve, dvt2)
                    s.activation(rows[:, 512:512 + w], rows[:, 512:512 + w],
                                 AF.Ln, bias=eps_s[0:1, 0:1]).then_inc(s_act, 1)
                    s.wait_ge(s_act, led.inc(s_act, 1))
                    s.activation(rows[:, 512:512 + w], rows[:, 512:512 + w],
                                 AF.Exp, scale=-0.5).then_inc(s_act, 1)
                    led.inc(s_act, 1)
                sc(_st3)
                at2 = led.val(s_act)

                def _st4(v, w=w, at2=at2):
                    v.wait_ge(s_act, at2)
                    v.tensor_mul(rows[:, 1536:1536 + w], rows[:, 0:w],
                                 rows[:, 512:512 + w]).then_inc(s_dve, 1)
                    led.inc(s_dve, 1)
                ve(_st4)
                dvt3 = led.val(s_dve)

                def _bc(t, w=w, dvt3=dvt3):
                    t.wait_ge(s_dve, dvt3)
                    t.matmul(ph1A[:, 0:w], ones_f[:, :], rows[:, 512:512 + w],
                             start=True, stop=True)
                    t.matmul(ph3A[:, 0:w], ones_f[:, :], rows[:, 1536:1536 + w],
                             start=True, stop=True).then_inc(s_pe, 1)
                    led.inc(s_pe, 1)
                pe(_bc)
                pet2 = led.val(s_pe)
                dve1 = led.val(s_dve)
                for k in range(KD):
                    sem = s_xa if k % 2 == 0 else s_xb

                    def _ld2(s2, k=k, boff=boff, w=w, dve1=dve1, sem=sem):
                        s2.wait_ge(s_dve, dve1 + 2 * (k - 1) if k >= 2 else dve1)
                        dst = xln if k % 2 == 0 else xln2
                        s2.dma_start(out=dst[:, 0:w],
                                     in_=src_ap[k * 128:(k + 1) * 128,
                                                boff:boff + w]).then_inc(sem, 16)
                        led.inc(sem, 16)
                    sy(_ld2)
                    xdt2 = led.val(sem)

                    dvp = led.val(s_dve)

                    def _nm(v, k=k, boff=boff, w=w, xdt2=xdt2, pet2=pet2, sem=sem,
                            dvp=dvp):
                        v.wait_ge(sem, xdt2)
                        v.wait_ge(s_dve, dvp)
                        if k == 0:
                            v.wait_ge(s_pe, pet2)
                        src = xln if k % 2 == 0 else xln2
                        v.tensor_mul(xfp[:, 0:w], src[:, 0:w],
                                     ph1A[:, 0:w]).then_inc(s_dve, 1)
                        v.wait_ge(s_dve, led.inc(s_dve, 1))
                        v.tensor_sub(xn_all[:, k * T_ALL + boff:
                                            k * T_ALL + boff + w],
                                     xfp[:, 0:w], ph3A[:, 0:w]).then_inc(s_dve, 1)
                        led.inc(s_dve, 1)
                    ve(_nm)
            barrier()

        def gate_phase(tiles):
            for gi in tiles:
                def _gmm(t, gi=gi):
                    for k in range(KD):
                        mm = t.matmul(pamix[:, 0:8],
                                      xn_all[:, k * T_ALL + gi * 128:
                                             k * T_ALL + gi * 128 + 128],
                                      gwb[:, k * 8:(k + 1) * 8],
                                      start=(k == 0), stop=(k == KD - 1))
                    mm.then_inc(s_pe, 1)
                    led.inc(s_pe, 1)
                pe(_gmm)
                pt = led.val(s_pe)

                def _g1(v, pt=pt):
                    v.wait_ge(s_pe, pt)
                    v.tensor_reduce(grow[:, 0:1], pamix[:, 0:8], AX.X,
                                    ALU.max).then_inc(s_dve, 1)
                    v.tensor_reduce(grow[:, 1:2], pamix[:, 0:8], AX.X, ALU.max,
                                    negate=True)
                    v.wait_ge(s_dve, led.inc(s_dve, 1))
                    v.tensor_single_scalar(grow[:, 8:16], pamix[:, 0:8],
                                           grow[:, 0:1], ALU.is_ge).then_inc(s_dve, 1)
                    v.wait_ge(s_dve, led.inc(s_dve, 1))
                    v.scalar_tensor_tensor(grow[:, 16:24], grow[:, 8:16], -1e30,
                                           pamix[:, 0:8], ALU.mult,
                                           ALU.add).then_inc(s_dve, 1)
                    v.wait_ge(s_dve, led.inc(s_dve, 1))
                    v.tensor_reduce(grow[:, 2:3], grow[:, 16:24], AX.X,
                                    ALU.max).then_inc(s_dve, 1)
                    led.inc(s_dve, 1)
                ve(_g1)
                dt = led.val(s_dve)

                def _g2(s, dt=dt):
                    s.wait_ge(s_dve, dt)
                    s.activation(grow[:, 24:32], pamix[:, 0:8], AF.Exp,
                                 bias=grow[:, 1:2])
                    s.activation(grow[:, 3:4], grow[:, 2:3], AF.Exp,
                                 bias=grow[:, 1:2]).then_inc(s_act, 1)
                    led.inc(s_act, 1)
                sc(_g2)
                at = led.val(s_act)

                def _g3(v, gi=gi, at=at):
                    v.wait_ge(s_act, at)
                    v.tensor_single_scalar(grow[:, 4:5], grow[:, 3:4], 1.0,
                                           ALU.add).then_inc(s_dve, 1)
                    v.wait_ge(s_dve, led.inc(s_dve, 1))
                    v.reciprocal(grow[:, 5:6], grow[:, 4:5])
                    v.tensor_single_scalar(grow[:, 16:24], pamix[:, 0:8],
                                           grow[:, 2:3], ALU.is_ge).then_inc(s_dve, 1)
                    v.wait_ge(s_dve, led.inc(s_dve, 1))
                    v.scalar_tensor_tensor(wfull[:, gi * 8:gi * 8 + 8],
                                           grow[:, 24:32], grow[:, 5:6],
                                           grow[:, 16:24], ALU.mult,
                                           ALU.mult).then_inc(s_dve, 1)
                    v.wait_ge(s_dve, led.inc(s_dve, 1))
                    v.tensor_copy(wf_b[:, :],
                                  wfull[:, gi * 8:gi * 8 + 8]).then_inc(s_dve, 1)
                    led.inc(s_dve, 1)
                ve(_g3)
                dt2 = led.val(s_dve)

                for e in range(E):
                    def _g4b(t, e=e, dt2=dt2):
                        t.wait_ge(s_dve, dt2)
                        if e > 0:
                            t.wait_ge(s_act, led.val(s_act))
                        t.matmul(ph1B[0:1, 0:128], wf_b[:, e:e + 1], id_b[:, :],
                                 start=True, stop=True).then_inc(s_pe, 1)
                        led.inc(s_pe, 1)
                    pe(_g4b)
                    pt2 = led.val(s_pe)

                    def _g5(s, gi=gi, e=e, pt2=pt2):
                        s.wait_ge(s_pe, pt2)
                        s.activation(wtrow[:, (e * NT + gi) * 128:
                                           (e * NT + gi) * 128 + 128],
                                     ph1B[0:1, 0:128],
                                     AF.Identity).then_inc(s_act, 1)
                        led.inc(s_act, 1)
                    sc(_g5)
            att = led.val(s_act)
            for e in range(E):
                for gi in tiles:
                    def _g6(t, e=e, gi=gi, att=att):
                        t.wait_ge(s_act, att)
                        if not (e == 0 and gi == tiles[0]):
                            t.wait_ge(s_act, led.val(s_act))
                        t.matmul(ph3B[:, 0:128], onesrow_b[:, :],
                                 wtrow[:, (e * NT + gi) * 128:
                                       (e * NT + gi) * 128 + 128],
                                 start=True, stop=True).then_inc(s_pe, 1)
                        led.inc(s_pe, 1)
                    pe(_g6)
                    pt3 = led.val(s_pe)

                    def _g7(s, e=e, gi=gi, pt3=pt3):
                        s.wait_ge(s_pe, pt3)
                        s.activation(wb_sb[:, e * T_ALL + gi * 128:
                                           e * T_ALL + gi * 128 + 128],
                                     ph3B[:, 0:128], AF.Identity).then_inc(s_act, 1)
                        led.inc(s_act, 1)
                    sc(_g7)
            barrier()

        def uprod(blocks):
            prev_at = [led.val(s_act)]
            prev_xd = [led.val(s_gd)]
            for p in range(2):
                for (boff, w) in blocks:
                    def _u1(t, p=p, boff=boff, w=w, pa=prev_at[0]):
                        t.wait_ge(s_act, pa)
                        for k in range(KD):
                            mm = t.matmul(ph1A[:, 0:w],
                                          a13b[:, (p * KD + k) * 128:
                                               (p * KD + k + 1) * 128],
                                          xn_all[:, k * T_ALL + boff:
                                                 k * T_ALL + boff + w],
                                          start=(k == 0), stop=(k == KD - 1))
                            if k == KD - 1:
                                mm.then_inc(s_pe, 1)
                        led.inc(s_pe, 1)
                    pe(_u1)
                    pt = led.val(s_pe)

                    def _u2(s, w=w, pt=pt, px=prev_xd[0]):
                        s.wait_ge(s_pe, pt)
                        s.wait_ge(s_gd, px)
                        s.activation(u2s[:, 0:w], ph1A[:, 0:w], AF.Identity,
                                     scale=2.0).then_inc(s_act, 1)
                        led.inc(s_act, 1)
                    sc(_u2)
                    at = led.val(s_act)
                    prev_at[0] = at

                    def _u3(g, p=p, boff=boff, w=w, at=at):
                        g.wait_ge(s_act, at)
                        for e in range(E):
                            col = (p * E + e) * T_ALL + boff
                            g.dma_start(out=uN[:, col:col + w],
                                        in_=u2s[16 * e:16 * e + 16, 0:w]
                                        ).then_inc(s_gd, 16)
                            led.inc(s_gd, 16)
                    gp(_u3)
                    prev_xd[0] = led.val(s_gd)
            gfin = led.val(s_gd)

            def _udrain(t, gfin=gfin):
                t.wait_ge(s_gd, gfin)
            pe(_udrain)
            barrier()

        def floop(blocks):
            H_LORA = [ph1B, ph3B, pamix]
            w13t, chunk_pe_done, o_fin = {}, {}, {}
            P0_ = led.val(s_pe)
            A0_ = led.val(s_act)
            D0_ = led.val(s_dve)
            PL_ = led.val(s_pool)
            P2_ = led.val(s_pe2)
            NU = CF * len(blocks)

            def h_lora(G):
                return P0_ + (G // 16) * 18 + 2 + (G % 16) + 1

            def h_pe_done(u):
                return P0_ + (u + 1) * 18

            def h_bsev(u):
                return A0_ + u * 11 + 2

            def h_silu(g):
                return A0_ + (g // 8) * 11 + 2 + (g % 8) + 1

            def h_amxev(u):
                return A0_ + (u + 1) * 11

            def h_add(G):
                return D0_ + G + 1

            def h_tmp1(g):
                return PL_ + (g // 8) * 17 + 2 * (g % 8) + 1

            def h_actw(g):
                return PL_ + (g // 8) * 17 + 2 * (g % 8) + 2

            def h_amix(u):
                return PL_ + (u + 1) * 17

            def h_u2(u):
                return P2_ + u + 1

            def load_chunk(c, wait_pe):
                wsem = s_wa if c % 2 == 0 else s_wb
                led.inc(wsem, 32)

                def _w(s2, c=c, wait_pe=wait_pe, wsem=wsem):
                    if wait_pe is not None:
                        s2.wait_ge(s_pe, wait_pe)
                    par = c % 2
                    s2.dma_start(out=ws[:, par * 4096:(par + 1) * 4096],
                                 in_=w13b[c]).then_inc(wsem, 16)
                    s2.dma_start(out=(b13A if par == 0 else b13B),
                                 in_=b13bd[c]).then_inc(wsem, 16)
                sy(_w)
                w13t[c] = (wsem, led.val(wsem))

            load_chunk(0, None)
            u = -1
            for c in range(CF):
                if c + 1 < CF:
                    load_chunk(c + 1, chunk_pe_done.get(c - 1))
                par = c % 2
                wtile = ws[:, par * 4096:(par + 1) * 4096]
                bt = b13A if par == 0 else b13B
                for bi, (boff, w) in enumerate(blocks):
                    u += 1
                    g0 = u * E
                    G0 = u * 16
                    osem = s_oa if u % 2 == 0 else s_ob

                    def _pe(t, c=c, bi=bi, boff=boff, w=w, u=u,
                            wtile=wtile, bt=bt, g0=g0, G0=G0):
                        t.wait_ge(w13t[c][0], w13t[c][1])
                        if u >= 1:
                            t.wait_ge(s_act, h_bsev(u - 1))
                        for k in range(KD):
                            mm = t.matmul(ph1A[:, 0:w],
                                          wtile[:, k * 128:(k + 1) * 128],
                                          xn_all[:, k * T_ALL + boff:
                                                 k * T_ALL + boff + w],
                                          start=(k == 0), stop=(k == KD - 1))
                        mm.then_inc(s_pe, 1)
                        led.inc(s_pe, 1)
                        for k in range(KD):
                            mm = t.matmul(ph3A[:, 0:w],
                                          wtile[:, (KD + k) * 128:
                                                (KD + k + 1) * 128],
                                          xn_all[:, k * T_ALL + boff:
                                                 k * T_ALL + boff + w],
                                          start=(k == 0), stop=(k == KD - 1))
                        mm.then_inc(s_pe, 1)
                        led.inc(s_pe, 1)
                        for j in range(16):
                            e, half = j // 2, j % 2
                            L = H_LORA[(G0 + j) % 3]
                            if G0 + j >= 3:
                                t.wait_ge(s_dve, h_add(G0 + j - 3))
                            col = e * 128 if half == 0 else 1024 + e * 128
                            roff = (e if half == 0 else E + e) * T_ALL + boff
                            t.matmul(L[:, 0:w], bt[:, col:col + 128],
                                     uN[:, roff:roff + w],
                                     start=True, stop=True).then_inc(s_pe, 1)
                            led.inc(s_pe, 1)
                        t.wait_ge(s_pool, h_actw(g0 + E - 1))
                        mm = None
                        for ep in range(E):
                            aw = actw_bf[:, ep * 512:ep * 512 + w]
                            for sl in range(w // 128):
                                mm = t.matmul(
                                    PU2[bi][:, (sl * E + ep) * 16:
                                            (sl * E + ep) * 16 + 16],
                                    aw[:, sl * 128:(sl + 1) * 128],
                                    a2pb[:, (c * E + ep) * 16:
                                         (c * E + ep) * 16 + 16],
                                    start=(c == 0 and ep == 0 and sl == 0),
                                    stop=(c == CF - 1 and ep == E - 1
                                          and sl == w // 128 - 1))
                        mm.then_inc(s_pe2, 1)
                        led.inc(s_pe2, 1)
                    pe(_pe)
                    assert led.val(s_pe) == h_pe_done(u)

                    def _sc(s, u=u, w=w, g0=g0, G0=G0):
                        s.wait_ge(s_pe, P0_ + u * 18 + 2)
                        if u >= 2:
                            s.wait_ge(s_dve, h_add(G0 - 17))
                        s.activation(bsREG[(u % 2) * 2][:, 0:w], ph1A[:, 0:w],
                                     AF.Identity).then_inc(s_act, 1)
                        led.inc(s_act, 1)
                        s.activation(bsREG[(u % 2) * 2 + 1][:, 0:w],
                                     ph3A[:, 0:w],
                                     AF.Identity).then_inc(s_act, 1)
                        led.inc(s_act, 1)
                        for e in range(E):
                            g = g0 + e
                            s.wait_ge(s_dve, h_add(G0 + 2 * e))
                            if g >= 3:
                                s.wait_ge(s_pool, h_tmp1(g - 3))
                            s.activation(silu_sb[:, (g % 3) * 512:
                                                 (g % 3) * 512 + w],
                                         h1s[:, (g % 3) * 512:(g % 3) * 512 + w],
                                         AF.Silu).then_inc(s_act, 1)
                            led.inc(s_act, 1)
                        s.wait_ge(s_pool, h_amix(u))
                        if u >= 2:
                            s.wait_ge(o_fin[u - 2][0], o_fin[u - 2][1])
                        s.activation(amx_st[:, (u % 2) * 512:(u % 2) * 512 + w],
                                     amixf[:, (u % 2) * 512:(u % 2) * 512 + w],
                                     AF.Identity).then_inc(s_act, 1)
                        led.inc(s_act, 1)
                    sc(_sc)
                    assert led.val(s_act) == h_amxev(u)

                    def _ve(v, u=u, w=w, g0=g0, G0=G0):
                        for j in range(16):
                            e, half = j // 2, j % 2
                            g = g0 + e
                            G = G0 + j
                            L = H_LORA[G % 3]
                            v.wait_ge(s_pe, h_lora(G))
                            if j == 0:
                                v.wait_ge(s_act, h_bsev(u))
                            if half == 0:
                                if g >= 3:
                                    v.wait_ge(s_act, h_silu(g - 3))
                                dst = h1s[:, (g % 3) * 512:(g % 3) * 512 + w]
                                bs = bsREG[(u % 2) * 2]
                            else:
                                if g >= 3:
                                    v.wait_ge(s_pool, h_actw(g - 3))
                                dst = h3s[:, (g % 3) * 512:(g % 3) * 512 + w]
                                bs = bsREG[(u % 2) * 2 + 1]
                            v.tensor_add(dst, L[:, 0:w],
                                         bs[:, 0:w]).then_inc(s_dve, 1)
                            led.inc(s_dve, 1)
                    ve(_ve)
                    assert led.val(s_dve) == h_add(G0 + 15)

                    def _gp(g2, u=u, boff=boff, w=w, g0=g0, G0=G0):
                        for e in range(E):
                            g = g0 + e
                            g2.wait_ge(s_act, h_silu(g))
                            if e == 0 and u >= 1:
                                g2.wait_ge(s_pe2, h_u2(u - 1))
                            g2.tensor_mul(
                                tmp_sb[:, (g % 2) * 512:(g % 2) * 512 + w],
                                silu_sb[:, (g % 3) * 512:(g % 3) * 512 + w],
                                wb_sb[:, e * T_ALL + boff:e * T_ALL + boff + w]
                            ).then_inc(s_pool, 1)
                            led.inc(s_pool, 1)
                            g2.wait_ge(s_dve, h_add(G0 + 2 * e + 1))
                            g2.tensor_mul(
                                actw_bf[:, e * 512:e * 512 + w],
                                tmp_sb[:, (g % 2) * 512:(g % 2) * 512 + w],
                                h3s[:, (g % 3) * 512:(g % 3) * 512 + w]
                            ).then_inc(s_pool, 1)
                            led.inc(s_pool, 1)
                            am = amixf[:, (u % 2) * 512:(u % 2) * 512 + w]
                            if e == 0:
                                if u >= 2:
                                    g2.wait_ge(s_act, h_amxev(u - 2))
                                g2.tensor_copy(am, actw_bf[:, 0:w])
                            elif e < E - 1:
                                g2.tensor_add(am, am,
                                              actw_bf[:, e * 512:e * 512 + w])
                            else:
                                g2.tensor_add(am, am,
                                              actw_bf[:, e * 512:e * 512 + w]
                                              ).then_inc(s_pool, 1)
                                led.inc(s_pool, 1)
                    gp(_gp)
                    assert led.val(s_pool) == h_amix(u)

                    def _amxd(s2, c=c, boff=boff, w=w, u=u, osem=osem):
                        s2.wait_ge(s_act, h_amxev(u))
                        s2.dma_start(out=amx[c][:, boff:boff + w],
                                     in_=amx_st[:, (u % 2) * 512:
                                                (u % 2) * 512 + w]
                                     ).then_inc(osem, 16)
                        led.inc(osem, 16)
                    sy(_amxd)
                    o_fin[u] = (osem, led.val(osem))
                chunk_pe_done[c] = led.val(s_pe)

            oa_f, ob_f = led.val(s_oa), led.val(s_ob)

            def _odrain(t, oa_f=oa_f, ob_f=ob_f):
                t.wait_ge(s_oa, oa_f)
                t.wait_ge(s_ob, ob_f)
            pe(_odrain)
            barrier()

        def u2fin(blocks):
            prev_tr = [None]
            gd_prev = [None]
            for bi, (boff, w) in enumerate(blocks):
                nsl = w // 128

                def _sc0(v, bi=bi, nsl=nsl, ptr=prev_tr[0]):
                    if ptr is not None:
                        v.wait_ge(s_pe, ptr)
                    v.tensor_scalar(u2s[:, 0:nsl * E * 16],
                                    PU2[bi][:, 0:nsl * E * 16],
                                    2.0, None, ALU.mult).then_inc(s_dve, 1)
                    led.inc(s_dve, 1)
                ve(_sc0)
                dvt = led.val(s_dve)
                for sl in range(nsl):
                    atp = led.val(s_act)

                    def _tr(t, sl=sl, dvt=dvt, gdp=gd_prev[0], atp=atp):
                        t.wait_ge(s_dve, dvt)
                        t.wait_ge(s_act, atp)
                        if gdp is not None:
                            t.wait_ge(s_gd, gdp)
                        t.matmul(pamix[:, 0:128],
                                 u2s[:, sl * 128:(sl + 1) * 128], id_b[:, :],
                                 start=True, stop=True).then_inc(s_pe, 1)
                        led.inc(s_pe, 1)
                    pe(_tr)
                    pt = led.val(s_pe)

                    def _ev(s, pt=pt):
                        s.wait_ge(s_pe, pt)
                        s.activation(s_bf[:, 0:128], pamix[:, 0:128],
                                     AF.Identity).then_inc(s_act, 1)
                        led.inc(s_act, 1)
                    sc(_ev)
                    at = led.val(s_act)

                    def _rp(g, sl=sl, boff=boff, at=at):
                        g.wait_ge(s_act, at)
                        for e in range(E):
                            g.dma_start(
                                out=u2T[:, e * T_ALL + boff + sl * 128:
                                        e * T_ALL + boff + sl * 128 + 128],
                                in_=s_bf[16 * e:16 * e + 16, 0:128]
                            ).then_inc(s_gd, 16)
                            led.inc(s_gd, 16)
                    gp(_rp)
                    gd_prev[0] = led.val(s_gd)
                prev_tr[0] = led.val(s_pe)
            gdf = led.val(s_gd)

            def _gdrain(t, gdf=gdf):
                t.wait_ge(s_gd, gdf)
            pe(_gdrain)
            barrier()

        def dloop(xsrc_ap, dst_ap, dst_shift, blocks):
            for bi, (boff, w) in enumerate(blocks):
                for dg in range(KD // 2):
                    d0 = dg * 2
                    prev_dve = led.val(s_dve)
                    prev_od = led.val(s_od)

                    def _wl(s2, d0=d0, bi=bi, boff=boff, w=w, prev_dve=prev_dve,
                            prev_od=prev_od):
                        s2.wait_ge(s_dve, prev_dve)
                        s2.wait_ge(s_od, prev_od)
                        s2.dma_start(out=w2sl[:, 0:F],
                                     in_=w2rb[d0]).then_inc(s_wd, 16)
                        s2.dma_start(out=w2sl[:, F:2 * F],
                                     in_=w2rb[d0 + 1]).then_inc(s_wd, 16)
                        s2.dma_start(out=b2dg[:, :],
                                     in_=b2pb[:, d0 * E * 128:(d0 + 2) * E * 128]
                                     ).then_inc(s_wd, 16)
                        s2.dma_start(
                            out=xio[:, :].rearrange("p (g t) -> p g t", g=2)
                            [:, :, 0:w],
                            in_=xsrc_ap[d0 * 128:(d0 + 2) * 128, boff:boff + w]
                            .rearrange("(g p) t -> p g t", p=128)
                        ).then_inc(s_wd, 16)
                        led.inc(s_wd, 64)
                    sy(_wl)
                    wdt = led.val(s_wd)
                    mm_groups = []
                    for ci in range(0, CF, 4):
                        gidx = ci // 4

                        asem = s_xa if gidx % 2 == 0 else s_xb

                        def _al(s2, ci=ci, gidx=gidx, bi=bi, boff=boff, w=w,
                                asem=asem):
                            if gidx >= 2:
                                s2.wait_ge(s_pe, mm_groups[gidx - 2])
                            par = gidx % 2
                            s2.dma_start(
                                out=adg[:, par * 2048:par * 2048 + 2048]
                                .rearrange("p (cc t) -> p cc t", cc=4)[:, :, 0:w],
                                in_=amx[ci:ci + 4, :, boff:boff + w]
                                .rearrange("cc p t -> p cc t")
                            ).then_inc(asem, 16)
                            led.inc(asem, 16)
                        sy(_al)
                        xdt = led.val(asem)

                        def _mm(t, ci=ci, gidx=gidx, w=w, xdt=xdt, wdt=wdt,
                                asem=asem, prev_dve=prev_dve):
                            t.wait_ge(asem, xdt)
                            if ci == 0:
                                t.wait_ge(s_wd, wdt)
                                t.wait_ge(s_dve, prev_dve)
                            par = gidx % 2
                            for cj in range(4):
                                cc = ci + cj
                                rhs = adg[:, par * 2048 + cj * 512:
                                          par * 2048 + cj * 512 + w]
                                t.matmul(ph1A[:, 0:w],
                                         w2sl[:, cc * 128:(cc + 1) * 128],
                                         rhs, start=(cc == 0), stop=False)
                                mm2 = t.matmul(
                                    ph3A[:, 0:w],
                                    w2sl[:, F + cc * 128:F + (cc + 1) * 128],
                                    rhs, start=(cc == 0), stop=False)
                            mm2.then_inc(s_pe, 1)
                            mm_groups.append(led.inc(s_pe, 1))
                        pe(_mm)

                    def _lora(t, boff=boff, w=w):
                        for e in range(E):
                            t.matmul(ph1A[:, 0:w],
                                     b2dg[:, e * 128:(e + 1) * 128],
                                     u2T[:, e * T_ALL + boff:e * T_ALL + boff + w],
                                     start=False, stop=(e == E - 1))
                            mm2 = t.matmul(
                                ph3A[:, 0:w],
                                b2dg[:, (E + e) * 128:(E + e + 1) * 128],
                                u2T[:, e * T_ALL + boff:e * T_ALL + boff + w],
                                start=False, stop=(e == E - 1))
                        mm2.then_inc(s_pe, 1)
                        led.inc(s_pe, 1)
                    pe(_lora)
                    pt = led.val(s_pe)

                    def _res(v, w=w, pt=pt, wdt=wdt):
                        v.wait_ge(s_pe, pt)
                        v.wait_ge(s_wd, wdt)
                        v.scalar_tensor_tensor(xout[:, 0:w], ph1A[:, 0:w], 0.5,
                                               xio[:, 0:w], ALU.mult, ALU.add)
                        v.scalar_tensor_tensor(xout[:, 1024:1024 + w],
                                               ph3A[:, 0:w], 0.5,
                                               xio[:, 1024:1024 + w],
                                               ALU.mult, ALU.add).then_inc(s_dve, 1)
                        led.inc(s_dve, 1)
                    ve(_res)
                    dvt = led.val(s_dve)

                    def _st(s2, d0=d0, boff=boff, w=w, dvt=dvt,
                            dst_shift=dst_shift):
                        s2.wait_ge(s_dve, dvt)
                        s2.dma_start(
                            out=dst_ap[d0 * 128:(d0 + 2) * 128,
                                       boff - dst_shift:boff - dst_shift + w]
                            .rearrange("(g p) t -> p g t", p=128),
                            in_=xout[:, :].rearrange("p (g t) -> p g t", g=2)
                            [:, :, 0:w]
                        ).then_inc(s_od, 16)
                        led.inc(s_od, 16)
                    sy(_st)

                    odt = led.val(s_od)

                    def _fr(t, dvt=dvt, odt=odt):
                        t.wait_ge(s_dve, dvt)
                        t.wait_ge(s_od, odt)
                        t.nop().then_inc(s_pe, 1)
                        led.inc(s_pe, 1)
                    pe(_fr)
            barrier()

        def qkv_phase():
            wdts = {}
            pe_done = {}
            ev_hist = {}

            def loadw(j, wait_pe):
                wsem = s_wa if j % 2 == 0 else s_wb

                def _l(s2, j=j, wait_pe=wait_pe, wsem=wsem):
                    if wait_pe is not None:
                        s2.wait_ge(s_pe, wait_pe)
                    s2.dma_start(out=wqs[:, (j % 2) * 2048:(j % 2) * 2048 + 2048],
                                 in_=wqb[j]).then_inc(wsem, 16)
                    led.inc(wsem, 16)
                sy(_l)
                wdts[j] = (wsem, led.val(wsem))

            loadw(0, None)
            for j in range(48):
                if j + 1 < 48:
                    loadw(j + 1, pe_done.get(j - 1))
                for sbi, (boff, w) in enumerate(BLK1):
                    gidx = j * 3 + sbi
                    ps = ph1A if gidx % 2 == 0 else ph1B
                    atprev = ev_hist.get(gidx - 2, 0)

                    def _mm(t, j=j, boff=boff, w=w, ps=ps, atprev=atprev):
                        t.wait_ge(wdts[j][0], wdts[j][1])
                        t.wait_ge(s_act, atprev)
                        for k in range(KD):
                            mm = t.matmul(ps[:, 0:w],
                                          wqs[:, (j % 2) * 2048 + k * 128:
                                              (j % 2) * 2048 + (k + 1) * 128],
                                          xn_all[:, k * T_ALL + boff:
                                                 k * T_ALL + boff + w],
                                          start=(k == 0), stop=(k == KD - 1))
                            if k == KD - 1:
                                mm.then_inc(s_pe, 1)
                        led.inc(s_pe, 1)
                    pe(_mm)
                    pt = led.val(s_pe)

                    odp = led.val(s_od)
                    if j < 16:
                        def _ev(s, j=j, w=w, pt=pt, ps=ps, odp=odp):
                            s.wait_ge(s_pe, pt)
                            s.wait_ge(s_od, odp)
                            s.activation(qkst[:, 0:w], ps[:, 0:w], AF.Identity,
                                         bias=battn_q[:, j:j + 1],
                                         scale=0.125).then_inc(s_act, 1)
                            led.inc(s_act, 1)
                        sc(_ev)
                        at = led.val(s_act)
                        ev_hist[gidx] = at

                        def _qst(s2, j=j, boff=boff, w=w, at=at):
                            s2.wait_ge(s_act, at)
                            s2.dma_start(out=qd[j * 128:(j + 1) * 128,
                                                boff:boff + w],
                                         in_=qkst[:, 0:w]).then_inc(s_od, 16)
                            led.inc(s_od, 16)
                        sy(_qst)
                    elif j < 32:
                        def _ev(s, j=j, w=w, pt=pt, ps=ps, odp=odp):
                            s.wait_ge(s_pe, pt)
                            s.wait_ge(s_od, odp)
                            s.activation(qkst[:, 0:w], ps[:, 0:w], AF.Identity,
                                         bias=battn_s[:, j:j + 1]
                                         ).then_inc(s_act, 1)
                            led.inc(s_act, 1)
                        sc(_ev)
                        at = led.val(s_act)
                        ev_hist[gidx] = at

                        def _kst(s2, j=j, boff=boff, w=w, at=at):
                            s2.wait_ge(s_act, at)
                            ha = 2 * (j - 16)
                            s2.dma_start(out=kdr[0:64, ha * T_ALL + boff:
                                                ha * T_ALL + boff + w],
                                         in_=qkst[0:64, 0:w]).then_inc(s_od, 16)
                            s2.dma_start(out=kdr[0:64, (ha + 1) * T_ALL + boff:
                                                (ha + 1) * T_ALL + boff + w],
                                         in_=qkst[64:128, 0:w]).then_inc(s_od, 16)
                            led.inc(s_od, 32)
                        sy(_kst)
                    else:
                        def _ev(s, j=j, w=w, pt=pt, ps=ps, odp=odp):
                            s.wait_ge(s_pe, pt)
                            s.wait_ge(s_od, odp)
                            s.activation(qkst[:, 0:w], ps[:, 0:w], AF.Identity,
                                         bias=battn_s[:, j:j + 1]
                                         ).then_inc(s_act, 1)
                            led.inc(s_act, 1)
                        sc(_ev)
                        at0 = led.val(s_act)
                        ev_hist[gidx] = at0
                        for ts in range(w // 128):
                            atprev2 = led.val(s_act)

                            def _tr(t, ts=ts, at0=at0, atprev2=atprev2):
                                t.wait_ge(s_act, at0 if ts == 0 else atprev2)
                                t.matmul(pamix[:, 0:128],
                                         qkst[:, ts * 128:(ts + 1) * 128],
                                         id_b[:, :], start=True,
                                         stop=True).then_inc(s_pe, 1)
                                led.inc(s_pe, 1)
                            pe(_tr)
                            pt2 = led.val(s_pe)

                            odp2 = led.val(s_od)

                            def _e2(s, pt2=pt2, odp2=odp2):
                                s.wait_ge(s_pe, pt2)
                                s.wait_ge(s_od, odp2)
                                s.activation(qkst[:, 512:640], pamix[:, 0:128],
                                             AF.Identity).then_inc(s_act, 1)
                                led.inc(s_act, 1)
                            sc(_e2)
                            at2 = led.val(s_act)

                            def _vst(s2, j=j, boff=boff, ts=ts, at2=at2):
                                s2.wait_ge(s_act, at2)
                                s2.dma_start(
                                    out=vdr[boff + ts * 128:boff + ts * 128 + 128,
                                            (j - 32) * 128:(j - 31) * 128],
                                    in_=qkst[:, 512:640]).then_inc(s_od, 16)
                                led.inc(s_od, 16)
                            sy(_vst)
                pe_done[j] = led.val(s_pe)
            barrier()

        def attn_phase():
            def _qa64(g):
                g.dma_start(out=qaug64[:, :], in_=qaug).then_inc(s_gd, 16)
                led.inc(s_gd, 16)
            gp(_qa64)
            qa64_t = led.val(s_gd)
            odt0 = led.val(s_od)

            def _ms(v, odt0=odt0):
                v.wait_ge(s_od, odt0)
                vr = vring.rearrange("p (s h x) -> p s h x", s=4, h=NH)
                v.memset(vr[:, :, :, 64:65], 1.0).then_inc(s_dve, 1)
                led.inc(s_dve, 1)
            ve(_ms)
            ms_t = led.val(s_dve)
            ring_t, q_t = {}, {}
            ST_PS = [ph1A, ph3A, ph1B, ph3B]
            PV_PS = [pamix, pu2a]
            h_st, h_mask, h_ex, h_pv, h_rc, h_sc, h_tb, h_fin = \
                {}, {}, {}, {}, {}, {}, {}, {}

            def load_ring(cl, wait_pe):
                rsem = [s_xa, s_xb, s_xc, s_xd][cl % 4]

                def _l(s2, cl=cl, wait_pe=wait_pe, rsem=rsem):
                    if wait_pe is not None:
                        s2.wait_ge(s_pe, wait_pe)
                    else:
                        s2.wait_ge(s_od, odt0)
                    slot = cl % 4
                    s2.dma_start(
                        out=kring[:, slot * NH * 128:(slot + 1) * NH * 128]
                        .rearrange("p (h t) -> p h t", h=NH),
                        in_=kdr.rearrange("p (h t) -> p h t", h=NH)
                        [:, :, cl * 128:(cl + 1) * 128]).then_inc(rsem, 16)
                    s2.dma_start(
                        out=vring[:, slot * NH * 65:(slot + 1) * NH * 65]
                        .rearrange("p (h x) -> p h x", h=NH)[:, :, 0:64],
                        in_=vdr[cl * 128:(cl + 1) * 128, :]
                        .rearrange("p (h x) -> p h x", h=NH)).then_inc(rsem, 16)
                sy(_l)
                led.inc(rsem, 32)
                ring_t[cl] = (rsem, led.val(rsem))

            def load_q(qt, wait_pe):
                qsem = s_oa if qt % 2 == 0 else s_ob

                def _l(s2, qt=qt, wait_pe=wait_pe, qsem=qsem):
                    if wait_pe is not None:
                        s2.wait_ge(s_pe, wait_pe)
                    else:
                        s2.wait_ge(s_od, odt0)
                    cl = qt + 2
                    s2.dma_start(
                        out=q_sb[:, (qt % 2) * 4096:(qt % 2) * 4096 + 4096]
                        .rearrange("p (h t) -> p h t", h=NH),
                        in_=qd[:, cl * 128:(cl + 1) * 128]
                        .rearrange("(h p) t -> p h t", p=64)).then_inc(qsem, 16)
                sy(_l)
                led.inc(qsem, 16)
                q_t[qt] = (qsem, led.val(qsem))

            for cl in range(2):
                load_ring(cl, None)
            load_q(0, None)
            qt_pe_done = {}
            for qt in range(NQ):
                cl = qt + 2
                load_ring(cl, qt_pe_done.get(qt - 2))
                if qt + 1 < NQ:
                    load_q(qt + 1, qt_pe_done.get(qt - 1))
                for h in range(NH):
                    stps = ST_PS[(qt * NH + h) % 4]
                    pvps = PV_PS[h % 2]
                    qh = q_sb[:, (qt % 2) * 4096 + h * 128:
                              (qt % 2) * 4096 + h * 128 + 128]
                    gq = qt * NH + h

                    def _s2(t, qt=qt, cl=cl, h=h, stps=stps, qh=qh, gq=gq):
                        for cc in range(max(0, cl - 2), cl + 1):
                            t.wait_ge(ring_t[cc][0], ring_t[cc][1])
                        t.wait_ge(q_t[qt][0], q_t[qt][1])
                        if gq == 0:
                            t.wait_ge(s_gd, qa64_t)
                        if gq >= 4:
                            t.wait_ge(s_act, h_ex[gq - 4])
                        for s in range(NSLOT):
                            slot = (cl - 2 + s) % 4
                            kcol = slot * NH * 128 + h * 128
                            t.matmul(stps[:, s * 128:(s + 1) * 128],
                                     kring[0:64, kcol:kcol + 128], qh,
                                     start=(s == 0), stop=False)
                            mm = t.matmul(stps[:, s * 128:(s + 1) * 128],
                                          kring[64:65, kcol:kcol + 128],
                                          qaug64[0:1, qt * 128:(qt + 1) * 128],
                                          start=False, stop=(s == NSLOT - 1))
                        mm.then_inc(s_pe, 1)
                        h_st[gq] = led.inc(s_pe, 1)
                    pe(_s2)

                    def _d2(v, stps=stps, gq=gq):
                        v.wait_ge(s_pe, h_st[gq])
                        v.tensor_add(stps[:, 256:384], stps[:, 256:384],
                                     trit_s[:, :]).then_inc(s_dve, 1)
                        h_mask[gq] = led.inc(s_dve, 1)
                    ve(_d2)

                    def _ex(s, qt=qt, stps=stps, gq=gq):
                        s.wait_ge(s_dve, h_mask[gq])
                        if gq >= 4:
                            s.wait_ge(s_pe, h_pv[gq - 4])
                        for sl in range(NSLOT):
                            ai = s.activation(
                                pt_bf[:, (gq % 4) * 384 + sl * 128:
                                      (gq % 4) * 384 + (sl + 1) * 128],
                                stps[:, sl * 128:(sl + 1) * 128], AF.Exp,
                                bias=stb_s[:, qt * NSLOT + sl:
                                           qt * NSLOT + sl + 1])
                        ai.then_inc(s_act, 1)
                        h_ex[gq] = led.inc(s_act, 1)
                    sc(_ex)

                    def _pv(t, cl=cl, h=h, pvps=pvps, gq=gq):
                        t.wait_ge(s_act, h_ex[gq])
                        if gq == 0:
                            t.wait_ge(s_dve, ms_t)
                        if gq >= 2:
                            t.wait_ge(s_dve, h_sc[gq - 2])
                        for s in range(NSLOT):
                            slot = (cl - 2 + s) % 4
                            mm = t.matmul(pvps[0:128, 0:65],
                                          pt_bf[:, (gq % 4) * 384 + s * 128:
                                                (gq % 4) * 384 + (s + 1) * 128],
                                          vring[:, slot * NH * 65 + h * 65:
                                                slot * NH * 65 + (h + 1) * 65],
                                          start=(s == 0), stop=(s == NSLOT - 1))
                        mm.then_inc(s_pe, 1)
                        h_pv[gq] = led.inc(s_pe, 1)
                    pe(_pv)

                    def _rc(v, pvps=pvps, gq=gq):
                        v.wait_ge(s_pe, h_pv[gq])
                        if gq >= 4:
                            v.wait_ge(s_dve, h_sc[gq - 4])
                        v.reciprocal(grow[:, 33 + (gq % 4):34 + (gq % 4)],
                                     pvps[:, 64:65]).then_inc(s_dve, 1)
                        h_rc[gq] = led.inc(s_dve, 1)
                    ve(_rc)

                    def _sc2(v, pvps=pvps, gq=gq):
                        v.wait_ge(s_dve, h_rc[gq])
                        if gq >= 4:
                            v.wait_ge(s_pe, h_tb[(gq - 4) | 1])
                        v.tensor_single_scalar(
                            tm_b[:, (gq % 4) * 64:(gq % 4) * 64 + 64],
                            pvps[:, 0:64],
                            grow[:, 33 + (gq % 4):34 + (gq % 4)],
                            ALU.mult).then_inc(s_dve, 1)
                        h_sc[gq] = led.inc(s_dve, 1)
                    ve(_sc2)

                    if h % 2 == 1:
                        def _tb(t, gq=gq):
                            t.wait_ge(s_dve, h_sc[gq])
                            if gq >= 2:
                                t.wait_ge(s_act, h_fin[gq - 2])
                            pr = (gq // 2) % 2
                            t.matmul(pu2c[:, 0:128],
                                     tm_b[:, pr * 128:pr * 128 + 128],
                                     id_b[:, :], start=True,
                                     stop=True).then_inc(s_pe, 1)
                            h_tb[gq] = h_tb[gq - 1] = led.inc(s_pe, 1)
                        pe(_tb)

                        def _fin(s, qt=qt, h=h, gq=gq):
                            s.wait_ge(s_pe, h_tb[gq])
                            hc = h // 2
                            s.activation(attn_sb[:, hc * T_OWN + qt * 128:
                                                 hc * T_OWN + qt * 128 + 128],
                                         pu2c[:, 0:128],
                                         AF.Identity).then_inc(s_act, 1)
                            h_fin[gq] = h_fin[gq - 1] = led.inc(s_act, 1)
                        sc(_fin)
                qt_pe_done[qt] = led.val(s_pe)
            barrier()

        def proj_phase():
            wdts = {}
            pe_done = {}
            pev_hist = {}

            def loadw(dj, wait_pe):
                wsem = s_wa if dj % 2 == 0 else s_wb

                def _l(s2, dj=dj, wait_pe=wait_pe, wsem=wsem):
                    if wait_pe is not None:
                        s2.wait_ge(s_pe, wait_pe)
                    s2.dma_start(out=wqs[:, (dj % 2) * 2048:(dj % 2) * 2048 + 2048],
                                 in_=wpb[dj]).then_inc(wsem, 16)
                    led.inc(wsem, 16)
                sy(_l)
                wdts[dj] = (wsem, led.val(wsem))

            loadw(0, None)
            for dj in range(KD):
                if dj + 1 < KD:
                    loadw(dj + 1, pe_done.get(dj - 1))
                for sbi, (boff, w) in enumerate(BLK2):
                    ps = ph1A if (dj * 2 + sbi) % 2 == 0 else ph1B
                    oprev = led.val(s_od)

                    def _xl(s2, dj=dj, boff=boff, w=w, sbi=sbi, oprev=oprev):
                        s2.wait_ge(s_od, oprev)
                        s2.dma_start(out=xpj[:, (sbi % 2) * 512:
                                             (sbi % 2) * 512 + w],
                                     in_=xres[dj * 128:(dj + 1) * 128,
                                              boff:boff + w]).then_inc(s_xd, 16)
                        led.inc(s_xd, 16)
                    sy(_xl)
                    xdt = led.val(s_xd)

                    def _mm(t, dj=dj, boff=boff, w=w, ps=ps,
                            dvp=pev_hist.get(dj * 2 + sbi - 2, 0)):
                        t.wait_ge(wdts[dj][0], wdts[dj][1])
                        t.wait_ge(s_dve, dvp)
                        for k in range(KD):
                            mm = t.matmul(ps[:, 0:w],
                                          wqs[:, (dj % 2) * 2048 + k * 128:
                                              (dj % 2) * 2048 + (k + 1) * 128],
                                          attn_sb[:, k * T_OWN + boff - HALO:
                                                  k * T_OWN + boff - HALO + w],
                                          start=(k == 0), stop=(k == KD - 1))
                            if k == KD - 1:
                                mm.then_inc(s_pe, 1)
                        led.inc(s_pe, 1)
                    pe(_mm)
                    pt = led.val(s_pe)

                    def _ev(v, dj=dj, w=w, sbi=sbi, pt=pt, ps=ps, xdt=xdt):
                        v.wait_ge(s_pe, pt)
                        v.wait_ge(s_xd, xdt)
                        v.scalar_tensor_tensor(
                            xfp[:, 0:w], ps[:, 0:w], bproj_s[:, dj:dj + 1],
                            xpj[:, (sbi % 2) * 512:(sbi % 2) * 512 + w],
                            ALU.add, ALU.add).then_inc(s_dve, 1)
                        led.inc(s_dve, 1)
                    ve(_ev)
                    dvt = led.val(s_dve)
                    pev_hist[dj * 2 + sbi] = dvt

                    def _st(s2, dj=dj, boff=boff, w=w, dvt=dvt):
                        s2.wait_ge(s_dve, dvt)
                        s2.dma_start(out=xres[dj * 128:(dj + 1) * 128,
                                              boff:boff + w],
                                     in_=xfp[:, 0:w]).then_inc(s_od, 16)
                        led.inc(s_od, 16)
                    sy(_st)
                pe_done[dj] = led.val(s_pe)
            odf = led.val(s_od)

            def _oddrain2(t, odf=odf):
                t.wait_ge(s_od, odf)
            pe(_oddrain2)
            barrier()

        ln_phase(x0, BLK1)

        def _castwait(t, gdt=gdt):
            t.wait_ge(s_gd, gdt)
        pe(_castwait)
        barrier()
        gate_phase(list(range(NT)))
        uprod(BLK1)
        floop(BLK1)
        u2fin(BLK1)
        dloop(x0, xres, 0, BLK1)
        ln_phase(xres, BLK1)
        qkv_phase()
        attn_phase()
        proj_phase()
        ln_phase(xres, BLK2)
        gate_phase(list(range(2, NT)))
        uprod(BLK2)
        floop(BLK2)
        u2fin(BLK2)
        dloop(xres, y, HALO, BLK2)

        fin = led.val(s_od)

        def _fin(s2, fin=fin):
            s2.wait_ge(s_od, fin)
        sy(_fin)

        B.tensor(R_pe.replay)
        B.scalar(R_sc.replay)
        B.vector(R_ve.replay)
        B.gpsimd(R_gp.replay)
        B.sync(R_sy.replay)

    return nc


def _prep_weights(i):
    w1, w3, w2 = i["L0_ffn_w1"], i["L0_ffn_w3"], i["L0_ffn_w2"]
    a1, b1 = i["L0_ffn_a1"], i["L0_ffn_b1"]
    a3, b3 = i["L0_ffn_a3"], i["L0_ffn_b3"]
    a2, b2 = i["L0_ffn_a2"], i["L0_ffn_b2"]
    gw = i["L0_ffn_gate_w"]
    W_attn, b_attn = i["L0_attn_W_attn"], i["L0_attn_b_attn"]
    W_proj, b_proj = i["L0_attn_W_proj"], i["L0_attn_b_proj"]

    c = np.ascontiguousarray
    w1r = np.asarray(w1).reshape(CF, 128, KD, 128).transpose(0, 3, 2, 1)
    w3r = np.asarray(w3).reshape(CF, 128, KD, 128).transpose(0, 3, 2, 1)
    w13v = c(np.stack([w1r, w3r], axis=2).reshape(CF, 128, 4096))
    b1r = np.asarray(b1).reshape(E, CF, 128, R).transpose(1, 3, 0, 2)
    b3r = np.asarray(b3).reshape(E, CF, 128, R).transpose(1, 3, 0, 2)
    b13v = c(np.stack([b1r, b3r], axis=2).reshape(CF, 16, 2048))
    a1r = np.asarray(a1).reshape(E, R, KD, 128).transpose(3, 2, 0, 1)
    a3r = np.asarray(a3).reshape(E, R, KD, 128).transpose(3, 2, 0, 1)
    a13v = c(np.stack([a1r, a3r], axis=1).reshape(128, 4096))
    a2v = c(np.asarray(a2).reshape(E, R, CF, 128)
            .transpose(3, 2, 0, 1).reshape(128, CF * 128))
    b2v = c(np.asarray(b2).reshape(E, KD, 128, R)
            .transpose(3, 1, 0, 2).reshape(16, KD * E * 128))
    gwv = c(np.asarray(gw).reshape(E, KD, 128).transpose(2, 1, 0)
            .reshape(128, KD * E))
    wqv = c(np.asarray(W_attn).reshape(KD, 128, 48, 128)
            .transpose(2, 1, 0, 3).reshape(48, 128, 2048))
    wpv = c(np.asarray(W_proj).reshape(KD, 128, KD, 128)
            .transpose(2, 1, 0, 3).reshape(KD, 128, 2048))
    w2v = c(np.asarray(w2).reshape(KD, 128, CF, 128)
            .transpose(0, 3, 2, 1).reshape(KD, 128, F))
    bav = c(np.asarray(b_attn).reshape(48, 128).T)
    bpv = c(np.asarray(b_proj).reshape(KD, 128).T)
    return dict(w13=w13v, b13=b13v, a13=a13v, a2pi=a2v, b2pi=b2v, gwt=gwv,
                wq=wqv, wp=wpv, w2r=w2v, battn=bav, bproj=bpv)


def _prep_meta(core):
    stbv = np.zeros((NQ * NSLOT, 128), np.float32)
    for qt in range(NQ):
        for s in range(NSLOT):
            cl = qt + s
            col = M_AL * (cl * 128 + np.arange(128, dtype=np.float32))
            if core == 0 and cl < 2:
                col = col + NEGM
            stbv[qt * NSLOT + s] = col
    qau = np.zeros((2, NQ * 128), np.float32)
    for qt in range(NQ):
        qau[0, qt * 128:(qt + 1) * 128] = -M_AL * (
            (qt + 2) * 128 + np.arange(128, dtype=np.float32)) - 12.0
    qau[1] = 1.0
    mk = np.zeros((2, T_ALL), np.float32)
    mk[0] = 1.0
    mk[1] = M_AL * np.arange(T_ALL, dtype=np.float32)
    tri_v = np.where(np.arange(128)[None, :] > np.arange(128)[:, None],
                     np.float32(NEGM), np.float32(0.0))
    return dict(stb=np.ascontiguousarray(stbv.T), qaug=qau, mkrow=mk,
                ident=np.eye(128, dtype=np.float32),
                tri=np.ascontiguousarray(tri_v),
                trit=np.ascontiguousarray(tri_v.T),
                onesf=np.ones((1, 128), np.float32))


def kernel(**inputs):
    x = np.asarray(inputs["L0_attn_hidden_states"], np.float32)[0]
    xT = np.ascontiguousarray(x.T)
    wdict = _prep_weights(inputs)
    nc = build_program()
    in_maps = []
    for corei in range(8):
        g0 = corei * T_OWN
        if corei == 0:
            halo = np.zeros((D, HALO), np.float32)
        else:
            halo = xT[:, g0 - HALO:g0]
        x0c = np.ascontiguousarray(
            np.concatenate([halo, xT[:, g0:g0 + T_OWN]], axis=1))
        m = dict(wdict)
        m.update(_prep_meta(corei))
        m["x0"] = x0c
        in_maps.append(m)
    res = run_bass_kernel_spmd(nc, in_maps, core_ids=list(range(8)))
    out = np.empty((8192, D), np.float32)
    for corei in range(8):
        out[corei * T_OWN:(corei + 1) * T_OWN, :] = res.results[corei]["y"].T
    return out.reshape(1, 8192, D)


# version 11
# speedup vs baseline: 1.2725x; 1.2725x over previous
import numpy as np
import concourse.bass as bass
import concourse.mybir as mybir
from concourse.bass_utils import run_bass_kernel_spmd

F32 = mybir.dt.float32
BF16 = mybir.dt.bfloat16
AF = mybir.ActivationFunctionType
ALU = mybir.AluOpType
AX = mybir.AxisListType

T_OWN, HALO = 1024, 256
T_ALL = T_OWN + HALO
D, KD, F, CF, E, R, NH, DH = 2048, 16, 5632, 44, 8, 16, 32, 64
NT, NQ, NSLOT = T_ALL // 128, T_OWN // 128, 3
M_AL = 2.0 ** (-0.25)
BLK1 = [(0, 512), (512, 512), (1024, 256)]
BLK2 = [(256, 512), (768, 512)]
NEGM = -30000.0




class _InstRec:
    __slots__ = ("chain",)

    def __init__(self):
        self.chain = []

    def then_inc(self, sem, val):
        self.chain.append(("then_inc", (sem, val)))
        return self


class _EngRec:
    __slots__ = ("calls",)

    def __init__(self):
        self.calls = []

    def __getattr__(self, name):
        def _call(*a, **k):
            inst = _InstRec()
            self.calls.append((name, a, k, inst))
            return inst
        return _call

    def replay(self, eng):
        for name, a, k, inst in self.calls:
            real = getattr(eng, name)(*a, **k)
            for cn, ca in inst.chain:
                real = getattr(real, cn)(*ca)


class Led:
    def __init__(self):
        self.c = {}

    def inc(self, s, n=1):
        self.c[s.name] = self.c.get(s.name, 0) + n
        return self.c[s.name]

    def val(self, s):
        return self.c.get(s.name, 0)


def build_program():
    nc = bass.Bass("TRN2", target_bir_lowering=False, debug=False)
    led = Led()

    def inp(name, shape):
        return nc.dram_tensor(name, shape, F32, kind="ExternalInput").ap()

    x0 = inp("x0", [D, T_ALL])
    w13 = inp("w13", [CF, 128, 4096])
    b13 = inp("b13", [CF, 16, 2048])
    a13 = inp("a13", [128, 4096])
    a2pi = inp("a2pi", [128, CF * 128])
    b2pi = inp("b2pi", [16, KD * E * 128])
    gwt = inp("gwt", [128, KD * E])
    wq = inp("wq", [48, 128, 2048])
    wp = inp("wp", [KD, 128, 2048])
    w2r = inp("w2r", [KD, 128, F])
    battn = inp("battn", [128, 48])
    bproj = inp("bproj", [128, KD])
    stb = inp("stb", [128, NQ * NSLOT])
    qaug = inp("qaug", [2, NQ * 128])
    mkrow = inp("mkrow", [2, T_ALL])
    ident = inp("ident", [128, 128])
    tri = inp("tri", [128, 128])
    trit = inp("trit", [128, 128])
    onesf = inp("onesf", [1, 128])

    y = nc.dram_tensor("y", [D, T_OWN], F32, kind="ExternalOutput").ap()

    xres = nc.dram_tensor("xres", [D, T_ALL], F32).ap()
    w13b = nc.dram_tensor("w13b", [CF, 128, 4096], BF16).ap()
    b13bd = nc.dram_tensor("b13bd", [CF, 16, 2048], BF16).ap()
    wqb = nc.dram_tensor("wqb", [48, 128, 2048], BF16).ap()
    wpb = nc.dram_tensor("wpb", [KD, 128, 2048], BF16).ap()
    w2rb = nc.dram_tensor("w2rb", [KD, 128, F], BF16).ap()
    b2pb = nc.dram_tensor("b2pb", [16, KD * E * 128], BF16).ap()
    qd = nc.dram_tensor("qd", [D, T_ALL], BF16).ap()
    kdr = nc.dram_tensor("kdr", [66, NH * T_ALL], BF16).ap()
    vdr = nc.dram_tensor("vdr", [T_ALL, NH * DH], BF16).ap()
    amx = nc.dram_tensor("amx", [CF, 128, T_ALL], BF16).ap()

    import contextlib
    with contextlib.ExitStack() as _st:
        xn_all = _st.enter_context(nc.sbuf_tensor("xn_all", [128, KD * T_ALL], BF16))
        REGA = _st.enter_context(nc.sbuf_tensor("REGA", [128, 2 * E * T_ALL], BF16))
        REGB = _st.enter_context(nc.sbuf_tensor("REGB", [128, E * T_ALL], BF16))
        REGC = _st.enter_context(nc.sbuf_tensor("REGC", [128, KD * T_OWN], BF16))
        REGD = _st.enter_context(nc.sbuf_tensor("REGD", [128, 17408], BF16))
        REGF = _st.enter_context(nc.sbuf_tensor("REGF", [128, 4096], F32))
        a13b = _st.enter_context(nc.sbuf_tensor("a13b", [128, 4096], BF16))
        a2pb = _st.enter_context(nc.sbuf_tensor("a2pb", [128, CF * 128], BF16))
        gwb = _st.enter_context(nc.sbuf_tensor("gwb", [128, KD * E], BF16))
        wfull = _st.enter_context(nc.sbuf_tensor("wfull", [128, E * NT], F32))
        battn_s = _st.enter_context(nc.sbuf_tensor("battn_s", [128, 48], F32))
        battn_q = _st.enter_context(nc.sbuf_tensor("battn_q", [128, 16], F32))
        bproj_s = _st.enter_context(nc.sbuf_tensor("bproj_s", [128, KD], F32))
        stb_s = _st.enter_context(nc.sbuf_tensor("stb_s", [128, NQ * NSLOT], F32))
        qaug_b = _st.enter_context(nc.sbuf_tensor("qaug_b", [2, NQ * 128], BF16))
        id_b = _st.enter_context(nc.sbuf_tensor("id_b", [128, 128], BF16))
        tri_s = _st.enter_context(nc.sbuf_tensor("tri_s", [128, 128], F32))
        trit_s = _st.enter_context(nc.sbuf_tensor("trit_s", [128, 128], F32))
        ones_f = _st.enter_context(nc.sbuf_tensor("ones_f", [1, 128], F32))
        onescol_b = _st.enter_context(nc.sbuf_tensor("onescol_b", [128, 1], BF16))
        one_b = _st.enter_context(nc.sbuf_tensor("one_b", [1, 1], BF16))
        grow = _st.enter_context(nc.sbuf_tensor("grow", [128, 40], F32))
        wf_b = _st.enter_context(nc.sbuf_tensor("wf_b", [128, 8], BF16))
        nmx_b = _st.enter_context(nc.sbuf_tensor("nmx_b", [128, 1], BF16))
        nmx2 = _st.enter_context(nc.sbuf_tensor("nmx2", [128, 2], BF16))
        eps_s = _st.enter_context(nc.sbuf_tensor("eps_s", [128, 1], F32))
        onesrow_b = _st.enter_context(nc.sbuf_tensor("onesrow_b", [1, 128], BF16))
        qa2 = _st.enter_context(nc.sbuf_tensor("qa2", [65, 128], BF16))
        tm_b = _st.enter_context(nc.sbuf_tensor("tm_b", [128, 4 * 64], BF16))
        ph1A = _st.enter_context(nc.psum_tensor("ph1A", [128, 512], F32))
        ph3A = _st.enter_context(nc.psum_tensor("ph3A", [128, 512], F32))
        ph1B = _st.enter_context(nc.psum_tensor("ph1B", [128, 512], F32))
        ph3B = _st.enter_context(nc.psum_tensor("ph3B", [128, 512], F32))
        pamix = _st.enter_context(nc.psum_tensor("pamix", [128, 512], F32))
        pu2a = _st.enter_context(nc.psum_tensor("pu2a", [128, 512], F32))
        pu2b = _st.enter_context(nc.psum_tensor("pu2b", [128, 512], F32))
        pu2c = _st.enter_context(nc.psum_tensor("pu2c", [128, 512], F32))
        s_ld = _st.enter_context(nc.semaphore("s_ld"))
        s_gd = _st.enter_context(nc.semaphore("s_gd"))
        s_xa = _st.enter_context(nc.semaphore("s_xa"))
        s_xc = _st.enter_context(nc.semaphore("s_xc"))
        s_xb = _st.enter_context(nc.semaphore("s_xb"))
        s_wa = _st.enter_context(nc.semaphore("s_wa"))
        s_wb = _st.enter_context(nc.semaphore("s_wb"))
        s_oa = _st.enter_context(nc.semaphore("s_oa"))
        s_ob = _st.enter_context(nc.semaphore("s_ob"))
        s_wd = _st.enter_context(nc.semaphore("s_wd"))
        s_xd = _st.enter_context(nc.semaphore("s_xd"))
        s_od = _st.enter_context(nc.semaphore("s_od"))
        s_pe = _st.enter_context(nc.semaphore("s_pe"))
        s_pe2 = _st.enter_context(nc.semaphore("s_pe2"))
        s_act = _st.enter_context(nc.semaphore("s_act"))
        s_dve = _st.enter_context(nc.semaphore("s_dve"))
        s_pool = _st.enter_context(nc.semaphore("s_pool"))
        s_bar = _st.enter_context(nc.semaphore("s_bar"))
        B = _st.enter_context(nc.Block())
        uN = REGA[64:80, :]
        u2T = REGA[32:48, 0:E * T_ALL]
        kring = REGA[0:66, 0:4 * NH * 128]
        wb_sb = REGB[:, :]
        vring = REGB[:, 0:4 * NH * 65]
        s_bf = REGC[:, 0:1536]
        act_bf = REGC[:, 1536:3072]
        actw_bf = REGC[:, 3072:7168]
        amx_st = REGC[:, 7168:8192]
        u2s = REGC[:, 8192:8704]
        silu_sb = REGC[:, 8704:10240]
        tmp_sb = REGC[:, 10240:11264]
        b13A = REGD[64:80, 8192:10240]
        b13B = REGD[64:80, 10240:12288]
        bsREG = [REGC[:, 12800 + i * 512:12800 + (i + 1) * 512] for i in range(4)]
        qkst = REGC[:, 14848:15872]
        xt16 = REGC[:, 0:2048]
        attn_sb = REGC[:, :]
        h1s = REGF[:, 1024:2560]
        h3s = REGF[:, 2560:4096]
        ws = REGD[:, 0:8192]
        w2sl = REGD[:, 0:11264]
        b2dg = REGD[32:48, 11264:13312]
        adg = REGD[:, 13312:17408]
        wtrow = REGD[0:1, 0:E * NT * 128]
        wqs = REGD[:, 0:4096]
        q_sb = REGD[0:64, 4096:12288]
        pt_bf = REGD[:, 12288:12288 + 4 * NSLOT * 128]
        qaug64 = REGD[64:66, 4096:4096 + NQ * 128]
        qr64 = REGD[64:65, 13056:13056 + 256]
        xln = REGF[:, 0:512]
        xln2 = REGF[:, 512:1024]
        xfp = REGF[:, 1024:1536]
        rows = REGF[0:1, 1536:3584]
        xio = REGF[:, 0:2048]
        xout = REGF[:, 2048:4096]
        xpj = REGF[:, 1536:2560]
        amixf = REGF[:, 0:1024]

        PU2 = [pu2a, pu2b, pu2c]
        R_pe, R_sc, R_ve, R_gp, R_sy = (_EngRec() for _ in range(5))
        pe = lambda f: f(R_pe)
        sc = lambda f: f(R_sc)
        ve = lambda f: f(R_ve)
        gp = lambda f: f(R_gp)
        sy = lambda f: f(R_sy)
        nbar = [0]

        def barrier():
            nbar[0] += 1
            tgt = 5 * nbar[0]
            for em in (pe, sc, ve, gp, sy):
                def _f(e, tgt=tgt):
                    e.drain().then_inc(s_bar, 1)
                    e.wait_ge(s_bar, tgt)
                em(_f)
            led.c[s_bar.name] = tgt

        def _p0g(g):
            for t_in, t_out, n in ((w13, w13b, CF), (b13, b13bd, CF),
                                   (wq, wqb, 48), (wp, wpb, KD), (w2r, w2rb, KD)):
                step = max(1, (n + 3) // 4)
                i = 0
                while i < n:
                    j = min(n, i + step)
                    g.dma_start(out=t_out[i:j], in_=t_in[i:j]).then_inc(s_gd, 16)
                    led.inc(s_gd, 16)
                    i = j
            g.dma_start(out=b2pb[:, :], in_=b2pi).then_inc(s_gd, 16)
            led.inc(s_gd, 16)
            for t_in, t_out in ((a13, a13b[:, :]), (a2pi, a2pb[:, :]),
                                (gwt, gwb[:, :]), (ident, id_b[:, :]),
                                (qaug, qaug_b[:, :])):
                g.dma_start(out=t_out, in_=t_in).then_inc(s_gd, 16)
                led.inc(s_gd, 16)
            for h in range(NH):
                g.dma_start(out=kdr[64:66, h * T_ALL:(h + 1) * T_ALL],
                            in_=mkrow).then_inc(s_gd, 16)
                led.inc(s_gd, 16)
        gp(_p0g)

        def _p0s(s):
            for t_in, t_out in ((battn, battn_s[:, :]), (bproj, bproj_s[:, :]),
                                (stb, stb_s[:, :]), (tri, tri_s[:, :]),
                                (trit, trit_s[:, :]), (onesf, ones_f[:, :])):
                s.dma_start(out=t_out, in_=t_in).then_inc(s_ld, 16)
                led.inc(s_ld, 16)
        sy(_p0s)
        ldt = led.val(s_ld)
        gdt = led.val(s_gd)

        def _p0v(v, ldt=ldt, gdt=gdt):
            v.wait_ge(s_ld, ldt)
            v.memset(onescol_b[:, :], 1.0)
            v.memset(one_b[:, :], 1.0)
            v.memset(eps_s[:, :], 1e-5)
            v.memset(onesrow_b[:, :], 1.0)
        ve(_p0v)

        def _p0sc(s, ldt=ldt, gdt=gdt):
            s.wait_ge(s_ld, ldt)
            s.activation(battn_q[:, :], battn_s[:, 0:16], AF.Identity, scale=0.125)
        sc(_p0sc)
        barrier()

        def ln_phase(src_ap, blocks):
            for (boff, w) in blocks:
                dve0 = led.val(s_dve)
                pe0 = led.val(s_pe)
                for k in range(KD):
                    sem = s_xa if k % 2 == 0 else s_xb

                    def _ld(s2, k=k, boff=boff, w=w, dve0=dve0, sem=sem):
                        s2.wait_ge(s_dve, dve0 + k - 1 if k >= 2 else dve0)
                        dst = xln if k % 2 == 0 else xln2
                        s2.dma_start(out=dst[:, 0:w],
                                     in_=src_ap[k * 128:(k + 1) * 128,
                                                boff:boff + w]).then_inc(sem, 16)
                        led.inc(sem, 16)
                    sy(_ld)
                    xdt = led.val(sem)

                    def _cast(v, k=k, w=w, xdt=xdt, sem=sem, pe0=pe0):
                        v.wait_ge(sem, xdt)
                        if k >= 2:
                            v.wait_ge(s_pe, pe0 + k - 1)
                        src = xln if k % 2 == 0 else xln2
                        xb = xt16[:, (k % 2) * 1024:(k % 2) * 1024 + w]
                        xq = xt16[:, (k % 2) * 1024 + 512:(k % 2) * 1024 + 512 + w]
                        v.tensor_copy(xb, src[:, 0:w])
                        v.tensor_mul(xq, src[:, 0:w], src[:, 0:w]).then_inc(s_dve, 1)
                        led.inc(s_dve, 1)
                    ve(_cast)
                    dvt = led.val(s_dve)

                    def _mm(t, k=k, w=w, dvt=dvt):
                        t.wait_ge(s_dve, dvt)
                        xb = xt16[:, (k % 2) * 1024:(k % 2) * 1024 + w]
                        xq = xt16[:, (k % 2) * 1024 + 512:(k % 2) * 1024 + 512 + w]
                        t.matmul(pamix[0:1, 0:w], onescol_b[:, :], xb,
                                 start=(k == 0), stop=(k == KD - 1))
                        t.matmul(ph1B[0:1, 0:w], onescol_b[:, :], xq,
                                 start=(k == 0),
                                 stop=(k == KD - 1)).then_inc(s_pe, 1)
                        led.inc(s_pe, 1)
                    pe(_mm)
                pet = led.val(s_pe)

                def _st1(s, w=w, pet=pet):
                    s.wait_ge(s_pe, pet)
                    s.activation(rows[:, 0:w], pamix[0:1, 0:w], AF.Identity,
                                 scale=1.0 / D)
                    s.activation(rows[:, 512:512 + w], ph1B[0:1, 0:w],
                                 AF.Identity, scale=1.0 / D).then_inc(s_act, 1)
                    led.inc(s_act, 1)
                sc(_st1)
                at = led.val(s_act)

                def _st2(v, w=w, at=at):
                    v.wait_ge(s_act, at)
                    v.tensor_mul(rows[:, 1024:1024 + w], rows[:, 0:w],
                                 rows[:, 0:w]).then_inc(s_dve, 1)
                    v.wait_ge(s_dve, led.inc(s_dve, 1))
                    v.tensor_sub(rows[:, 512:512 + w], rows[:, 512:512 + w],
                                 rows[:, 1024:1024 + w]).then_inc(s_dve, 1)
                    led.inc(s_dve, 1)
                ve(_st2)
                dvt2 = led.val(s_dve)

                def _st3(s, w=w, dvt2=dvt2):
                    s.wait_ge(s_dve, dvt2)
                    s.activation(rows[:, 512:512 + w], rows[:, 512:512 + w],
                                 AF.Ln, bias=eps_s[0:1, 0:1]).then_inc(s_act, 1)
                    s.wait_ge(s_act, led.inc(s_act, 1))
                    s.activation(rows[:, 512:512 + w], rows[:, 512:512 + w],
                                 AF.Exp, scale=-0.5).then_inc(s_act, 1)
                    led.inc(s_act, 1)
                sc(_st3)
                at2 = led.val(s_act)

                def _st4(v, w=w, at2=at2):
                    v.wait_ge(s_act, at2)
                    v.tensor_mul(rows[:, 1536:1536 + w], rows[:, 0:w],
                                 rows[:, 512:512 + w]).then_inc(s_dve, 1)
                    led.inc(s_dve, 1)
                ve(_st4)
                dvt3 = led.val(s_dve)

                def _bc(t, w=w, dvt3=dvt3):
                    t.wait_ge(s_dve, dvt3)
                    t.matmul(ph1A[:, 0:w], ones_f[:, :], rows[:, 512:512 + w],
                             start=True, stop=True)
                    t.matmul(ph3A[:, 0:w], ones_f[:, :], rows[:, 1536:1536 + w],
                             start=True, stop=True).then_inc(s_pe, 1)
                    led.inc(s_pe, 1)
                pe(_bc)
                pet2 = led.val(s_pe)
                dve1 = led.val(s_dve)
                for k in range(KD):
                    sem = s_xa if k % 2 == 0 else s_xb

                    def _ld2(s2, k=k, boff=boff, w=w, dve1=dve1, sem=sem):
                        s2.wait_ge(s_dve, dve1 + 2 * (k - 1) if k >= 2 else dve1)
                        dst = xln if k % 2 == 0 else xln2
                        s2.dma_start(out=dst[:, 0:w],
                                     in_=src_ap[k * 128:(k + 1) * 128,
                                                boff:boff + w]).then_inc(sem, 16)
                        led.inc(sem, 16)
                    sy(_ld2)
                    xdt2 = led.val(sem)

                    dvp = led.val(s_dve)

                    def _nm(v, k=k, boff=boff, w=w, xdt2=xdt2, pet2=pet2, sem=sem,
                            dvp=dvp):
                        v.wait_ge(sem, xdt2)
                        v.wait_ge(s_dve, dvp)
                        if k == 0:
                            v.wait_ge(s_pe, pet2)
                        src = xln if k % 2 == 0 else xln2
                        v.tensor_mul(xfp[:, 0:w], src[:, 0:w],
                                     ph1A[:, 0:w]).then_inc(s_dve, 1)
                        v.wait_ge(s_dve, led.inc(s_dve, 1))
                        v.tensor_sub(xn_all[:, k * T_ALL + boff:
                                            k * T_ALL + boff + w],
                                     xfp[:, 0:w], ph3A[:, 0:w]).then_inc(s_dve, 1)
                        led.inc(s_dve, 1)
                    ve(_nm)
            barrier()

        def gate_phase(tiles):
            for gi in tiles:
                def _gmm(t, gi=gi):
                    for k in range(KD):
                        mm = t.matmul(pamix[:, 0:8],
                                      xn_all[:, k * T_ALL + gi * 128:
                                             k * T_ALL + gi * 128 + 128],
                                      gwb[:, k * 8:(k + 1) * 8],
                                      start=(k == 0), stop=(k == KD - 1))
                    mm.then_inc(s_pe, 1)
                    led.inc(s_pe, 1)
                pe(_gmm)
                pt = led.val(s_pe)

                def _g1(v, pt=pt):
                    v.wait_ge(s_pe, pt)
                    v.tensor_reduce(grow[:, 0:1], pamix[:, 0:8], AX.X,
                                    ALU.max).then_inc(s_dve, 1)
                    v.tensor_reduce(grow[:, 1:2], pamix[:, 0:8], AX.X, ALU.max,
                                    negate=True)
                    v.wait_ge(s_dve, led.inc(s_dve, 1))
                    v.tensor_single_scalar(grow[:, 8:16], pamix[:, 0:8],
                                           grow[:, 0:1], ALU.is_ge).then_inc(s_dve, 1)
                    v.wait_ge(s_dve, led.inc(s_dve, 1))
                    v.scalar_tensor_tensor(grow[:, 16:24], grow[:, 8:16], -1e30,
                                           pamix[:, 0:8], ALU.mult,
                                           ALU.add).then_inc(s_dve, 1)
                    v.wait_ge(s_dve, led.inc(s_dve, 1))
                    v.tensor_reduce(grow[:, 2:3], grow[:, 16:24], AX.X,
                                    ALU.max).then_inc(s_dve, 1)
                    led.inc(s_dve, 1)
                ve(_g1)
                dt = led.val(s_dve)

                def _g2(s, dt=dt):
                    s.wait_ge(s_dve, dt)
                    s.activation(grow[:, 24:32], pamix[:, 0:8], AF.Exp,
                                 bias=grow[:, 1:2])
                    s.activation(grow[:, 3:4], grow[:, 2:3], AF.Exp,
                                 bias=grow[:, 1:2]).then_inc(s_act, 1)
                    led.inc(s_act, 1)
                sc(_g2)
                at = led.val(s_act)

                def _g3(v, gi=gi, at=at):
                    v.wait_ge(s_act, at)
                    v.tensor_single_scalar(grow[:, 4:5], grow[:, 3:4], 1.0,
                                           ALU.add).then_inc(s_dve, 1)
                    v.wait_ge(s_dve, led.inc(s_dve, 1))
                    v.reciprocal(grow[:, 5:6], grow[:, 4:5])
                    v.tensor_single_scalar(grow[:, 16:24], pamix[:, 0:8],
                                           grow[:, 2:3], ALU.is_ge).then_inc(s_dve, 1)
                    v.wait_ge(s_dve, led.inc(s_dve, 1))
                    v.scalar_tensor_tensor(wfull[:, gi * 8:gi * 8 + 8],
                                           grow[:, 24:32], grow[:, 5:6],
                                           grow[:, 16:24], ALU.mult,
                                           ALU.mult).then_inc(s_dve, 1)
                    v.wait_ge(s_dve, led.inc(s_dve, 1))
                    v.tensor_copy(wf_b[:, :],
                                  wfull[:, gi * 8:gi * 8 + 8]).then_inc(s_dve, 1)
                    led.inc(s_dve, 1)
                ve(_g3)
                dt2 = led.val(s_dve)

                for e in range(E):
                    def _g4b(t, e=e, dt2=dt2):
                        t.wait_ge(s_dve, dt2)
                        if e > 0:
                            t.wait_ge(s_act, led.val(s_act))
                        t.matmul(ph1B[0:1, 0:128], wf_b[:, e:e + 1], id_b[:, :],
                                 start=True, stop=True).then_inc(s_pe, 1)
                        led.inc(s_pe, 1)
                    pe(_g4b)
                    pt2 = led.val(s_pe)

                    def _g5(s, gi=gi, e=e, pt2=pt2):
                        s.wait_ge(s_pe, pt2)
                        s.activation(wtrow[:, (e * NT + gi) * 128:
                                           (e * NT + gi) * 128 + 128],
                                     ph1B[0:1, 0:128],
                                     AF.Identity).then_inc(s_act, 1)
                        led.inc(s_act, 1)
                    sc(_g5)
            att = led.val(s_act)
            for e in range(E):
                for gi in tiles:
                    def _g6(t, e=e, gi=gi, att=att):
                        t.wait_ge(s_act, att)
                        if not (e == 0 and gi == tiles[0]):
                            t.wait_ge(s_act, led.val(s_act))
                        t.matmul(ph3B[:, 0:128], onesrow_b[:, :],
                                 wtrow[:, (e * NT + gi) * 128:
                                       (e * NT + gi) * 128 + 128],
                                 start=True, stop=True).then_inc(s_pe, 1)
                        led.inc(s_pe, 1)
                    pe(_g6)
                    pt3 = led.val(s_pe)

                    def _g7(s, e=e, gi=gi, pt3=pt3):
                        s.wait_ge(s_pe, pt3)
                        s.activation(wb_sb[:, e * T_ALL + gi * 128:
                                           e * T_ALL + gi * 128 + 128],
                                     ph3B[:, 0:128], AF.Identity).then_inc(s_act, 1)
                        led.inc(s_act, 1)
                    sc(_g7)
            barrier()

        def uprod(blocks):
            prev_at = [led.val(s_act)]
            prev_xd = [led.val(s_gd)]
            for p in range(2):
                for (boff, w) in blocks:
                    def _u1(t, p=p, boff=boff, w=w, pa=prev_at[0]):
                        t.wait_ge(s_act, pa)
                        for k in range(KD):
                            mm = t.matmul(ph1A[:, 0:w],
                                          a13b[:, (p * KD + k) * 128:
                                               (p * KD + k + 1) * 128],
                                          xn_all[:, k * T_ALL + boff:
                                                 k * T_ALL + boff + w],
                                          start=(k == 0), stop=(k == KD - 1))
                            if k == KD - 1:
                                mm.then_inc(s_pe, 1)
                        led.inc(s_pe, 1)
                    pe(_u1)
                    pt = led.val(s_pe)

                    def _u2(s, w=w, pt=pt, px=prev_xd[0]):
                        s.wait_ge(s_pe, pt)
                        s.wait_ge(s_gd, px)
                        s.activation(u2s[:, 0:w], ph1A[:, 0:w], AF.Identity,
                                     scale=2.0).then_inc(s_act, 1)
                        led.inc(s_act, 1)
                    sc(_u2)
                    at = led.val(s_act)
                    prev_at[0] = at

                    def _u3(g, p=p, boff=boff, w=w, at=at):
                        g.wait_ge(s_act, at)
                        for e in range(E):
                            col = (p * E + e) * T_ALL + boff
                            g.dma_start(out=uN[:, col:col + w],
                                        in_=u2s[16 * e:16 * e + 16, 0:w]
                                        ).then_inc(s_gd, 16)
                            led.inc(s_gd, 16)
                    gp(_u3)
                    prev_xd[0] = led.val(s_gd)
            gfin = led.val(s_gd)

            def _udrain(t, gfin=gfin):
                t.wait_ge(s_gd, gfin)
            pe(_udrain)
            barrier()

        def floop(blocks):
            H_LORA = [ph1B, ph3B, pamix]
            w13t, chunk_pe_done, o_fin = {}, {}, {}
            P0_ = led.val(s_pe)
            A0_ = led.val(s_act)
            D0_ = led.val(s_dve)
            PL_ = led.val(s_pool)
            P2_ = led.val(s_pe2)
            actwP = [[actw_bf[:, e * 512:(e + 1) * 512] for e in range(E)],
                     [REGC[:, e * 512:(e + 1) * 512] for e in range(6)]
                     + [REGC[:, 11264 + e * 512:11264 + (e + 1) * 512]
                        for e in range(2)]]

            def h_lora(G):
                return P0_ + (G // 16) * 18 + 2 + (G % 16) + 1

            def h_pe_done(u):
                return P0_ + (u + 1) * 18

            def h_bsev(u):
                return A0_ + u * 11 + 2

            def h_silu(g):
                return A0_ + (g // 8) * 11 + 2 + (g % 8) + 1

            def h_amxev(u):
                return A0_ + (u + 1) * 11

            def h_add(G):
                return D0_ + (G // 16) * 24 + (G % 16) + 1

            def h_amixd(u):
                return D0_ + (u + 1) * 24

            def h_tmp1(g):
                return PL_ + (g // 8) * 16 + 2 * (g % 8) + 1

            def h_actw(g):
                return PL_ + (g // 8) * 16 + 2 * (g % 8) + 2

            def h_u2(u):
                return P2_ + u + 1

            unit_meta = []
            NB = len(blocks)

            def emit_u2(t, up):
                cu, biu, boffu, wu = unit_meta[up]
                t.wait_ge(s_pool, h_actw(up * 8 + E - 1))
                aw = actwP[up % 2]
                mm = None
                for ep in range(E):
                    for sl in range(wu // 128):
                        mm = t.matmul(
                            PU2[biu][:, (sl * E + ep) * 16:
                                     (sl * E + ep) * 16 + 16],
                            aw[ep][:, sl * 128:(sl + 1) * 128],
                            a2pb[:, (cu * E + ep) * 16:(cu * E + ep) * 16 + 16],
                            start=(cu == 0 and ep == 0 and sl == 0),
                            stop=(cu == CF - 1 and ep == E - 1
                                  and sl == wu // 128 - 1))
                mm.then_inc(s_pe2, 1)
                led.inc(s_pe2, 1)

            def load_chunk(c, wait_pe):
                wsem = s_wa if c % 2 == 0 else s_wb
                led.inc(wsem, 32)

                def _w(s2, c=c, wait_pe=wait_pe, wsem=wsem):
                    if wait_pe is not None:
                        s2.wait_ge(s_pe, wait_pe)
                    par = c % 2
                    s2.dma_start(out=ws[:, par * 4096:(par + 1) * 4096],
                                 in_=w13b[c]).then_inc(wsem, 16)
                    s2.dma_start(out=(b13A if par == 0 else b13B),
                                 in_=b13bd[c]).then_inc(wsem, 16)
                sy(_w)
                w13t[c] = (wsem, led.val(wsem))

            load_chunk(0, None)
            u = -1
            for c in range(CF):
                if c + 1 < CF:
                    load_chunk(c + 1, chunk_pe_done.get(c - 1))
                par = c % 2
                wtile = ws[:, par * 4096:(par + 1) * 4096]
                bt = b13A if par == 0 else b13B
                for bi, (boff, w) in enumerate(blocks):
                    u += 1
                    g0 = u * E
                    G0 = u * 16
                    osem = s_oa if u % 2 == 0 else s_ob
                    unit_meta.append((c, bi, boff, w))

                    def _pe(t, c=c, bi=bi, boff=boff, w=w, u=u,
                            wtile=wtile, bt=bt, g0=g0, G0=G0):
                        t.wait_ge(w13t[c][0], w13t[c][1])
                        if u >= 1:
                            t.wait_ge(s_act, h_bsev(u - 1))
                        for k in range(KD):
                            mm = t.matmul(ph1A[:, 0:w],
                                          wtile[:, k * 128:(k + 1) * 128],
                                          xn_all[:, k * T_ALL + boff:
                                                 k * T_ALL + boff + w],
                                          start=(k == 0), stop=(k == KD - 1))
                        mm.then_inc(s_pe, 1)
                        led.inc(s_pe, 1)
                        for k in range(KD):
                            mm = t.matmul(ph3A[:, 0:w],
                                          wtile[:, (KD + k) * 128:
                                                (KD + k + 1) * 128],
                                          xn_all[:, k * T_ALL + boff:
                                                 k * T_ALL + boff + w],
                                          start=(k == 0), stop=(k == KD - 1))
                        mm.then_inc(s_pe, 1)
                        led.inc(s_pe, 1)
                        for j in range(16):
                            e, half = j // 2, j % 2
                            L = H_LORA[(G0 + j) % 3]
                            if G0 + j >= 3:
                                t.wait_ge(s_dve, h_add(G0 + j - 3))
                            col = e * 128 if half == 0 else 1024 + e * 128
                            roff = (e if half == 0 else E + e) * T_ALL + boff
                            t.matmul(L[:, 0:w], bt[:, col:col + 128],
                                     uN[:, roff:roff + w],
                                     start=True, stop=True).then_inc(s_pe, 1)
                            led.inc(s_pe, 1)
                        if u >= 1:
                            emit_u2(t, u - 1)
                    pe(_pe)
                    assert led.val(s_pe) == h_pe_done(u)

                    def _sc(s, u=u, w=w, g0=g0, G0=G0):
                        s.wait_ge(s_pe, P0_ + u * 18 + 2)
                        if u >= 2:
                            s.wait_ge(s_dve, h_add(G0 - 17))
                        s.activation(bsREG[(u % 2) * 2][:, 0:w], ph1A[:, 0:w],
                                     AF.Identity).then_inc(s_act, 1)
                        led.inc(s_act, 1)
                        s.activation(bsREG[(u % 2) * 2 + 1][:, 0:w],
                                     ph3A[:, 0:w],
                                     AF.Identity).then_inc(s_act, 1)
                        led.inc(s_act, 1)
                        for e in range(E):
                            g = g0 + e
                            s.wait_ge(s_dve, h_add(G0 + 2 * e))
                            if g >= 3:
                                s.wait_ge(s_pool, h_tmp1(g - 3))
                            s.activation(silu_sb[:, (g % 3) * 512:
                                                 (g % 3) * 512 + w],
                                         h1s[:, (g % 3) * 512:(g % 3) * 512 + w],
                                         AF.Silu).then_inc(s_act, 1)
                            led.inc(s_act, 1)
                        s.wait_ge(s_dve, h_amixd(u))
                        if u >= 2:
                            s.wait_ge(o_fin[u - 2][0], o_fin[u - 2][1])
                        s.activation(amx_st[:, (u % 2) * 512:(u % 2) * 512 + w],
                                     amixf[:, (u % 2) * 512:(u % 2) * 512 + w],
                                     AF.Identity).then_inc(s_act, 1)
                        led.inc(s_act, 1)
                    sc(_sc)
                    assert led.val(s_act) == h_amxev(u)

                    def _ve(v, u=u, w=w, g0=g0, G0=G0):
                        aw = actwP[u % 2]
                        for j in range(16):
                            e, half = j // 2, j % 2
                            g = g0 + e
                            G = G0 + j
                            L = H_LORA[G % 3]
                            v.wait_ge(s_pe, h_lora(G))
                            if j == 0:
                                v.wait_ge(s_act, h_bsev(u))
                            if half == 0:
                                if g >= 3:
                                    v.wait_ge(s_act, h_silu(g - 3))
                                dst = h1s[:, (g % 3) * 512:(g % 3) * 512 + w]
                                bs = bsREG[(u % 2) * 2]
                            else:
                                if g >= 3:
                                    v.wait_ge(s_pool, h_actw(g - 3))
                                dst = h3s[:, (g % 3) * 512:(g % 3) * 512 + w]
                                bs = bsREG[(u % 2) * 2 + 1]
                            v.tensor_add(dst, L[:, 0:w],
                                         bs[:, 0:w]).then_inc(s_dve, 1)
                            led.inc(s_dve, 1)
                        am = amixf[:, (u % 2) * 512:(u % 2) * 512 + w]
                        for e in range(E):
                            g = g0 + e
                            v.wait_ge(s_pool, h_actw(g))
                            if e == 0:
                                if u >= 2:
                                    v.wait_ge(s_act, h_amxev(u - 2))
                                v.tensor_copy(am, aw[0][:, 0:w]
                                              ).then_inc(s_dve, 1)
                            else:
                                v.tensor_add(am, am, aw[e][:, 0:w]
                                             ).then_inc(s_dve, 1)
                            led.inc(s_dve, 1)
                    ve(_ve)
                    assert led.val(s_dve) == h_amixd(u)

                    def _gp(g2, u=u, boff=boff, w=w, g0=g0, G0=G0):
                        aw = actwP[u % 2]
                        for e in range(E):
                            g = g0 + e
                            g2.wait_ge(s_act, h_silu(g))
                            if e == 0 and u >= 2:
                                g2.wait_ge(s_pe2, h_u2(u - 2))
                                g2.wait_ge(s_dve, h_amixd(u - 2))
                            g2.tensor_mul(
                                tmp_sb[:, (g % 2) * 512:(g % 2) * 512 + w],
                                silu_sb[:, (g % 3) * 512:(g % 3) * 512 + w],
                                wb_sb[:, e * T_ALL + boff:e * T_ALL + boff + w]
                            ).then_inc(s_pool, 1)
                            led.inc(s_pool, 1)
                            g2.wait_ge(s_dve, h_add(G0 + 2 * e + 1))
                            g2.tensor_mul(
                                aw[e][:, 0:w],
                                tmp_sb[:, (g % 2) * 512:(g % 2) * 512 + w],
                                h3s[:, (g % 3) * 512:(g % 3) * 512 + w]
                            ).then_inc(s_pool, 1)
                            led.inc(s_pool, 1)
                    gp(_gp)
                    assert led.val(s_pool) == h_actw(g0 + 7)

                    def _amxd(s2, c=c, boff=boff, w=w, u=u, osem=osem):
                        s2.wait_ge(s_act, h_amxev(u))
                        s2.dma_start(out=amx[c][:, boff:boff + w],
                                     in_=amx_st[:, (u % 2) * 512:
                                                (u % 2) * 512 + w]
                                     ).then_inc(osem, 16)
                        led.inc(osem, 16)
                    sy(_amxd)
                    o_fin[u] = (osem, led.val(osem))
                chunk_pe_done[c] = led.val(s_pe)

            ulast = u

            def _u2fl(t, ulast=ulast):
                emit_u2(t, ulast)
            pe(_u2fl)
            oa_f, ob_f = led.val(s_oa), led.val(s_ob)

            def _odrain(t, oa_f=oa_f, ob_f=ob_f):
                t.wait_ge(s_oa, oa_f)
                t.wait_ge(s_ob, ob_f)
            pe(_odrain)
            barrier()

        def u2fin(blocks):
            prev_tr = [None]
            gd_prev = [None]
            for bi, (boff, w) in enumerate(blocks):
                nsl = w // 128

                def _sc0(v, bi=bi, nsl=nsl, ptr=prev_tr[0]):
                    if ptr is not None:
                        v.wait_ge(s_pe, ptr)
                    v.tensor_scalar(u2s[:, 0:nsl * E * 16],
                                    PU2[bi][:, 0:nsl * E * 16],
                                    2.0, None, ALU.mult).then_inc(s_dve, 1)
                    led.inc(s_dve, 1)
                ve(_sc0)
                dvt = led.val(s_dve)
                for sl in range(nsl):
                    atp = led.val(s_act)

                    def _tr(t, sl=sl, dvt=dvt, gdp=gd_prev[0], atp=atp):
                        t.wait_ge(s_dve, dvt)
                        t.wait_ge(s_act, atp)
                        if gdp is not None:
                            t.wait_ge(s_gd, gdp)
                        t.matmul(pamix[:, 0:128],
                                 u2s[:, sl * 128:(sl + 1) * 128], id_b[:, :],
                                 start=True, stop=True).then_inc(s_pe, 1)
                        led.inc(s_pe, 1)
                    pe(_tr)
                    pt = led.val(s_pe)

                    def _ev(s, pt=pt):
                        s.wait_ge(s_pe, pt)
                        s.activation(s_bf[:, 0:128], pamix[:, 0:128],
                                     AF.Identity).then_inc(s_act, 1)
                        led.inc(s_act, 1)
                    sc(_ev)
                    at = led.val(s_act)

                    def _rp(g, sl=sl, boff=boff, at=at):
                        g.wait_ge(s_act, at)
                        for e in range(E):
                            g.dma_start(
                                out=u2T[:, e * T_ALL + boff + sl * 128:
                                        e * T_ALL + boff + sl * 128 + 128],
                                in_=s_bf[16 * e:16 * e + 16, 0:128]
                            ).then_inc(s_gd, 16)
                            led.inc(s_gd, 16)
                    gp(_rp)
                    gd_prev[0] = led.val(s_gd)
                prev_tr[0] = led.val(s_pe)
            gdf = led.val(s_gd)

            def _gdrain(t, gdf=gdf):
                t.wait_ge(s_gd, gdf)
            pe(_gdrain)
            barrier()

        def dloop(xsrc_ap, dst_ap, dst_shift, blocks):
            for bi, (boff, w) in enumerate(blocks):
                for dg in range(KD // 2):
                    d0 = dg * 2
                    prev_dve = led.val(s_dve)
                    prev_od = led.val(s_od)

                    def _wl(s2, d0=d0, bi=bi, boff=boff, w=w, prev_dve=prev_dve,
                            prev_od=prev_od):
                        s2.wait_ge(s_dve, prev_dve)
                        s2.wait_ge(s_od, prev_od)
                        s2.dma_start(out=w2sl[:, 0:F],
                                     in_=w2rb[d0]).then_inc(s_wd, 16)
                        s2.dma_start(out=w2sl[:, F:2 * F],
                                     in_=w2rb[d0 + 1]).then_inc(s_wd, 16)
                        s2.dma_start(out=b2dg[:, :],
                                     in_=b2pb[:, d0 * E * 128:(d0 + 2) * E * 128]
                                     ).then_inc(s_wd, 16)
                        s2.dma_start(
                            out=xio[:, :].rearrange("p (g t) -> p g t", g=2)
                            [:, :, 0:w],
                            in_=xsrc_ap[d0 * 128:(d0 + 2) * 128, boff:boff + w]
                            .rearrange("(g p) t -> p g t", p=128)
                        ).then_inc(s_wd, 16)
                        led.inc(s_wd, 64)
                    sy(_wl)
                    wdt = led.val(s_wd)
                    mm_groups = []
                    for ci in range(0, CF, 4):
                        gidx = ci // 4

                        asem = s_xa if gidx % 2 == 0 else s_xb

                        def _al(s2, ci=ci, gidx=gidx, bi=bi, boff=boff, w=w,
                                asem=asem):
                            if gidx >= 2:
                                s2.wait_ge(s_pe, mm_groups[gidx - 2])
                            par = gidx % 2
                            s2.dma_start(
                                out=adg[:, par * 2048:par * 2048 + 2048]
                                .rearrange("p (cc t) -> p cc t", cc=4)[:, :, 0:w],
                                in_=amx[ci:ci + 4, :, boff:boff + w]
                                .rearrange("cc p t -> p cc t")
                            ).then_inc(asem, 16)
                            led.inc(asem, 16)
                        sy(_al)
                        xdt = led.val(asem)

                        def _mm(t, ci=ci, gidx=gidx, w=w, xdt=xdt, wdt=wdt,
                                asem=asem, prev_dve=prev_dve):
                            t.wait_ge(asem, xdt)
                            if ci == 0:
                                t.wait_ge(s_wd, wdt)
                                t.wait_ge(s_dve, prev_dve)
                            par = gidx % 2
                            for cj in range(4):
                                cc = ci + cj
                                rhs = adg[:, par * 2048 + cj * 512:
                                          par * 2048 + cj * 512 + w]
                                t.matmul(ph1A[:, 0:w],
                                         w2sl[:, cc * 128:(cc + 1) * 128],
                                         rhs, start=(cc == 0), stop=False)
                                mm2 = t.matmul(
                                    ph3A[:, 0:w],
                                    w2sl[:, F + cc * 128:F + (cc + 1) * 128],
                                    rhs, start=(cc == 0), stop=False)
                            mm2.then_inc(s_pe, 1)
                            mm_groups.append(led.inc(s_pe, 1))
                        pe(_mm)

                    def _lora(t, boff=boff, w=w):
                        for e in range(E):
                            t.matmul(ph1A[:, 0:w],
                                     b2dg[:, e * 128:(e + 1) * 128],
                                     u2T[:, e * T_ALL + boff:e * T_ALL + boff + w],
                                     start=False, stop=(e == E - 1))
                            mm2 = t.matmul(
                                ph3A[:, 0:w],
                                b2dg[:, (E + e) * 128:(E + e + 1) * 128],
                                u2T[:, e * T_ALL + boff:e * T_ALL + boff + w],
                                start=False, stop=(e == E - 1))
                        mm2.then_inc(s_pe, 1)
                        led.inc(s_pe, 1)
                    pe(_lora)
                    pt = led.val(s_pe)

                    def _res(v, w=w, pt=pt, wdt=wdt):
                        v.wait_ge(s_pe, pt)
                        v.wait_ge(s_wd, wdt)
                        v.scalar_tensor_tensor(xout[:, 0:w], ph1A[:, 0:w], 0.5,
                                               xio[:, 0:w], ALU.mult, ALU.add)
                        v.scalar_tensor_tensor(xout[:, 1024:1024 + w],
                                               ph3A[:, 0:w], 0.5,
                                               xio[:, 1024:1024 + w],
                                               ALU.mult, ALU.add).then_inc(s_dve, 1)
                        led.inc(s_dve, 1)
                    ve(_res)
                    dvt = led.val(s_dve)

                    def _st(s2, d0=d0, boff=boff, w=w, dvt=dvt,
                            dst_shift=dst_shift):
                        s2.wait_ge(s_dve, dvt)
                        s2.dma_start(
                            out=dst_ap[d0 * 128:(d0 + 2) * 128,
                                       boff - dst_shift:boff - dst_shift + w]
                            .rearrange("(g p) t -> p g t", p=128),
                            in_=xout[:, :].rearrange("p (g t) -> p g t", g=2)
                            [:, :, 0:w]
                        ).then_inc(s_od, 16)
                        led.inc(s_od, 16)
                    sy(_st)

                    odt = led.val(s_od)

                    def _fr(t, dvt=dvt, odt=odt):
                        t.wait_ge(s_dve, dvt)
                        t.wait_ge(s_od, odt)
                        t.nop().then_inc(s_pe, 1)
                        led.inc(s_pe, 1)
                    pe(_fr)
            barrier()

        def qkv_phase():
            wdts = {}
            pe_done = {}
            ev_hist = {}

            def loadw(j, wait_pe):
                wsem = s_wa if j % 2 == 0 else s_wb

                def _l(s2, j=j, wait_pe=wait_pe, wsem=wsem):
                    if wait_pe is not None:
                        s2.wait_ge(s_pe, wait_pe)
                    s2.dma_start(out=wqs[:, (j % 2) * 2048:(j % 2) * 2048 + 2048],
                                 in_=wqb[j]).then_inc(wsem, 16)
                    led.inc(wsem, 16)
                sy(_l)
                wdts[j] = (wsem, led.val(wsem))

            loadw(0, None)
            for j in range(48):
                if j + 1 < 48:
                    loadw(j + 1, pe_done.get(j - 1))
                for sbi, (boff, w) in enumerate(BLK1):
                    gidx = j * 3 + sbi
                    ps = ph1A if gidx % 2 == 0 else ph1B
                    atprev = ev_hist.get(gidx - 2, 0)

                    def _mm(t, j=j, boff=boff, w=w, ps=ps, atprev=atprev):
                        t.wait_ge(wdts[j][0], wdts[j][1])
                        t.wait_ge(s_act, atprev)
                        for k in range(KD):
                            mm = t.matmul(ps[:, 0:w],
                                          wqs[:, (j % 2) * 2048 + k * 128:
                                              (j % 2) * 2048 + (k + 1) * 128],
                                          xn_all[:, k * T_ALL + boff:
                                                 k * T_ALL + boff + w],
                                          start=(k == 0), stop=(k == KD - 1))
                            if k == KD - 1:
                                mm.then_inc(s_pe, 1)
                        led.inc(s_pe, 1)
                    pe(_mm)
                    pt = led.val(s_pe)

                    odp = led.val(s_od)
                    if j < 16:
                        def _ev(s, j=j, w=w, pt=pt, ps=ps, odp=odp):
                            s.wait_ge(s_pe, pt)
                            s.wait_ge(s_od, odp)
                            s.activation(qkst[:, 0:w], ps[:, 0:w], AF.Identity,
                                         bias=battn_q[:, j:j + 1],
                                         scale=0.125).then_inc(s_act, 1)
                            led.inc(s_act, 1)
                        sc(_ev)
                        at = led.val(s_act)
                        ev_hist[gidx] = at

                        def _qst(s2, j=j, boff=boff, w=w, at=at):
                            s2.wait_ge(s_act, at)
                            s2.dma_start(out=qd[j * 128:(j + 1) * 128,
                                                boff:boff + w],
                                         in_=qkst[:, 0:w]).then_inc(s_od, 16)
                            led.inc(s_od, 16)
                        sy(_qst)
                    elif j < 32:
                        def _ev(s, j=j, w=w, pt=pt, ps=ps, odp=odp):
                            s.wait_ge(s_pe, pt)
                            s.wait_ge(s_od, odp)
                            s.activation(qkst[:, 0:w], ps[:, 0:w], AF.Identity,
                                         bias=battn_s[:, j:j + 1]
                                         ).then_inc(s_act, 1)
                            led.inc(s_act, 1)
                        sc(_ev)
                        at = led.val(s_act)
                        ev_hist[gidx] = at

                        def _kst(s2, j=j, boff=boff, w=w, at=at):
                            s2.wait_ge(s_act, at)
                            ha = 2 * (j - 16)
                            s2.dma_start(out=kdr[0:64, ha * T_ALL + boff:
                                                ha * T_ALL + boff + w],
                                         in_=qkst[0:64, 0:w]).then_inc(s_od, 16)
                            s2.dma_start(out=kdr[0:64, (ha + 1) * T_ALL + boff:
                                                (ha + 1) * T_ALL + boff + w],
                                         in_=qkst[64:128, 0:w]).then_inc(s_od, 16)
                            led.inc(s_od, 32)
                        sy(_kst)
                    else:
                        def _ev(s, j=j, w=w, pt=pt, ps=ps, odp=odp):
                            s.wait_ge(s_pe, pt)
                            s.wait_ge(s_od, odp)
                            s.activation(qkst[:, 0:w], ps[:, 0:w], AF.Identity,
                                         bias=battn_s[:, j:j + 1]
                                         ).then_inc(s_act, 1)
                            led.inc(s_act, 1)
                        sc(_ev)
                        at0 = led.val(s_act)
                        ev_hist[gidx] = at0
                        for ts in range(w // 128):
                            atprev2 = led.val(s_act)

                            def _tr(t, ts=ts, at0=at0, atprev2=atprev2):
                                t.wait_ge(s_act, at0 if ts == 0 else atprev2)
                                t.matmul(pamix[:, 0:128],
                                         qkst[:, ts * 128:(ts + 1) * 128],
                                         id_b[:, :], start=True,
                                         stop=True).then_inc(s_pe, 1)
                                led.inc(s_pe, 1)
                            pe(_tr)
                            pt2 = led.val(s_pe)

                            odp2 = led.val(s_od)

                            def _e2(s, pt2=pt2, odp2=odp2):
                                s.wait_ge(s_pe, pt2)
                                s.wait_ge(s_od, odp2)
                                s.activation(qkst[:, 512:640], pamix[:, 0:128],
                                             AF.Identity).then_inc(s_act, 1)
                                led.inc(s_act, 1)
                            sc(_e2)
                            at2 = led.val(s_act)

                            def _vst(s2, j=j, boff=boff, ts=ts, at2=at2):
                                s2.wait_ge(s_act, at2)
                                s2.dma_start(
                                    out=vdr[boff + ts * 128:boff + ts * 128 + 128,
                                            (j - 32) * 128:(j - 31) * 128],
                                    in_=qkst[:, 512:640]).then_inc(s_od, 16)
                                led.inc(s_od, 16)
                            sy(_vst)
                pe_done[j] = led.val(s_pe)
            barrier()

        def attn_phase():
            def _qa64(g):
                g.dma_start(out=qaug64[:, :], in_=qaug).then_inc(s_gd, 16)
                led.inc(s_gd, 16)
            gp(_qa64)
            qa64_t = led.val(s_gd)
            odt0 = led.val(s_od)

            def _ms(v, odt0=odt0):
                v.wait_ge(s_od, odt0)
                vr = vring.rearrange("p (s h x) -> p s h x", s=4, h=NH)
                v.memset(vr[:, :, :, 64:65], 1.0).then_inc(s_dve, 1)
                led.inc(s_dve, 1)
            ve(_ms)
            ms_t = led.val(s_dve)
            ring_t, q_t = {}, {}
            ST_PS = [ph1A, ph3A, ph1B, ph3B]
            PV_PS = [pamix, pu2a]
            h_st, h_mask, h_ex, h_pv, h_rc, h_sc, h_tb, h_fin = \
                {}, {}, {}, {}, {}, {}, {}, {}

            def load_ring(cl, wait_pe):
                rsem = [s_xa, s_xb, s_xc, s_xd][cl % 4]

                def _l(s2, cl=cl, wait_pe=wait_pe, rsem=rsem):
                    if wait_pe is not None:
                        s2.wait_ge(s_pe, wait_pe)
                    else:
                        s2.wait_ge(s_od, odt0)
                    slot = cl % 4
                    s2.dma_start(
                        out=kring[:, slot * NH * 128:(slot + 1) * NH * 128]
                        .rearrange("p (h t) -> p h t", h=NH),
                        in_=kdr.rearrange("p (h t) -> p h t", h=NH)
                        [:, :, cl * 128:(cl + 1) * 128]).then_inc(rsem, 16)
                    s2.dma_start(
                        out=vring[:, slot * NH * 65:(slot + 1) * NH * 65]
                        .rearrange("p (h x) -> p h x", h=NH)[:, :, 0:64],
                        in_=vdr[cl * 128:(cl + 1) * 128, :]
                        .rearrange("p (h x) -> p h x", h=NH)).then_inc(rsem, 16)
                sy(_l)
                led.inc(rsem, 32)
                ring_t[cl] = (rsem, led.val(rsem))

            def load_q(qt, wait_pe):
                qsem = s_oa if qt % 2 == 0 else s_ob

                def _l(s2, qt=qt, wait_pe=wait_pe, qsem=qsem):
                    if wait_pe is not None:
                        s2.wait_ge(s_pe, wait_pe)
                    else:
                        s2.wait_ge(s_od, odt0)
                    cl = qt + 2
                    s2.dma_start(
                        out=q_sb[:, (qt % 2) * 4096:(qt % 2) * 4096 + 4096]
                        .rearrange("p (h t) -> p h t", h=NH),
                        in_=qd[:, cl * 128:(cl + 1) * 128]
                        .rearrange("(h p) t -> p h t", p=64)).then_inc(qsem, 16)
                sy(_l)
                led.inc(qsem, 16)
                q_t[qt] = (qsem, led.val(qsem))

            for cl in range(2):
                load_ring(cl, None)
            load_q(0, None)
            qt_pe_done = {}
            for qt in range(NQ):
                cl = qt + 2
                load_ring(cl, qt_pe_done.get(qt - 2))
                if qt + 1 < NQ:
                    load_q(qt + 1, qt_pe_done.get(qt - 1))
                for h in range(NH):
                    stps = ST_PS[(qt * NH + h) % 4]
                    pvps = PV_PS[h % 2]
                    qh = q_sb[:, (qt % 2) * 4096 + h * 128:
                              (qt % 2) * 4096 + h * 128 + 128]
                    gq = qt * NH + h

                    def _s2(t, qt=qt, cl=cl, h=h, stps=stps, qh=qh, gq=gq):
                        for cc in range(max(0, cl - 2), cl + 1):
                            t.wait_ge(ring_t[cc][0], ring_t[cc][1])
                        t.wait_ge(q_t[qt][0], q_t[qt][1])
                        if gq == 0:
                            t.wait_ge(s_gd, qa64_t)
                        if gq >= 4:
                            t.wait_ge(s_act, h_ex[gq - 4])
                        for s in range(NSLOT):
                            slot = (cl - 2 + s) % 4
                            kcol = slot * NH * 128 + h * 128
                            t.matmul(stps[:, s * 128:(s + 1) * 128],
                                     kring[0:64, kcol:kcol + 128], qh,
                                     start=(s == 0), stop=False)
                            mm = t.matmul(stps[:, s * 128:(s + 1) * 128],
                                          kring[64:65, kcol:kcol + 128],
                                          qaug64[0:1, qt * 128:(qt + 1) * 128],
                                          start=False, stop=(s == NSLOT - 1))
                        mm.then_inc(s_pe, 1)
                        h_st[gq] = led.inc(s_pe, 1)
                    pe(_s2)

                    def _d2(v, stps=stps, gq=gq):
                        v.wait_ge(s_pe, h_st[gq])
                        v.tensor_add(stps[:, 256:384], stps[:, 256:384],
                                     trit_s[:, :]).then_inc(s_dve, 1)
                        h_mask[gq] = led.inc(s_dve, 1)
                    ve(_d2)

                    def _ex(s, qt=qt, stps=stps, gq=gq):
                        s.wait_ge(s_dve, h_mask[gq])
                        if gq >= 4:
                            s.wait_ge(s_pe, h_pv[gq - 4])
                        for sl in range(NSLOT):
                            ai = s.activation(
                                pt_bf[:, (gq % 4) * 384 + sl * 128:
                                      (gq % 4) * 384 + (sl + 1) * 128],
                                stps[:, sl * 128:(sl + 1) * 128], AF.Exp,
                                bias=stb_s[:, qt * NSLOT + sl:
                                           qt * NSLOT + sl + 1])
                        ai.then_inc(s_act, 1)
                        h_ex[gq] = led.inc(s_act, 1)
                    sc(_ex)

                    def _pv(t, cl=cl, h=h, pvps=pvps, gq=gq):
                        t.wait_ge(s_act, h_ex[gq])
                        if gq == 0:
                            t.wait_ge(s_dve, ms_t)
                        if gq >= 2:
                            t.wait_ge(s_dve, h_sc[gq - 2])
                        for s in range(NSLOT):
                            slot = (cl - 2 + s) % 4
                            mm = t.matmul(pvps[0:128, 0:65],
                                          pt_bf[:, (gq % 4) * 384 + s * 128:
                                                (gq % 4) * 384 + (s + 1) * 128],
                                          vring[:, slot * NH * 65 + h * 65:
                                                slot * NH * 65 + (h + 1) * 65],
                                          start=(s == 0), stop=(s == NSLOT - 1))
                        mm.then_inc(s_pe, 1)
                        h_pv[gq] = led.inc(s_pe, 1)
                    pe(_pv)

                    def _rc(v, pvps=pvps, gq=gq):
                        v.wait_ge(s_pe, h_pv[gq])
                        if gq >= 4:
                            v.wait_ge(s_dve, h_sc[gq - 4])
                        v.reciprocal(grow[:, 33 + (gq % 4):34 + (gq % 4)],
                                     pvps[:, 64:65]).then_inc(s_dve, 1)
                        h_rc[gq] = led.inc(s_dve, 1)
                    ve(_rc)

                    def _sc2(v, pvps=pvps, gq=gq):
                        v.wait_ge(s_dve, h_rc[gq])
                        if gq >= 4:
                            v.wait_ge(s_pe, h_tb[(gq - 4) | 1])
                        v.tensor_single_scalar(
                            tm_b[:, (gq % 4) * 64:(gq % 4) * 64 + 64],
                            pvps[:, 0:64],
                            grow[:, 33 + (gq % 4):34 + (gq % 4)],
                            ALU.mult).then_inc(s_dve, 1)
                        h_sc[gq] = led.inc(s_dve, 1)
                    ve(_sc2)

                    if h % 2 == 1:
                        def _tb(t, gq=gq):
                            t.wait_ge(s_dve, h_sc[gq])
                            if gq >= 2:
                                t.wait_ge(s_act, h_fin[gq - 2])
                            pr = (gq // 2) % 2
                            t.matmul(pu2c[:, 0:128],
                                     tm_b[:, pr * 128:pr * 128 + 128],
                                     id_b[:, :], start=True,
                                     stop=True).then_inc(s_pe, 1)
                            h_tb[gq] = h_tb[gq - 1] = led.inc(s_pe, 1)
                        pe(_tb)

                        def _fin(s, qt=qt, h=h, gq=gq):
                            s.wait_ge(s_pe, h_tb[gq])
                            hc = h // 2
                            s.activation(attn_sb[:, hc * T_OWN + qt * 128:
                                                 hc * T_OWN + qt * 128 + 128],
                                         pu2c[:, 0:128],
                                         AF.Identity).then_inc(s_act, 1)
                            h_fin[gq] = h_fin[gq - 1] = led.inc(s_act, 1)
                        sc(_fin)
                qt_pe_done[qt] = led.val(s_pe)
            barrier()

        def proj_phase():
            wdts = {}
            pe_done = {}
            pev_hist = {}

            def loadw(dj, wait_pe):
                wsem = s_wa if dj % 2 == 0 else s_wb

                def _l(s2, dj=dj, wait_pe=wait_pe, wsem=wsem):
                    if wait_pe is not None:
                        s2.wait_ge(s_pe, wait_pe)
                    s2.dma_start(out=wqs[:, (dj % 2) * 2048:(dj % 2) * 2048 + 2048],
                                 in_=wpb[dj]).then_inc(wsem, 16)
                    led.inc(wsem, 16)
                sy(_l)
                wdts[dj] = (wsem, led.val(wsem))

            loadw(0, None)
            for dj in range(KD):
                if dj + 1 < KD:
                    loadw(dj + 1, pe_done.get(dj - 1))
                for sbi, (boff, w) in enumerate(BLK2):
                    ps = ph1A if (dj * 2 + sbi) % 2 == 0 else ph1B
                    oprev = led.val(s_od)

                    def _xl(s2, dj=dj, boff=boff, w=w, sbi=sbi, oprev=oprev):
                        s2.wait_ge(s_od, oprev)
                        s2.dma_start(out=xpj[:, (sbi % 2) * 512:
                                             (sbi % 2) * 512 + w],
                                     in_=xres[dj * 128:(dj + 1) * 128,
                                              boff:boff + w]).then_inc(s_xd, 16)
                        led.inc(s_xd, 16)
                    sy(_xl)
                    xdt = led.val(s_xd)

                    def _mm(t, dj=dj, boff=boff, w=w, ps=ps,
                            dvp=pev_hist.get(dj * 2 + sbi - 2, 0)):
                        t.wait_ge(wdts[dj][0], wdts[dj][1])
                        t.wait_ge(s_dve, dvp)
                        for k in range(KD):
                            mm = t.matmul(ps[:, 0:w],
                                          wqs[:, (dj % 2) * 2048 + k * 128:
                                              (dj % 2) * 2048 + (k + 1) * 128],
                                          attn_sb[:, k * T_OWN + boff - HALO:
                                                  k * T_OWN + boff - HALO + w],
                                          start=(k == 0), stop=(k == KD - 1))
                            if k == KD - 1:
                                mm.then_inc(s_pe, 1)
                        led.inc(s_pe, 1)
                    pe(_mm)
                    pt = led.val(s_pe)

                    def _ev(v, dj=dj, w=w, sbi=sbi, pt=pt, ps=ps, xdt=xdt):
                        v.wait_ge(s_pe, pt)
                        v.wait_ge(s_xd, xdt)
                        v.scalar_tensor_tensor(
                            xfp[:, 0:w], ps[:, 0:w], bproj_s[:, dj:dj + 1],
                            xpj[:, (sbi % 2) * 512:(sbi % 2) * 512 + w],
                            ALU.add, ALU.add).then_inc(s_dve, 1)
                        led.inc(s_dve, 1)
                    ve(_ev)
                    dvt = led.val(s_dve)
                    pev_hist[dj * 2 + sbi] = dvt

                    def _st(s2, dj=dj, boff=boff, w=w, dvt=dvt):
                        s2.wait_ge(s_dve, dvt)
                        s2.dma_start(out=xres[dj * 128:(dj + 1) * 128,
                                              boff:boff + w],
                                     in_=xfp[:, 0:w]).then_inc(s_od, 16)
                        led.inc(s_od, 16)
                    sy(_st)
                pe_done[dj] = led.val(s_pe)
            odf = led.val(s_od)

            def _oddrain2(t, odf=odf):
                t.wait_ge(s_od, odf)
            pe(_oddrain2)
            barrier()

        ln_phase(x0, BLK1)

        def _castwait(t, gdt=gdt):
            t.wait_ge(s_gd, gdt)
        pe(_castwait)
        barrier()
        gate_phase(list(range(NT)))
        uprod(BLK1)
        floop(BLK1)
        u2fin(BLK1)
        dloop(x0, xres, 0, BLK1)
        ln_phase(xres, BLK1)
        qkv_phase()
        attn_phase()
        proj_phase()
        ln_phase(xres, BLK2)
        gate_phase(list(range(2, NT)))
        uprod(BLK2)
        floop(BLK2)
        u2fin(BLK2)
        dloop(xres, y, HALO, BLK2)

        fin = led.val(s_od)

        def _fin(s2, fin=fin):
            s2.wait_ge(s_od, fin)
        sy(_fin)

        B.tensor(R_pe.replay)
        B.scalar(R_sc.replay)
        B.vector(R_ve.replay)
        B.gpsimd(R_gp.replay)
        B.sync(R_sy.replay)

    return nc


def _prep_weights(i):
    w1, w3, w2 = i["L0_ffn_w1"], i["L0_ffn_w3"], i["L0_ffn_w2"]
    a1, b1 = i["L0_ffn_a1"], i["L0_ffn_b1"]
    a3, b3 = i["L0_ffn_a3"], i["L0_ffn_b3"]
    a2, b2 = i["L0_ffn_a2"], i["L0_ffn_b2"]
    gw = i["L0_ffn_gate_w"]
    W_attn, b_attn = i["L0_attn_W_attn"], i["L0_attn_b_attn"]
    W_proj, b_proj = i["L0_attn_W_proj"], i["L0_attn_b_proj"]

    c = np.ascontiguousarray
    w1r = np.asarray(w1).reshape(CF, 128, KD, 128).transpose(0, 3, 2, 1)
    w3r = np.asarray(w3).reshape(CF, 128, KD, 128).transpose(0, 3, 2, 1)
    w13v = c(np.stack([w1r, w3r], axis=2).reshape(CF, 128, 4096))
    b1r = np.asarray(b1).reshape(E, CF, 128, R).transpose(1, 3, 0, 2)
    b3r = np.asarray(b3).reshape(E, CF, 128, R).transpose(1, 3, 0, 2)
    b13v = c(np.stack([b1r, b3r], axis=2).reshape(CF, 16, 2048))
    a1r = np.asarray(a1).reshape(E, R, KD, 128).transpose(3, 2, 0, 1)
    a3r = np.asarray(a3).reshape(E, R, KD, 128).transpose(3, 2, 0, 1)
    a13v = c(np.stack([a1r, a3r], axis=1).reshape(128, 4096))
    a2v = c(np.asarray(a2).reshape(E, R, CF, 128)
            .transpose(3, 2, 0, 1).reshape(128, CF * 128))
    b2v = c(np.asarray(b2).reshape(E, KD, 128, R)
            .transpose(3, 1, 0, 2).reshape(16, KD * E * 128))
    gwv = c(np.asarray(gw).reshape(E, KD, 128).transpose(2, 1, 0)
            .reshape(128, KD * E))
    wqv = c(np.asarray(W_attn).reshape(KD, 128, 48, 128)
            .transpose(2, 1, 0, 3).reshape(48, 128, 2048))
    wpv = c(np.asarray(W_proj).reshape(KD, 128, KD, 128)
            .transpose(2, 1, 0, 3).reshape(KD, 128, 2048))
    w2v = c(np.asarray(w2).reshape(KD, 128, CF, 128)
            .transpose(0, 3, 2, 1).reshape(KD, 128, F))
    bav = c(np.asarray(b_attn).reshape(48, 128).T)
    bpv = c(np.asarray(b_proj).reshape(KD, 128).T)
    return dict(w13=w13v, b13=b13v, a13=a13v, a2pi=a2v, b2pi=b2v, gwt=gwv,
                wq=wqv, wp=wpv, w2r=w2v, battn=bav, bproj=bpv)


def _prep_meta(core):
    stbv = np.zeros((NQ * NSLOT, 128), np.float32)
    for qt in range(NQ):
        for s in range(NSLOT):
            cl = qt + s
            col = M_AL * (cl * 128 + np.arange(128, dtype=np.float32))
            if core == 0 and cl < 2:
                col = col + NEGM
            stbv[qt * NSLOT + s] = col
    qau = np.zeros((2, NQ * 128), np.float32)
    for qt in range(NQ):
        qau[0, qt * 128:(qt + 1) * 128] = -M_AL * (
            (qt + 2) * 128 + np.arange(128, dtype=np.float32)) - 12.0
    qau[1] = 1.0
    mk = np.zeros((2, T_ALL), np.float32)
    mk[0] = 1.0
    mk[1] = M_AL * np.arange(T_ALL, dtype=np.float32)
    tri_v = np.where(np.arange(128)[None, :] > np.arange(128)[:, None],
                     np.float32(NEGM), np.float32(0.0))
    return dict(stb=np.ascontiguousarray(stbv.T), qaug=qau, mkrow=mk,
                ident=np.eye(128, dtype=np.float32),
                tri=np.ascontiguousarray(tri_v),
                trit=np.ascontiguousarray(tri_v.T),
                onesf=np.ones((1, 128), np.float32))


def kernel(**inputs):
    x = np.asarray(inputs["L0_attn_hidden_states"], np.float32)[0]
    xT = np.ascontiguousarray(x.T)
    wdict = _prep_weights(inputs)
    nc = build_program()
    in_maps = []
    for corei in range(8):
        g0 = corei * T_OWN
        if corei == 0:
            halo = np.zeros((D, HALO), np.float32)
        else:
            halo = xT[:, g0 - HALO:g0]
        x0c = np.ascontiguousarray(
            np.concatenate([halo, xT[:, g0:g0 + T_OWN]], axis=1))
        m = dict(wdict)
        m.update(_prep_meta(corei))
        m["x0"] = x0c
        in_maps.append(m)
    res = run_bass_kernel_spmd(nc, in_maps, core_ids=list(range(8)))
    out = np.empty((8192, D), np.float32)
    for corei in range(8):
        out[corei * T_OWN:(corei + 1) * T_OWN, :] = res.results[corei]["y"].T
    return out.reshape(1, 8192, D)


# version 14
# speedup vs baseline: 1.3617x; 1.0701x over previous
import numpy as np
import concourse.bass as bass
import concourse.mybir as mybir
from concourse.bass_utils import run_bass_kernel_spmd

F32 = mybir.dt.float32
BF16 = mybir.dt.bfloat16
AF = mybir.ActivationFunctionType
ALU = mybir.AluOpType
AX = mybir.AxisListType

T_OWN, HALO = 1024, 256
T_ALL = T_OWN + HALO
D, KD, F, CF, E, R, NH, DH = 2048, 16, 5632, 44, 8, 16, 32, 64
NT, NQ, NSLOT = T_ALL // 128, T_OWN // 128, 3
M_AL = 2.0 ** (-0.25)
BLK1 = [(0, 512), (512, 512), (1024, 256)]
BLK2 = [(256, 512), (768, 512)]
NEGM = -30000.0




class _InstRec:
    __slots__ = ("chain",)

    def __init__(self):
        self.chain = []

    def then_inc(self, sem, val):
        self.chain.append(("then_inc", (sem, val)))
        return self


class _EngRec:
    __slots__ = ("calls",)

    def __init__(self):
        self.calls = []

    def __getattr__(self, name):
        def _call(*a, **k):
            inst = _InstRec()
            self.calls.append((name, a, k, inst))
            return inst
        return _call

    def replay(self, eng):
        for name, a, k, inst in self.calls:
            real = getattr(eng, name)(*a, **k)
            for cn, ca in inst.chain:
                real = getattr(real, cn)(*ca)


class Led:
    def __init__(self):
        self.c = {}

    def inc(self, s, n=1):
        self.c[s.name] = self.c.get(s.name, 0) + n
        return self.c[s.name]

    def val(self, s):
        return self.c.get(s.name, 0)


def build_program():
    nc = bass.Bass("TRN2", target_bir_lowering=False, debug=False)
    led = Led()

    def inp(name, shape):
        return nc.dram_tensor(name, shape, F32, kind="ExternalInput").ap()

    x0 = inp("x0", [D, T_ALL])
    w13 = inp("w13", [CF, 128, 4096])
    b13 = inp("b13", [CF, 16, 2048])
    a13 = inp("a13", [128, 4096])
    a2pi = inp("a2pi", [128, CF * 128])
    b2pi = inp("b2pi", [16, KD * E * 128])
    gwt = inp("gwt", [128, KD * E])
    wq = inp("wq", [48, 128, 2048])
    wp = inp("wp", [KD, 128, 2048])
    w2r = inp("w2r", [KD, 128, F])
    battn = inp("battn", [128, 48])
    bproj = inp("bproj", [128, KD])
    stb = inp("stb", [128, NQ * NSLOT])
    qaug = inp("qaug", [2, NQ * 128])
    mkrow = inp("mkrow", [2, T_ALL])
    ident = inp("ident", [128, 128])
    tri = inp("tri", [128, 128])
    trit = inp("trit", [128, 128])
    onesf = inp("onesf", [1, 128])

    y = nc.dram_tensor("y", [D, T_OWN], F32, kind="ExternalOutput").ap()

    xres = nc.dram_tensor("xres", [D, T_ALL], F32).ap()
    qd = nc.dram_tensor("qd", [D, T_ALL], BF16).ap()
    kdr = nc.dram_tensor("kdr", [66, NH * T_ALL], BF16).ap()
    vdr = nc.dram_tensor("vdr", [T_ALL, NH * DH], BF16).ap()
    amx = nc.dram_tensor("amx", [CF, 128, T_ALL], BF16).ap()

    import contextlib
    with contextlib.ExitStack() as _st:
        xn_all = _st.enter_context(nc.sbuf_tensor("xn_all", [128, KD * T_ALL], BF16))
        REGA = _st.enter_context(nc.sbuf_tensor("REGA", [128, 2 * E * T_ALL], BF16))
        REGB = _st.enter_context(nc.sbuf_tensor("REGB", [128, E * T_ALL], BF16))
        REGC = _st.enter_context(nc.sbuf_tensor("REGC", [128, KD * T_OWN], BF16))
        REGD = _st.enter_context(nc.sbuf_tensor("REGD", [128, 17408], BF16))
        REGF = _st.enter_context(nc.sbuf_tensor("REGF", [128, 4096], F32))
        a13b = _st.enter_context(nc.sbuf_tensor("a13b", [128, 4096], BF16))
        a2pb = _st.enter_context(nc.sbuf_tensor("a2pb", [128, CF * 128], BF16))
        gwb = _st.enter_context(nc.sbuf_tensor("gwb", [128, KD * E], BF16))
        wfull = _st.enter_context(nc.sbuf_tensor("wfull", [128, E * NT], F32))
        battn_s = _st.enter_context(nc.sbuf_tensor("battn_s", [128, 48], F32))
        battn_q = _st.enter_context(nc.sbuf_tensor("battn_q", [128, 16], F32))
        bproj_s = _st.enter_context(nc.sbuf_tensor("bproj_s", [128, KD], F32))
        stb_s = _st.enter_context(nc.sbuf_tensor("stb_s", [128, NQ * NSLOT], F32))
        qaug_b = _st.enter_context(nc.sbuf_tensor("qaug_b", [2, NQ * 128], BF16))
        id_b = _st.enter_context(nc.sbuf_tensor("id_b", [128, 128], BF16))
        tri_s = _st.enter_context(nc.sbuf_tensor("tri_s", [128, 128], F32))
        trit_s = _st.enter_context(nc.sbuf_tensor("trit_s", [128, 128], F32))
        ones_f = _st.enter_context(nc.sbuf_tensor("ones_f", [1, 128], F32))
        onescol_b = _st.enter_context(nc.sbuf_tensor("onescol_b", [128, 1], BF16))
        one_b = _st.enter_context(nc.sbuf_tensor("one_b", [1, 1], BF16))
        grow = _st.enter_context(nc.sbuf_tensor("grow", [128, 40], F32))
        wf_b = _st.enter_context(nc.sbuf_tensor("wf_b", [128, 8], BF16))
        nmx_b = _st.enter_context(nc.sbuf_tensor("nmx_b", [128, 1], BF16))
        nmx2 = _st.enter_context(nc.sbuf_tensor("nmx2", [128, 2], BF16))
        eps_s = _st.enter_context(nc.sbuf_tensor("eps_s", [128, 1], F32))
        onesrow_b = _st.enter_context(nc.sbuf_tensor("onesrow_b", [1, 128], BF16))
        qa2 = _st.enter_context(nc.sbuf_tensor("qa2", [65, 128], BF16))
        tm_b = _st.enter_context(nc.sbuf_tensor("tm_b", [128, 4 * 64], BF16))
        ph1A = _st.enter_context(nc.psum_tensor("ph1A", [128, 512], F32))
        ph3A = _st.enter_context(nc.psum_tensor("ph3A", [128, 512], F32))
        ph1B = _st.enter_context(nc.psum_tensor("ph1B", [128, 512], F32))
        ph3B = _st.enter_context(nc.psum_tensor("ph3B", [128, 512], F32))
        pamix = _st.enter_context(nc.psum_tensor("pamix", [128, 512], F32))
        pu2a = _st.enter_context(nc.psum_tensor("pu2a", [128, 512], F32))
        pu2b = _st.enter_context(nc.psum_tensor("pu2b", [128, 512], F32))
        pu2c = _st.enter_context(nc.psum_tensor("pu2c", [128, 512], F32))
        s_ld = _st.enter_context(nc.semaphore("s_ld"))
        s_gd = _st.enter_context(nc.semaphore("s_gd"))
        s_xa = _st.enter_context(nc.semaphore("s_xa"))
        s_xc = _st.enter_context(nc.semaphore("s_xc"))
        s_xb = _st.enter_context(nc.semaphore("s_xb"))
        s_wa = _st.enter_context(nc.semaphore("s_wa"))
        s_wb = _st.enter_context(nc.semaphore("s_wb"))
        s_oa = _st.enter_context(nc.semaphore("s_oa"))
        s_ob = _st.enter_context(nc.semaphore("s_ob"))
        s_wd = _st.enter_context(nc.semaphore("s_wd"))
        s_xd = _st.enter_context(nc.semaphore("s_xd"))
        s_od = _st.enter_context(nc.semaphore("s_od"))
        s_pe = _st.enter_context(nc.semaphore("s_pe"))
        s_pe2 = _st.enter_context(nc.semaphore("s_pe2"))
        s_act = _st.enter_context(nc.semaphore("s_act"))
        s_dve = _st.enter_context(nc.semaphore("s_dve"))
        s_pool = _st.enter_context(nc.semaphore("s_pool"))
        s_cast1 = _st.enter_context(nc.semaphore("s_cast1"))
        s_cast2 = _st.enter_context(nc.semaphore("s_cast2"))
        s_bar = _st.enter_context(nc.semaphore("s_bar"))
        B = _st.enter_context(nc.Block())
        uN = REGA[64:80, :]
        u2T = REGA[32:48, 0:E * T_ALL]
        kring = REGA[0:66, 0:4 * NH * 128]
        wb_sb = REGB[:, :]
        vring = REGB[:, 0:4 * NH * 65]
        s_bf = REGC[:, 0:1536]
        act_bf = REGC[:, 1536:3072]
        actw_bf = REGC[:, 3072:7168]
        amx_st = REGC[:, 7168:8192]
        u2s = REGC[:, 8192:8704]
        silu_sb = REGC[:, 8704:10240]
        tmp_sb = REGC[:, 10240:11264]
        b13A = REGD[64:80, 8192:10240]
        b13B = REGD[64:80, 10240:12288]
        bsREG = [REGC[:, 12800 + i * 512:12800 + (i + 1) * 512] for i in range(4)]
        qkst = REGC[:, 14848:15872]
        xt16 = REGC[:, 0:2048]
        attn_sb = REGC[:, :]
        h1s = REGF[:, 1024:2560]
        h3s = REGF[:, 2560:4096]
        ws = REGD[:, 0:8192]
        w2sl = REGD[:, 0:11264]
        b2dg = REGD[32:48, 11264:13312]
        adg = REGD[:, 13312:17408]
        wtrow = REGD[0:1, 0:E * NT * 128]
        wqs = REGD[:, 0:4096]
        q_sb = REGD[0:64, 4096:12288]
        pt_bf = REGD[:, 12288:12288 + 4 * NSLOT * 128]
        qaug64 = REGD[64:66, 4096:4096 + NQ * 128]
        qr64 = REGD[64:65, 13056:13056 + 256]
        xln = REGF[:, 0:512]
        xln2 = REGF[:, 512:1024]
        xfp = REGF[:, 1024:1536]
        rows = REGF[0:1, 1536:3584]
        xio = REGF[:, 0:2048]
        xout = REGF[:, 2048:4096]
        xpj = REGF[:, 1536:2560]
        amixf = REGF[:, 0:1024]

        PU2 = [pu2a, pu2b, pu2c]
        R_pe, R_sc, R_ve, R_gp, R_sy = (_EngRec() for _ in range(5))
        pe = lambda f: f(R_pe)
        sc = lambda f: f(R_sc)
        ve = lambda f: f(R_ve)
        gp = lambda f: f(R_gp)
        sy = lambda f: f(R_sy)
        nbar = [0]

        def barrier():
            nbar[0] += 1
            tgt = 5 * nbar[0]
            for em in (pe, sc, ve, gp, sy):
                def _f(e, tgt=tgt):
                    e.drain().then_inc(s_bar, 1)
                    e.wait_ge(s_bar, tgt)
                em(_f)
            led.c[s_bar.name] = tgt

        def _p0g(g):
            for t_in, t_out in ((a13, a13b[:, :]), (a2pi, a2pb[:, :]),
                                (gwt, gwb[:, :]), (ident, id_b[:, :]),
                                (qaug, qaug_b[:, :])):
                g.dma_start(out=t_out, in_=t_in).then_inc(s_gd, 16)
                led.inc(s_gd, 16)
            for h in range(NH):
                g.dma_start(out=kdr[64:66, h * T_ALL:(h + 1) * T_ALL],
                            in_=mkrow).then_inc(s_gd, 16)
                led.inc(s_gd, 16)
        gp(_p0g)

        def _p0s(s):
            for t_in, t_out in ((battn, battn_s[:, :]), (bproj, bproj_s[:, :]),
                                (stb, stb_s[:, :]), (tri, tri_s[:, :]),
                                (trit, trit_s[:, :]), (onesf, ones_f[:, :])):
                s.dma_start(out=t_out, in_=t_in).then_inc(s_ld, 16)
                led.inc(s_ld, 16)
        sy(_p0s)
        ldt = led.val(s_ld)
        gdt = led.val(s_gd)

        def _p0v(v, ldt=ldt, gdt=gdt):
            v.wait_ge(s_ld, ldt)
            v.memset(onescol_b[:, :], 1.0)
            v.memset(one_b[:, :], 1.0)
            v.memset(eps_s[:, :], 1e-5)
            v.memset(onesrow_b[:, :], 1.0)
        ve(_p0v)

        def _p0sc(s, ldt=ldt, gdt=gdt):
            s.wait_ge(s_ld, ldt)
            s.activation(battn_q[:, :], battn_s[:, 0:16], AF.Identity, scale=0.125)
        sc(_p0sc)
        barrier()

        def ln_phase(src_ap, blocks):
            for (boff, w) in blocks:
                dve0 = led.val(s_dve)
                pe0 = led.val(s_pe)
                for k in range(KD):
                    sem = s_xa if k % 2 == 0 else s_xb

                    def _ld(s2, k=k, boff=boff, w=w, dve0=dve0, sem=sem):
                        s2.wait_ge(s_dve, dve0 + k - 1 if k >= 2 else dve0)
                        dst = xln if k % 2 == 0 else xln2
                        s2.dma_start(out=dst[:, 0:w],
                                     in_=src_ap[k * 128:(k + 1) * 128,
                                                boff:boff + w]).then_inc(sem, 16)
                        led.inc(sem, 16)
                    sy(_ld)
                    xdt = led.val(sem)

                    def _cast(v, k=k, w=w, xdt=xdt, sem=sem, pe0=pe0):
                        v.wait_ge(sem, xdt)
                        if k >= 2:
                            v.wait_ge(s_pe, pe0 + k - 1)
                        src = xln if k % 2 == 0 else xln2
                        xb = xt16[:, (k % 2) * 1024:(k % 2) * 1024 + w]
                        xq = xt16[:, (k % 2) * 1024 + 512:(k % 2) * 1024 + 512 + w]
                        v.tensor_copy(xb, src[:, 0:w])
                        v.tensor_mul(xq, src[:, 0:w], src[:, 0:w]).then_inc(s_dve, 1)
                        led.inc(s_dve, 1)
                    ve(_cast)
                    dvt = led.val(s_dve)

                    def _mm(t, k=k, w=w, dvt=dvt):
                        t.wait_ge(s_dve, dvt)
                        xb = xt16[:, (k % 2) * 1024:(k % 2) * 1024 + w]
                        xq = xt16[:, (k % 2) * 1024 + 512:(k % 2) * 1024 + 512 + w]
                        t.matmul(pamix[0:1, 0:w], onescol_b[:, :], xb,
                                 start=(k == 0), stop=(k == KD - 1))
                        t.matmul(ph1B[0:1, 0:w], onescol_b[:, :], xq,
                                 start=(k == 0),
                                 stop=(k == KD - 1)).then_inc(s_pe, 1)
                        led.inc(s_pe, 1)
                    pe(_mm)
                pet = led.val(s_pe)

                def _st1(s, w=w, pet=pet):
                    s.wait_ge(s_pe, pet)
                    s.activation(rows[:, 0:w], pamix[0:1, 0:w], AF.Identity,
                                 scale=1.0 / D)
                    s.activation(rows[:, 512:512 + w], ph1B[0:1, 0:w],
                                 AF.Identity, scale=1.0 / D).then_inc(s_act, 1)
                    led.inc(s_act, 1)
                sc(_st1)
                at = led.val(s_act)

                def _st2(v, w=w, at=at):
                    v.wait_ge(s_act, at)
                    v.tensor_mul(rows[:, 1024:1024 + w], rows[:, 0:w],
                                 rows[:, 0:w]).then_inc(s_dve, 1)
                    v.wait_ge(s_dve, led.inc(s_dve, 1))
                    v.tensor_sub(rows[:, 512:512 + w], rows[:, 512:512 + w],
                                 rows[:, 1024:1024 + w]).then_inc(s_dve, 1)
                    led.inc(s_dve, 1)
                ve(_st2)
                dvt2 = led.val(s_dve)

                def _st3(s, w=w, dvt2=dvt2):
                    s.wait_ge(s_dve, dvt2)
                    s.activation(rows[:, 512:512 + w], rows[:, 512:512 + w],
                                 AF.Ln, bias=eps_s[0:1, 0:1]).then_inc(s_act, 1)
                    s.wait_ge(s_act, led.inc(s_act, 1))
                    s.activation(rows[:, 512:512 + w], rows[:, 512:512 + w],
                                 AF.Exp, scale=-0.5).then_inc(s_act, 1)
                    led.inc(s_act, 1)
                sc(_st3)
                at2 = led.val(s_act)

                def _st4(v, w=w, at2=at2):
                    v.wait_ge(s_act, at2)
                    v.tensor_mul(rows[:, 1536:1536 + w], rows[:, 0:w],
                                 rows[:, 512:512 + w]).then_inc(s_dve, 1)
                    led.inc(s_dve, 1)
                ve(_st4)
                dvt3 = led.val(s_dve)

                def _bc(t, w=w, dvt3=dvt3):
                    t.wait_ge(s_dve, dvt3)
                    t.matmul(ph1A[:, 0:w], ones_f[:, :], rows[:, 512:512 + w],
                             start=True, stop=True)
                    t.matmul(ph3A[:, 0:w], ones_f[:, :], rows[:, 1536:1536 + w],
                             start=True, stop=True).then_inc(s_pe, 1)
                    led.inc(s_pe, 1)
                pe(_bc)
                pet2 = led.val(s_pe)
                dve1 = led.val(s_dve)
                for k in range(KD):
                    sem = s_xa if k % 2 == 0 else s_xb

                    def _ld2(s2, k=k, boff=boff, w=w, dve1=dve1, sem=sem):
                        s2.wait_ge(s_dve, dve1 + 2 * (k - 1) if k >= 2 else dve1)
                        dst = xln if k % 2 == 0 else xln2
                        s2.dma_start(out=dst[:, 0:w],
                                     in_=src_ap[k * 128:(k + 1) * 128,
                                                boff:boff + w]).then_inc(sem, 16)
                        led.inc(sem, 16)
                    sy(_ld2)
                    xdt2 = led.val(sem)

                    dvp = led.val(s_dve)

                    def _nm(v, k=k, boff=boff, w=w, xdt2=xdt2, pet2=pet2, sem=sem,
                            dvp=dvp):
                        v.wait_ge(sem, xdt2)
                        v.wait_ge(s_dve, dvp)
                        if k == 0:
                            v.wait_ge(s_pe, pet2)
                        src = xln if k % 2 == 0 else xln2
                        v.tensor_mul(xfp[:, 0:w], src[:, 0:w],
                                     ph1A[:, 0:w]).then_inc(s_dve, 1)
                        v.wait_ge(s_dve, led.inc(s_dve, 1))
                        v.tensor_sub(xn_all[:, k * T_ALL + boff:
                                            k * T_ALL + boff + w],
                                     xfp[:, 0:w], ph3A[:, 0:w]).then_inc(s_dve, 1)
                        led.inc(s_dve, 1)
                    ve(_nm)
            barrier()

        def gate_phase(tiles):
            for gi in tiles:
                def _gmm(t, gi=gi):
                    for k in range(KD):
                        mm = t.matmul(pamix[:, 0:8],
                                      xn_all[:, k * T_ALL + gi * 128:
                                             k * T_ALL + gi * 128 + 128],
                                      gwb[:, k * 8:(k + 1) * 8],
                                      start=(k == 0), stop=(k == KD - 1))
                    mm.then_inc(s_pe, 1)
                    led.inc(s_pe, 1)
                pe(_gmm)
                pt = led.val(s_pe)

                def _g1(v, pt=pt):
                    v.wait_ge(s_pe, pt)
                    v.tensor_reduce(grow[:, 0:1], pamix[:, 0:8], AX.X,
                                    ALU.max).then_inc(s_dve, 1)
                    v.tensor_reduce(grow[:, 1:2], pamix[:, 0:8], AX.X, ALU.max,
                                    negate=True)
                    v.wait_ge(s_dve, led.inc(s_dve, 1))
                    v.tensor_single_scalar(grow[:, 8:16], pamix[:, 0:8],
                                           grow[:, 0:1], ALU.is_ge).then_inc(s_dve, 1)
                    v.wait_ge(s_dve, led.inc(s_dve, 1))
                    v.scalar_tensor_tensor(grow[:, 16:24], grow[:, 8:16], -1e30,
                                           pamix[:, 0:8], ALU.mult,
                                           ALU.add).then_inc(s_dve, 1)
                    v.wait_ge(s_dve, led.inc(s_dve, 1))
                    v.tensor_reduce(grow[:, 2:3], grow[:, 16:24], AX.X,
                                    ALU.max).then_inc(s_dve, 1)
                    led.inc(s_dve, 1)
                ve(_g1)
                dt = led.val(s_dve)

                def _g2(s, dt=dt):
                    s.wait_ge(s_dve, dt)
                    s.activation(grow[:, 24:32], pamix[:, 0:8], AF.Exp,
                                 bias=grow[:, 1:2])
                    s.activation(grow[:, 3:4], grow[:, 2:3], AF.Exp,
                                 bias=grow[:, 1:2]).then_inc(s_act, 1)
                    led.inc(s_act, 1)
                sc(_g2)
                at = led.val(s_act)

                def _g3(v, gi=gi, at=at):
                    v.wait_ge(s_act, at)
                    v.tensor_single_scalar(grow[:, 4:5], grow[:, 3:4], 1.0,
                                           ALU.add).then_inc(s_dve, 1)
                    v.wait_ge(s_dve, led.inc(s_dve, 1))
                    v.reciprocal(grow[:, 5:6], grow[:, 4:5])
                    v.tensor_single_scalar(grow[:, 16:24], pamix[:, 0:8],
                                           grow[:, 2:3], ALU.is_ge).then_inc(s_dve, 1)
                    v.wait_ge(s_dve, led.inc(s_dve, 1))
                    v.scalar_tensor_tensor(wfull[:, gi * 8:gi * 8 + 8],
                                           grow[:, 24:32], grow[:, 5:6],
                                           grow[:, 16:24], ALU.mult,
                                           ALU.mult).then_inc(s_dve, 1)
                    v.wait_ge(s_dve, led.inc(s_dve, 1))
                    v.tensor_copy(wf_b[:, :],
                                  wfull[:, gi * 8:gi * 8 + 8]).then_inc(s_dve, 1)
                    led.inc(s_dve, 1)
                ve(_g3)
                dt2 = led.val(s_dve)

                for e in range(E):
                    def _g4b(t, e=e, dt2=dt2):
                        t.wait_ge(s_dve, dt2)
                        if e > 0:
                            t.wait_ge(s_act, led.val(s_act))
                        t.matmul(ph1B[0:1, 0:128], wf_b[:, e:e + 1], id_b[:, :],
                                 start=True, stop=True).then_inc(s_pe, 1)
                        led.inc(s_pe, 1)
                    pe(_g4b)
                    pt2 = led.val(s_pe)

                    def _g5(s, gi=gi, e=e, pt2=pt2):
                        s.wait_ge(s_pe, pt2)
                        s.activation(wtrow[:, (e * NT + gi) * 128:
                                           (e * NT + gi) * 128 + 128],
                                     ph1B[0:1, 0:128],
                                     AF.Identity).then_inc(s_act, 1)
                        led.inc(s_act, 1)
                    sc(_g5)
            att = led.val(s_act)
            for e in range(E):
                for gi in tiles:
                    def _g6(t, e=e, gi=gi, att=att):
                        t.wait_ge(s_act, att)
                        if not (e == 0 and gi == tiles[0]):
                            t.wait_ge(s_act, led.val(s_act))
                        t.matmul(ph3B[:, 0:128], onesrow_b[:, :],
                                 wtrow[:, (e * NT + gi) * 128:
                                       (e * NT + gi) * 128 + 128],
                                 start=True, stop=True).then_inc(s_pe, 1)
                        led.inc(s_pe, 1)
                    pe(_g6)
                    pt3 = led.val(s_pe)

                    def _g7(s, e=e, gi=gi, pt3=pt3):
                        s.wait_ge(s_pe, pt3)
                        s.activation(wb_sb[:, e * T_ALL + gi * 128:
                                           e * T_ALL + gi * 128 + 128],
                                     ph3B[:, 0:128], AF.Identity).then_inc(s_act, 1)
                        led.inc(s_act, 1)
                    sc(_g7)
            barrier()

        def uprod(blocks):
            prev_at = [led.val(s_act)]
            prev_xd = [led.val(s_gd)]
            for p in range(2):
                for (boff, w) in blocks:
                    def _u1(t, p=p, boff=boff, w=w, pa=prev_at[0]):
                        t.wait_ge(s_act, pa)
                        for k in range(KD):
                            mm = t.matmul(ph1A[:, 0:w],
                                          a13b[:, (p * KD + k) * 128:
                                               (p * KD + k + 1) * 128],
                                          xn_all[:, k * T_ALL + boff:
                                                 k * T_ALL + boff + w],
                                          start=(k == 0), stop=(k == KD - 1))
                            if k == KD - 1:
                                mm.then_inc(s_pe, 1)
                        led.inc(s_pe, 1)
                    pe(_u1)
                    pt = led.val(s_pe)

                    def _u2(s, w=w, pt=pt, px=prev_xd[0]):
                        s.wait_ge(s_pe, pt)
                        s.wait_ge(s_gd, px)
                        s.activation(u2s[:, 0:w], ph1A[:, 0:w], AF.Identity,
                                     scale=2.0).then_inc(s_act, 1)
                        led.inc(s_act, 1)
                    sc(_u2)
                    at = led.val(s_act)
                    prev_at[0] = at

                    def _u3(g, p=p, boff=boff, w=w, at=at):
                        g.wait_ge(s_act, at)
                        for e in range(E):
                            col = (p * E + e) * T_ALL + boff
                            g.dma_start(out=uN[:, col:col + w],
                                        in_=u2s[16 * e:16 * e + 16, 0:w]
                                        ).then_inc(s_gd, 16)
                            led.inc(s_gd, 16)
                    gp(_u3)
                    prev_xd[0] = led.val(s_gd)
            gfin = led.val(s_gd)

            def _udrain(t, gfin=gfin):
                t.wait_ge(s_gd, gfin)
            pe(_udrain)
            barrier()

        def floop(blocks):
            H_LORA = [ph1B, ph3B, pamix]
            w13t, chunk_pe_done, o_fin = {}, {}, {}
            P0_ = led.val(s_pe)
            A0_ = led.val(s_act)
            D0_ = led.val(s_dve)
            PL_ = led.val(s_pool)
            P2_ = led.val(s_pe2)
            actwP = [[actw_bf[:, e * 512:(e + 1) * 512] for e in range(E)],
                     [REGC[:, e * 512:(e + 1) * 512] for e in range(6)]
                     + [REGC[:, 11264 + e * 512:11264 + (e + 1) * 512]
                        for e in range(2)]]

            def h_lora(G):
                return P0_ + (G // 16) * 18 + 2 + (G % 16) + 1

            def h_pe_done(u):
                return P0_ + (u + 1) * 18

            def h_bsev(u):
                return A0_ + u * 11 + 2

            def h_silu(g):
                return A0_ + (g // 8) * 11 + 2 + (g % 8) + 1

            def h_amxev(u):
                return A0_ + (u + 1) * 11

            def h_add(G):
                return D0_ + (G // 16) * 24 + (G % 16) + 1

            def h_amixd(u):
                return D0_ + (u + 1) * 24

            def h_tmp1(g):
                return PL_ + (g // 8) * 16 + 2 * (g % 8) + 1

            def h_actw(g):
                return PL_ + (g // 8) * 16 + 2 * (g % 8) + 2

            def h_u2(u):
                return P2_ + u + 1

            unit_meta = []
            NB = len(blocks)

            def emit_u2(t, up):
                cu, biu, boffu, wu = unit_meta[up]
                t.wait_ge(s_pool, h_actw(up * 8 + E - 1))
                aw = actwP[up % 2]
                mm = None
                for ep in range(E):
                    for sl in range(wu // 128):
                        mm = t.matmul(
                            PU2[biu][:, (sl * E + ep) * 16:
                                     (sl * E + ep) * 16 + 16],
                            aw[ep][:, sl * 128:(sl + 1) * 128],
                            a2pb[:, (cu * E + ep) * 16:(cu * E + ep) * 16 + 16],
                            start=(cu == 0 and ep == 0 and sl == 0),
                            stop=(cu == CF - 1 and ep == E - 1
                                  and sl == wu // 128 - 1))
                mm.then_inc(s_pe2, 1)
                led.inc(s_pe2, 1)

            def load_chunk(c, wait_pe):
                wsem = s_wa if c % 2 == 0 else s_wb
                led.inc(wsem, 32)

                def _w(g, c=c, wait_pe=wait_pe, wsem=wsem):
                    if wait_pe is not None:
                        g.wait_ge(s_pe, wait_pe)
                    par = c % 2
                    g.dma_start(out=ws[:, par * 4096:(par + 1) * 4096],
                                 in_=w13[c]).then_inc(wsem, 16)
                    g.dma_start(out=(b13A if par == 0 else b13B),
                                 in_=b13[c]).then_inc(wsem, 16)
                gp(_w)
                w13t[c] = (wsem, led.val(wsem))

            load_chunk(0, None)
            u = -1
            for c in range(CF):
                if c + 1 < CF:
                    load_chunk(c + 1, chunk_pe_done.get(c - 1))
                par = c % 2
                wtile = ws[:, par * 4096:(par + 1) * 4096]
                bt = b13A if par == 0 else b13B
                for bi, (boff, w) in enumerate(blocks):
                    u += 1
                    g0 = u * E
                    G0 = u * 16
                    osem = s_oa if u % 2 == 0 else s_ob
                    unit_meta.append((c, bi, boff, w))

                    def _pe(t, c=c, bi=bi, boff=boff, w=w, u=u,
                            wtile=wtile, bt=bt, g0=g0, G0=G0):
                        t.wait_ge(w13t[c][0], w13t[c][1])
                        if u >= 1:
                            t.wait_ge(s_act, h_bsev(u - 1))
                        for k in range(KD):
                            mm = t.matmul(ph1A[:, 0:w],
                                          wtile[:, k * 128:(k + 1) * 128],
                                          xn_all[:, k * T_ALL + boff:
                                                 k * T_ALL + boff + w],
                                          start=(k == 0), stop=(k == KD - 1))
                        mm.then_inc(s_pe, 1)
                        led.inc(s_pe, 1)
                        for k in range(KD):
                            mm = t.matmul(ph3A[:, 0:w],
                                          wtile[:, (KD + k) * 128:
                                                (KD + k + 1) * 128],
                                          xn_all[:, k * T_ALL + boff:
                                                 k * T_ALL + boff + w],
                                          start=(k == 0), stop=(k == KD - 1))
                        mm.then_inc(s_pe, 1)
                        led.inc(s_pe, 1)
                        for j in range(16):
                            e, half = j // 2, j % 2
                            L = H_LORA[(G0 + j) % 3]
                            if G0 + j >= 3:
                                t.wait_ge(s_dve, h_add(G0 + j - 3))
                            col = e * 128 if half == 0 else 1024 + e * 128
                            roff = (e if half == 0 else E + e) * T_ALL + boff
                            t.matmul(L[:, 0:w], bt[:, col:col + 128],
                                     uN[:, roff:roff + w],
                                     start=True, stop=True).then_inc(s_pe, 1)
                            led.inc(s_pe, 1)
                        if u >= 1:
                            emit_u2(t, u - 1)
                    pe(_pe)
                    assert led.val(s_pe) == h_pe_done(u)

                    def _sc(s, u=u, w=w, g0=g0, G0=G0):
                        s.wait_ge(s_pe, P0_ + u * 18 + 2)
                        if u >= 2:
                            s.wait_ge(s_dve, h_add(G0 - 17))
                        s.activation(bsREG[(u % 2) * 2][:, 0:w], ph1A[:, 0:w],
                                     AF.Identity).then_inc(s_act, 1)
                        led.inc(s_act, 1)
                        s.activation(bsREG[(u % 2) * 2 + 1][:, 0:w],
                                     ph3A[:, 0:w],
                                     AF.Identity).then_inc(s_act, 1)
                        led.inc(s_act, 1)
                        for e in range(E):
                            g = g0 + e
                            s.wait_ge(s_dve, h_add(G0 + 2 * e))
                            if g >= 3:
                                s.wait_ge(s_pool, h_tmp1(g - 3))
                            s.activation(silu_sb[:, (g % 3) * 512:
                                                 (g % 3) * 512 + w],
                                         h1s[:, (g % 3) * 512:(g % 3) * 512 + w],
                                         AF.Silu).then_inc(s_act, 1)
                            led.inc(s_act, 1)
                        s.wait_ge(s_dve, h_amixd(u))
                        if u >= 2:
                            s.wait_ge(o_fin[u - 2][0], o_fin[u - 2][1])
                        s.activation(amx_st[:, (u % 2) * 512:(u % 2) * 512 + w],
                                     amixf[:, (u % 2) * 512:(u % 2) * 512 + w],
                                     AF.Identity).then_inc(s_act, 1)
                        led.inc(s_act, 1)
                    sc(_sc)
                    assert led.val(s_act) == h_amxev(u)

                    def _ve(v, u=u, w=w, g0=g0, G0=G0):
                        aw = actwP[u % 2]
                        for j in range(16):
                            e, half = j // 2, j % 2
                            g = g0 + e
                            G = G0 + j
                            L = H_LORA[G % 3]
                            v.wait_ge(s_pe, h_lora(G))
                            if j == 0:
                                v.wait_ge(s_act, h_bsev(u))
                            if half == 0:
                                if g >= 3:
                                    v.wait_ge(s_act, h_silu(g - 3))
                                dst = h1s[:, (g % 3) * 512:(g % 3) * 512 + w]
                                bs = bsREG[(u % 2) * 2]
                            else:
                                if g >= 3:
                                    v.wait_ge(s_pool, h_actw(g - 3))
                                dst = h3s[:, (g % 3) * 512:(g % 3) * 512 + w]
                                bs = bsREG[(u % 2) * 2 + 1]
                            v.tensor_add(dst, L[:, 0:w],
                                         bs[:, 0:w]).then_inc(s_dve, 1)
                            led.inc(s_dve, 1)
                        am = amixf[:, (u % 2) * 512:(u % 2) * 512 + w]
                        for e in range(E):
                            g = g0 + e
                            v.wait_ge(s_pool, h_actw(g))
                            if e == 0:
                                if u >= 2:
                                    v.wait_ge(s_act, h_amxev(u - 2))
                                v.tensor_copy(am, aw[0][:, 0:w]
                                              ).then_inc(s_dve, 1)
                            else:
                                v.tensor_add(am, am, aw[e][:, 0:w]
                                             ).then_inc(s_dve, 1)
                            led.inc(s_dve, 1)
                    ve(_ve)
                    assert led.val(s_dve) == h_amixd(u)

                    def _gp(g2, u=u, boff=boff, w=w, g0=g0, G0=G0):
                        aw = actwP[u % 2]
                        for e in range(E):
                            g = g0 + e
                            g2.wait_ge(s_act, h_silu(g))
                            if e == 0 and u >= 2:
                                g2.wait_ge(s_pe2, h_u2(u - 2))
                                g2.wait_ge(s_dve, h_amixd(u - 2))
                            g2.tensor_mul(
                                tmp_sb[:, (g % 2) * 512:(g % 2) * 512 + w],
                                silu_sb[:, (g % 3) * 512:(g % 3) * 512 + w],
                                wb_sb[:, e * T_ALL + boff:e * T_ALL + boff + w]
                            ).then_inc(s_pool, 1)
                            led.inc(s_pool, 1)
                            g2.wait_ge(s_dve, h_add(G0 + 2 * e + 1))
                            g2.tensor_mul(
                                aw[e][:, 0:w],
                                tmp_sb[:, (g % 2) * 512:(g % 2) * 512 + w],
                                h3s[:, (g % 3) * 512:(g % 3) * 512 + w]
                            ).then_inc(s_pool, 1)
                            led.inc(s_pool, 1)
                    gp(_gp)
                    assert led.val(s_pool) == h_actw(g0 + 7)

                    def _amxd(s2, c=c, boff=boff, w=w, u=u, osem=osem):
                        s2.wait_ge(s_act, h_amxev(u))
                        s2.dma_start(out=amx[c][:, boff:boff + w],
                                     in_=amx_st[:, (u % 2) * 512:
                                                (u % 2) * 512 + w]
                                     ).then_inc(osem, 16)
                        led.inc(osem, 16)
                    sy(_amxd)
                    o_fin[u] = (osem, led.val(osem))
                chunk_pe_done[c] = led.val(s_pe)

            ulast = u

            def _u2fl(t, ulast=ulast):
                emit_u2(t, ulast)
            pe(_u2fl)
            oa_f, ob_f = led.val(s_oa), led.val(s_ob)

            def _odrain(t, oa_f=oa_f, ob_f=ob_f):
                t.wait_ge(s_oa, oa_f)
                t.wait_ge(s_ob, ob_f)
            pe(_odrain)
            barrier()

        def u2fin(blocks):
            prev_tr = [None]
            gd_prev = [None]
            for bi, (boff, w) in enumerate(blocks):
                nsl = w // 128

                def _sc0(v, bi=bi, nsl=nsl, ptr=prev_tr[0]):
                    if ptr is not None:
                        v.wait_ge(s_pe, ptr)
                    v.tensor_scalar(u2s[:, 0:nsl * E * 16],
                                    PU2[bi][:, 0:nsl * E * 16],
                                    2.0, None, ALU.mult).then_inc(s_dve, 1)
                    led.inc(s_dve, 1)
                ve(_sc0)
                dvt = led.val(s_dve)
                for sl in range(nsl):
                    atp = led.val(s_act)

                    def _tr(t, sl=sl, dvt=dvt, gdp=gd_prev[0], atp=atp):
                        t.wait_ge(s_dve, dvt)
                        t.wait_ge(s_act, atp)
                        if gdp is not None:
                            t.wait_ge(s_gd, gdp)
                        t.matmul(pamix[:, 0:128],
                                 u2s[:, sl * 128:(sl + 1) * 128], id_b[:, :],
                                 start=True, stop=True).then_inc(s_pe, 1)
                        led.inc(s_pe, 1)
                    pe(_tr)
                    pt = led.val(s_pe)

                    def _ev(s, pt=pt):
                        s.wait_ge(s_pe, pt)
                        s.activation(s_bf[:, 0:128], pamix[:, 0:128],
                                     AF.Identity).then_inc(s_act, 1)
                        led.inc(s_act, 1)
                    sc(_ev)
                    at = led.val(s_act)

                    def _rp(g, sl=sl, boff=boff, at=at):
                        g.wait_ge(s_act, at)
                        for e in range(E):
                            g.dma_start(
                                out=u2T[:, e * T_ALL + boff + sl * 128:
                                        e * T_ALL + boff + sl * 128 + 128],
                                in_=s_bf[16 * e:16 * e + 16, 0:128]
                            ).then_inc(s_gd, 16)
                            led.inc(s_gd, 16)
                    gp(_rp)
                    gd_prev[0] = led.val(s_gd)
                prev_tr[0] = led.val(s_pe)
            gdf = led.val(s_gd)

            def _gdrain(t, gdf=gdf):
                t.wait_ge(s_gd, gdf)
            pe(_gdrain)
            barrier()

        def dloop(xsrc_ap, dst_ap, dst_shift, blocks):
            W2SL = [REGD[:, 0:11264], REGC[:, 0:11264]]
            B2DG = [REGD[32:48, 11264:13312], REGC[32:48, 11264:13312]]
            ADG = [REGD[:, 13312:15360], REGD[:, 15360:17408],
                   REGA[:, 10240:12288], REGA[:, 12288:14336]]
            XIO = [REGF[:, 0:1024], REGF[:, 1024:2048]]
            XOUT = [REGF[:, 2048:3072], REGF[:, 3072:4096]]
            PS1 = [ph1A, ph1B]
            PS3 = [ph3A, ph3B]
            ASEM = [s_xa, s_xb, s_xc, s_xd]
            NGR = CF // 4
            h_w2t, h_xt, h_mmlast, h_res, h_st = {}, {}, {}, {}, {}
            mm_all = []
            it = -1
            for bi, (boff, w) in enumerate(blocks):
                for dg in range(KD // 2):
                    it += 1
                    d0 = dg * 2
                    par = it % 2

                    def _wlg(g, d0=d0, par=par, it=it):
                        if it >= 2:
                            g.wait_ge(s_pe, h_mmlast[it - 2])
                        g.dma_start(out=W2SL[par][:, 0:F],
                                    in_=w2r[d0]).then_inc(s_wd, 16)
                        g.dma_start(out=W2SL[par][:, F:2 * F],
                                    in_=w2r[d0 + 1]).then_inc(s_wd, 16)
                        g.dma_start(out=B2DG[par],
                                    in_=b2pi[:, d0 * E * 128:(d0 + 2) * E * 128]
                                    ).then_inc(s_wd, 16)
                        led.inc(s_wd, 48)
                    gp(_wlg)
                    h_w2t[it] = led.val(s_wd)

                    def _wl(s2, d0=d0, boff=boff, w=w, par=par, it=it):
                        if it >= 2:
                            s2.wait_ge(s_dve, h_res[it - 2])
                        s2.dma_start(
                            out=XIO[par].rearrange("p (g t) -> p g t", g=2)
                            [:, :, 0:w],
                            in_=xsrc_ap[d0 * 128:(d0 + 2) * 128, boff:boff + w]
                            .rearrange("(g p) t -> p g t", p=128)
                        ).then_inc(s_wd, 16)
                        led.inc(s_wd, 16)
                    sy(_wl)
                    h_xt[it] = led.val(s_wd)

                    for gidx in range(NGR):
                        ci = gidx * 4
                        gg = it * NGR + gidx
                        slot = gg % 4
                        asem = ASEM[slot]

                        def _al(s2, ci=ci, boff=boff, w=w, asem=asem,
                                slot=slot, gg=gg):
                            if gg >= 4:
                                s2.wait_ge(s_pe, mm_all[gg - 4])
                            s2.dma_start(
                                out=ADG[slot]
                                .rearrange("p (cc t) -> p cc t", cc=4)
                                [:, :, 0:w],
                                in_=amx[ci:ci + 4, :, boff:boff + w]
                                .rearrange("cc p t -> p cc t")
                            ).then_inc(asem, 16)
                            led.inc(asem, 16)
                        sy(_al)
                        xdt = led.val(asem)

                        def _mm(t, ci=ci, gidx=gidx, w=w, xdt=xdt, asem=asem,
                                par=par, it=it, slot=slot):
                            t.wait_ge(asem, xdt)
                            if gidx == 0:
                                t.wait_ge(s_wd, h_w2t[it])
                                if it >= 2:
                                    t.wait_ge(s_dve, h_res[it - 2])
                            for cj in range(4):
                                cc = ci + cj
                                rhs = ADG[slot][:, cj * 512:cj * 512 + w]
                                t.matmul(PS1[par][:, 0:w],
                                         W2SL[par][:, cc * 128:(cc + 1) * 128],
                                         rhs, start=(cc == 0), stop=False)
                                mm2 = t.matmul(
                                    PS3[par][:, 0:w],
                                    W2SL[par][:, F + cc * 128:F + (cc + 1) * 128],
                                    rhs, start=(cc == 0), stop=False)
                            mm2.then_inc(s_pe, 1)
                            mm_all.append(led.inc(s_pe, 1))
                        pe(_mm)

                    def _lora(t, boff=boff, w=w, par=par):
                        for e in range(E):
                            t.matmul(PS1[par][:, 0:w],
                                     B2DG[par][:, e * 128:(e + 1) * 128],
                                     u2T[:, e * T_ALL + boff:e * T_ALL + boff + w],
                                     start=False, stop=(e == E - 1))
                            mm2 = t.matmul(
                                PS3[par][:, 0:w],
                                B2DG[par][:, (E + e) * 128:(E + e + 1) * 128],
                                u2T[:, e * T_ALL + boff:e * T_ALL + boff + w],
                                start=False, stop=(e == E - 1))
                        mm2.then_inc(s_pe, 1)
                        led.inc(s_pe, 1)
                    pe(_lora)
                    h_mmlast[it] = led.val(s_pe)

                    def _res(v, w=w, par=par, it=it):
                        v.wait_ge(s_pe, h_mmlast[it])
                        v.wait_ge(s_wd, h_xt[it])
                        if it >= 2:
                            v.wait_ge(s_od, h_st[it - 2])
                        v.scalar_tensor_tensor(XOUT[par][:, 0:w],
                                               PS1[par][:, 0:w], 0.5,
                                               XIO[par][:, 0:w],
                                               ALU.mult, ALU.add)
                        v.scalar_tensor_tensor(XOUT[par][:, 512:512 + w],
                                               PS3[par][:, 0:w], 0.5,
                                               XIO[par][:, 512:512 + w],
                                               ALU.mult, ALU.add
                                               ).then_inc(s_dve, 1)
                        led.inc(s_dve, 1)
                    ve(_res)
                    h_res[it] = led.val(s_dve)

                    def _st(s2, d0=d0, boff=boff, w=w, par=par, it=it,
                            dst_shift=dst_shift):
                        s2.wait_ge(s_dve, h_res[it])
                        s2.dma_start(
                            out=dst_ap[d0 * 128:(d0 + 2) * 128,
                                       boff - dst_shift:boff - dst_shift + w]
                            .rearrange("(g p) t -> p g t", p=128),
                            in_=XOUT[par].rearrange("p (g t) -> p g t", g=2)
                            [:, :, 0:w]
                        ).then_inc(s_od, 16)
                        led.inc(s_od, 16)
                    sy(_st)
                    h_st[it] = led.val(s_od)
            barrier()

        def qkv_phase():
            wdts = {}
            pe_done = {}
            ev_hist = {}

            def loadw(j, wait_pe):
                wsem = s_wa if j % 2 == 0 else s_wb

                def _l(g, j=j, wait_pe=wait_pe, wsem=wsem):
                    if wait_pe is not None:
                        g.wait_ge(s_pe, wait_pe)
                    g.dma_start(out=wqs[:, (j % 2) * 2048:(j % 2) * 2048 + 2048],
                                 in_=wq[j]).then_inc(wsem, 16)
                    led.inc(wsem, 16)
                gp(_l)
                wdts[j] = (wsem, led.val(wsem))

            loadw(0, None)
            for j in range(48):
                if j + 1 < 48:
                    loadw(j + 1, pe_done.get(j - 1))
                for sbi, (boff, w) in enumerate(BLK1):
                    gidx = j * 3 + sbi
                    ps = ph1A if gidx % 2 == 0 else ph1B
                    atprev = ev_hist.get(gidx - 2, 0)

                    def _mm(t, j=j, boff=boff, w=w, ps=ps, atprev=atprev):
                        t.wait_ge(wdts[j][0], wdts[j][1])
                        t.wait_ge(s_act, atprev)
                        for k in range(KD):
                            mm = t.matmul(ps[:, 0:w],
                                          wqs[:, (j % 2) * 2048 + k * 128:
                                              (j % 2) * 2048 + (k + 1) * 128],
                                          xn_all[:, k * T_ALL + boff:
                                                 k * T_ALL + boff + w],
                                          start=(k == 0), stop=(k == KD - 1))
                            if k == KD - 1:
                                mm.then_inc(s_pe, 1)
                        led.inc(s_pe, 1)
                    pe(_mm)
                    pt = led.val(s_pe)

                    odp = led.val(s_od)
                    if j < 16:
                        def _ev(s, j=j, w=w, pt=pt, ps=ps, odp=odp):
                            s.wait_ge(s_pe, pt)
                            s.wait_ge(s_od, odp)
                            s.activation(qkst[:, 0:w], ps[:, 0:w], AF.Identity,
                                         bias=battn_q[:, j:j + 1],
                                         scale=0.125).then_inc(s_act, 1)
                            led.inc(s_act, 1)
                        sc(_ev)
                        at = led.val(s_act)
                        ev_hist[gidx] = at

                        def _qst(s2, j=j, boff=boff, w=w, at=at):
                            s2.wait_ge(s_act, at)
                            s2.dma_start(out=qd[j * 128:(j + 1) * 128,
                                                boff:boff + w],
                                         in_=qkst[:, 0:w]).then_inc(s_od, 16)
                            led.inc(s_od, 16)
                        sy(_qst)
                    elif j < 32:
                        def _ev(s, j=j, w=w, pt=pt, ps=ps, odp=odp):
                            s.wait_ge(s_pe, pt)
                            s.wait_ge(s_od, odp)
                            s.activation(qkst[:, 0:w], ps[:, 0:w], AF.Identity,
                                         bias=battn_s[:, j:j + 1]
                                         ).then_inc(s_act, 1)
                            led.inc(s_act, 1)
                        sc(_ev)
                        at = led.val(s_act)
                        ev_hist[gidx] = at

                        def _kst(s2, j=j, boff=boff, w=w, at=at):
                            s2.wait_ge(s_act, at)
                            ha = 2 * (j - 16)
                            s2.dma_start(out=kdr[0:64, ha * T_ALL + boff:
                                                ha * T_ALL + boff + w],
                                         in_=qkst[0:64, 0:w]).then_inc(s_od, 16)
                            s2.dma_start(out=kdr[0:64, (ha + 1) * T_ALL + boff:
                                                (ha + 1) * T_ALL + boff + w],
                                         in_=qkst[64:128, 0:w]).then_inc(s_od, 16)
                            led.inc(s_od, 32)
                        sy(_kst)
                    else:
                        def _ev(s, j=j, w=w, pt=pt, ps=ps, odp=odp):
                            s.wait_ge(s_pe, pt)
                            s.wait_ge(s_od, odp)
                            s.activation(qkst[:, 0:w], ps[:, 0:w], AF.Identity,
                                         bias=battn_s[:, j:j + 1]
                                         ).then_inc(s_act, 1)
                            led.inc(s_act, 1)
                        sc(_ev)
                        at0 = led.val(s_act)
                        ev_hist[gidx] = at0
                        for ts in range(w // 128):
                            atprev2 = led.val(s_act)

                            def _tr(t, ts=ts, at0=at0, atprev2=atprev2):
                                t.wait_ge(s_act, at0 if ts == 0 else atprev2)
                                t.matmul(pamix[:, 0:128],
                                         qkst[:, ts * 128:(ts + 1) * 128],
                                         id_b[:, :], start=True,
                                         stop=True).then_inc(s_pe, 1)
                                led.inc(s_pe, 1)
                            pe(_tr)
                            pt2 = led.val(s_pe)

                            odp2 = led.val(s_od)

                            def _e2(s, pt2=pt2, odp2=odp2):
                                s.wait_ge(s_pe, pt2)
                                s.wait_ge(s_od, odp2)
                                s.activation(qkst[:, 512:640], pamix[:, 0:128],
                                             AF.Identity).then_inc(s_act, 1)
                                led.inc(s_act, 1)
                            sc(_e2)
                            at2 = led.val(s_act)

                            def _vst(s2, j=j, boff=boff, ts=ts, at2=at2):
                                s2.wait_ge(s_act, at2)
                                s2.dma_start(
                                    out=vdr[boff + ts * 128:boff + ts * 128 + 128,
                                            (j - 32) * 128:(j - 31) * 128],
                                    in_=qkst[:, 512:640]).then_inc(s_od, 16)
                                led.inc(s_od, 16)
                            sy(_vst)
                pe_done[j] = led.val(s_pe)
            barrier()

        def attn_phase():
            def _qa64(g):
                g.dma_start(out=qaug64[:, :], in_=qaug).then_inc(s_gd, 16)
                led.inc(s_gd, 16)
            gp(_qa64)
            qa64_t = led.val(s_gd)
            odt0 = led.val(s_od)

            def _ms(v, odt0=odt0):
                v.wait_ge(s_od, odt0)
                vr = vring.rearrange("p (s h x) -> p s h x", s=4, h=NH)
                v.memset(vr[:, :, :, 64:65], 1.0).then_inc(s_dve, 1)
                led.inc(s_dve, 1)
            ve(_ms)
            ms_t = led.val(s_dve)
            ring_t, q_t = {}, {}
            ST_PS = [ph1A, ph3A, ph1B, ph3B]
            PV_PS = [pamix, pu2a]
            h_st, h_mask, h_ex, h_pv, h_rc, h_sc, h_tb, h_fin = \
                {}, {}, {}, {}, {}, {}, {}, {}

            def load_ring(cl, wait_pe):
                rsem = [s_xa, s_xb, s_xc, s_xd][cl % 4]

                def _l(s2, cl=cl, wait_pe=wait_pe, rsem=rsem):
                    if wait_pe is not None:
                        s2.wait_ge(s_pe, wait_pe)
                    else:
                        s2.wait_ge(s_od, odt0)
                    slot = cl % 4
                    s2.dma_start(
                        out=kring[:, slot * NH * 128:(slot + 1) * NH * 128]
                        .rearrange("p (h t) -> p h t", h=NH),
                        in_=kdr.rearrange("p (h t) -> p h t", h=NH)
                        [:, :, cl * 128:(cl + 1) * 128]).then_inc(rsem, 16)
                    s2.dma_start(
                        out=vring[:, slot * NH * 65:(slot + 1) * NH * 65]
                        .rearrange("p (h x) -> p h x", h=NH)[:, :, 0:64],
                        in_=vdr[cl * 128:(cl + 1) * 128, :]
                        .rearrange("p (h x) -> p h x", h=NH)).then_inc(rsem, 16)
                sy(_l)
                led.inc(rsem, 32)
                ring_t[cl] = (rsem, led.val(rsem))

            def load_q(qt, wait_pe):
                qsem = s_oa if qt % 2 == 0 else s_ob

                def _l(s2, qt=qt, wait_pe=wait_pe, qsem=qsem):
                    if wait_pe is not None:
                        s2.wait_ge(s_pe, wait_pe)
                    else:
                        s2.wait_ge(s_od, odt0)
                    cl = qt + 2
                    s2.dma_start(
                        out=q_sb[:, (qt % 2) * 4096:(qt % 2) * 4096 + 4096]
                        .rearrange("p (h t) -> p h t", h=NH),
                        in_=qd[:, cl * 128:(cl + 1) * 128]
                        .rearrange("(h p) t -> p h t", p=64)).then_inc(qsem, 16)
                sy(_l)
                led.inc(qsem, 16)
                q_t[qt] = (qsem, led.val(qsem))

            for cl in range(2):
                load_ring(cl, None)
            load_q(0, None)
            qt_pe_done = {}
            for qt in range(NQ):
                cl = qt + 2
                load_ring(cl, qt_pe_done.get(qt - 2))
                if qt + 1 < NQ:
                    load_q(qt + 1, qt_pe_done.get(qt - 1))
                for h in range(NH):
                    stps = ST_PS[(qt * NH + h) % 4]
                    pvps = PV_PS[h % 2]
                    qh = q_sb[:, (qt % 2) * 4096 + h * 128:
                              (qt % 2) * 4096 + h * 128 + 128]
                    gq = qt * NH + h

                    def _s2(t, qt=qt, cl=cl, h=h, stps=stps, qh=qh, gq=gq):
                        for cc in range(max(0, cl - 2), cl + 1):
                            t.wait_ge(ring_t[cc][0], ring_t[cc][1])
                        t.wait_ge(q_t[qt][0], q_t[qt][1])
                        if gq == 0:
                            t.wait_ge(s_gd, qa64_t)
                        if gq >= 4:
                            t.wait_ge(s_act, h_ex[gq - 4])
                        for s in range(NSLOT):
                            slot = (cl - 2 + s) % 4
                            kcol = slot * NH * 128 + h * 128
                            t.matmul(stps[:, s * 128:(s + 1) * 128],
                                     kring[0:64, kcol:kcol + 128], qh,
                                     start=(s == 0), stop=False)
                            mm = t.matmul(stps[:, s * 128:(s + 1) * 128],
                                          kring[64:65, kcol:kcol + 128],
                                          qaug64[0:1, qt * 128:(qt + 1) * 128],
                                          start=False, stop=(s == NSLOT - 1))
                        mm.then_inc(s_pe, 1)
                        h_st[gq] = led.inc(s_pe, 1)
                    pe(_s2)

                    def _d2(v, stps=stps, gq=gq):
                        v.wait_ge(s_pe, h_st[gq])
                        v.tensor_add(stps[:, 256:384], stps[:, 256:384],
                                     trit_s[:, :]).then_inc(s_dve, 1)
                        h_mask[gq] = led.inc(s_dve, 1)
                    ve(_d2)

                    def _ex(s, qt=qt, stps=stps, gq=gq):
                        s.wait_ge(s_dve, h_mask[gq])
                        if gq >= 4:
                            s.wait_ge(s_pe, h_pv[gq - 4])
                        for sl in range(NSLOT):
                            ai = s.activation(
                                pt_bf[:, (gq % 4) * 384 + sl * 128:
                                      (gq % 4) * 384 + (sl + 1) * 128],
                                stps[:, sl * 128:(sl + 1) * 128], AF.Exp,
                                bias=stb_s[:, qt * NSLOT + sl:
                                           qt * NSLOT + sl + 1])
                        ai.then_inc(s_act, 1)
                        h_ex[gq] = led.inc(s_act, 1)
                    sc(_ex)

                    def _pv(t, cl=cl, h=h, pvps=pvps, gq=gq):
                        t.wait_ge(s_act, h_ex[gq])
                        if gq == 0:
                            t.wait_ge(s_dve, ms_t)
                        if gq >= 2:
                            t.wait_ge(s_dve, h_sc[gq - 2])
                        for s in range(NSLOT):
                            slot = (cl - 2 + s) % 4
                            mm = t.matmul(pvps[0:128, 0:65],
                                          pt_bf[:, (gq % 4) * 384 + s * 128:
                                                (gq % 4) * 384 + (s + 1) * 128],
                                          vring[:, slot * NH * 65 + h * 65:
                                                slot * NH * 65 + (h + 1) * 65],
                                          start=(s == 0), stop=(s == NSLOT - 1))
                        mm.then_inc(s_pe, 1)
                        h_pv[gq] = led.inc(s_pe, 1)
                    pe(_pv)

                    def _rc(v, pvps=pvps, gq=gq):
                        v.wait_ge(s_pe, h_pv[gq])
                        if gq >= 4:
                            v.wait_ge(s_dve, h_sc[gq - 4])
                        v.reciprocal(grow[:, 33 + (gq % 4):34 + (gq % 4)],
                                     pvps[:, 64:65]).then_inc(s_dve, 1)
                        h_rc[gq] = led.inc(s_dve, 1)
                    ve(_rc)

                    def _sc2(v, pvps=pvps, gq=gq):
                        v.wait_ge(s_dve, h_rc[gq])
                        if gq >= 4:
                            v.wait_ge(s_pe, h_tb[(gq - 4) | 1])
                        v.tensor_single_scalar(
                            tm_b[:, (gq % 4) * 64:(gq % 4) * 64 + 64],
                            pvps[:, 0:64],
                            grow[:, 33 + (gq % 4):34 + (gq % 4)],
                            ALU.mult).then_inc(s_dve, 1)
                        h_sc[gq] = led.inc(s_dve, 1)
                    ve(_sc2)

                    if h % 2 == 1:
                        def _tb(t, gq=gq):
                            t.wait_ge(s_dve, h_sc[gq])
                            if gq >= 2:
                                t.wait_ge(s_act, h_fin[gq - 2])
                            pr = (gq // 2) % 2
                            t.matmul(pu2c[:, 0:128],
                                     tm_b[:, pr * 128:pr * 128 + 128],
                                     id_b[:, :], start=True,
                                     stop=True).then_inc(s_pe, 1)
                            h_tb[gq] = h_tb[gq - 1] = led.inc(s_pe, 1)
                        pe(_tb)

                        def _fin(s, qt=qt, h=h, gq=gq):
                            s.wait_ge(s_pe, h_tb[gq])
                            hc = h // 2
                            s.activation(attn_sb[:, hc * T_OWN + qt * 128:
                                                 hc * T_OWN + qt * 128 + 128],
                                         pu2c[:, 0:128],
                                         AF.Identity).then_inc(s_act, 1)
                            h_fin[gq] = h_fin[gq - 1] = led.inc(s_act, 1)
                        sc(_fin)
                qt_pe_done[qt] = led.val(s_pe)
            barrier()

        def proj_phase():
            wdts = {}
            pe_done = {}
            pev_hist = {}

            def loadw(dj, wait_pe):
                wsem = s_wa if dj % 2 == 0 else s_wb

                def _l(g, dj=dj, wait_pe=wait_pe, wsem=wsem):
                    if wait_pe is not None:
                        g.wait_ge(s_pe, wait_pe)
                    g.dma_start(out=wqs[:, (dj % 2) * 2048:(dj % 2) * 2048 + 2048],
                                 in_=wp[dj]).then_inc(wsem, 16)
                    led.inc(wsem, 16)
                gp(_l)
                wdts[dj] = (wsem, led.val(wsem))

            loadw(0, None)
            for dj in range(KD):
                if dj + 1 < KD:
                    loadw(dj + 1, pe_done.get(dj - 1))
                for sbi, (boff, w) in enumerate(BLK2):
                    ps = ph1A if (dj * 2 + sbi) % 2 == 0 else ph1B
                    oprev = led.val(s_od)

                    def _xl(s2, dj=dj, boff=boff, w=w, sbi=sbi, oprev=oprev):
                        s2.wait_ge(s_od, oprev)
                        s2.dma_start(out=xpj[:, (sbi % 2) * 512:
                                             (sbi % 2) * 512 + w],
                                     in_=xres[dj * 128:(dj + 1) * 128,
                                              boff:boff + w]).then_inc(s_xd, 16)
                        led.inc(s_xd, 16)
                    sy(_xl)
                    xdt = led.val(s_xd)

                    def _mm(t, dj=dj, boff=boff, w=w, ps=ps,
                            dvp=pev_hist.get(dj * 2 + sbi - 2, 0)):
                        t.wait_ge(wdts[dj][0], wdts[dj][1])
                        t.wait_ge(s_dve, dvp)
                        for k in range(KD):
                            mm = t.matmul(ps[:, 0:w],
                                          wqs[:, (dj % 2) * 2048 + k * 128:
                                              (dj % 2) * 2048 + (k + 1) * 128],
                                          attn_sb[:, k * T_OWN + boff - HALO:
                                                  k * T_OWN + boff - HALO + w],
                                          start=(k == 0), stop=(k == KD - 1))
                            if k == KD - 1:
                                mm.then_inc(s_pe, 1)
                        led.inc(s_pe, 1)
                    pe(_mm)
                    pt = led.val(s_pe)

                    def _ev(v, dj=dj, w=w, sbi=sbi, pt=pt, ps=ps, xdt=xdt):
                        v.wait_ge(s_pe, pt)
                        v.wait_ge(s_xd, xdt)
                        v.scalar_tensor_tensor(
                            xfp[:, 0:w], ps[:, 0:w], bproj_s[:, dj:dj + 1],
                            xpj[:, (sbi % 2) * 512:(sbi % 2) * 512 + w],
                            ALU.add, ALU.add).then_inc(s_dve, 1)
                        led.inc(s_dve, 1)
                    ve(_ev)
                    dvt = led.val(s_dve)
                    pev_hist[dj * 2 + sbi] = dvt

                    def _st(s2, dj=dj, boff=boff, w=w, dvt=dvt):
                        s2.wait_ge(s_dve, dvt)
                        s2.dma_start(out=xres[dj * 128:(dj + 1) * 128,
                                              boff:boff + w],
                                     in_=xfp[:, 0:w]).then_inc(s_od, 16)
                        led.inc(s_od, 16)
                    sy(_st)
                pe_done[dj] = led.val(s_pe)
            odf = led.val(s_od)

            def _oddrain2(t, odf=odf):
                t.wait_ge(s_od, odf)
            pe(_oddrain2)
            barrier()

        ln_phase(x0, BLK1)

        def _castwait(t, gdt=gdt):
            t.wait_ge(s_gd, gdt)
        pe(_castwait)
        barrier()
        gate_phase(list(range(NT)))
        uprod(BLK1)
        floop(BLK1)
        u2fin(BLK1)
        dloop(x0, xres, 0, BLK1)
        ln_phase(xres, BLK1)
        qkv_phase()
        attn_phase()
        proj_phase()
        ln_phase(xres, BLK2)
        gate_phase(list(range(2, NT)))
        uprod(BLK2)
        floop(BLK2)
        u2fin(BLK2)
        dloop(xres, y, HALO, BLK2)

        fin = led.val(s_od)

        def _fin(s2, fin=fin):
            s2.wait_ge(s_od, fin)
        sy(_fin)

        B.tensor(R_pe.replay)
        B.scalar(R_sc.replay)
        B.vector(R_ve.replay)
        B.gpsimd(R_gp.replay)
        B.sync(R_sy.replay)

    return nc


def _prep_weights(i):
    w1, w3, w2 = i["L0_ffn_w1"], i["L0_ffn_w3"], i["L0_ffn_w2"]
    a1, b1 = i["L0_ffn_a1"], i["L0_ffn_b1"]
    a3, b3 = i["L0_ffn_a3"], i["L0_ffn_b3"]
    a2, b2 = i["L0_ffn_a2"], i["L0_ffn_b2"]
    gw = i["L0_ffn_gate_w"]
    W_attn, b_attn = i["L0_attn_W_attn"], i["L0_attn_b_attn"]
    W_proj, b_proj = i["L0_attn_W_proj"], i["L0_attn_b_proj"]

    c = np.ascontiguousarray
    w1r = np.asarray(w1).reshape(CF, 128, KD, 128).transpose(0, 3, 2, 1)
    w3r = np.asarray(w3).reshape(CF, 128, KD, 128).transpose(0, 3, 2, 1)
    w13v = c(np.stack([w1r, w3r], axis=2).reshape(CF, 128, 4096))
    b1r = np.asarray(b1).reshape(E, CF, 128, R).transpose(1, 3, 0, 2)
    b3r = np.asarray(b3).reshape(E, CF, 128, R).transpose(1, 3, 0, 2)
    b13v = c(np.stack([b1r, b3r], axis=2).reshape(CF, 16, 2048))
    a1r = np.asarray(a1).reshape(E, R, KD, 128).transpose(3, 2, 0, 1)
    a3r = np.asarray(a3).reshape(E, R, KD, 128).transpose(3, 2, 0, 1)
    a13v = c(np.stack([a1r, a3r], axis=1).reshape(128, 4096))
    a2v = c(np.asarray(a2).reshape(E, R, CF, 128)
            .transpose(3, 2, 0, 1).reshape(128, CF * 128))
    b2v = c(np.asarray(b2).reshape(E, KD, 128, R)
            .transpose(3, 1, 0, 2).reshape(16, KD * E * 128))
    gwv = c(np.asarray(gw).reshape(E, KD, 128).transpose(2, 1, 0)
            .reshape(128, KD * E))
    wqv = c(np.asarray(W_attn).reshape(KD, 128, 48, 128)
            .transpose(2, 1, 0, 3).reshape(48, 128, 2048))
    wpv = c(np.asarray(W_proj).reshape(KD, 128, KD, 128)
            .transpose(2, 1, 0, 3).reshape(KD, 128, 2048))
    w2v = c(np.asarray(w2).reshape(KD, 128, CF, 128)
            .transpose(0, 3, 2, 1).reshape(KD, 128, F))
    bav = c(np.asarray(b_attn).reshape(48, 128).T)
    bpv = c(np.asarray(b_proj).reshape(KD, 128).T)
    return dict(w13=w13v, b13=b13v, a13=a13v, a2pi=a2v, b2pi=b2v, gwt=gwv,
                wq=wqv, wp=wpv, w2r=w2v, battn=bav, bproj=bpv)


def _prep_meta(core):
    stbv = np.zeros((NQ * NSLOT, 128), np.float32)
    for qt in range(NQ):
        for s in range(NSLOT):
            cl = qt + s
            col = M_AL * (cl * 128 + np.arange(128, dtype=np.float32))
            if core == 0 and cl < 2:
                col = col + NEGM
            stbv[qt * NSLOT + s] = col
    qau = np.zeros((2, NQ * 128), np.float32)
    for qt in range(NQ):
        qau[0, qt * 128:(qt + 1) * 128] = -M_AL * (
            (qt + 2) * 128 + np.arange(128, dtype=np.float32)) - 12.0
    qau[1] = 1.0
    mk = np.zeros((2, T_ALL), np.float32)
    mk[0] = 1.0
    mk[1] = M_AL * np.arange(T_ALL, dtype=np.float32)
    tri_v = np.where(np.arange(128)[None, :] > np.arange(128)[:, None],
                     np.float32(NEGM), np.float32(0.0))
    return dict(stb=np.ascontiguousarray(stbv.T), qaug=qau, mkrow=mk,
                ident=np.eye(128, dtype=np.float32),
                tri=np.ascontiguousarray(tri_v),
                trit=np.ascontiguousarray(tri_v.T),
                onesf=np.ones((1, 128), np.float32))


def kernel(**inputs):
    x = np.asarray(inputs["L0_attn_hidden_states"], np.float32)[0]
    xT = np.ascontiguousarray(x.T)
    wdict = _prep_weights(inputs)
    nc = build_program()
    in_maps = []
    for corei in range(8):
        g0 = corei * T_OWN
        if corei == 0:
            halo = np.zeros((D, HALO), np.float32)
        else:
            halo = xT[:, g0 - HALO:g0]
        x0c = np.ascontiguousarray(
            np.concatenate([halo, xT[:, g0:g0 + T_OWN]], axis=1))
        m = dict(wdict)
        m.update(_prep_meta(corei))
        m["x0"] = x0c
        in_maps.append(m)
    res = run_bass_kernel_spmd(nc, in_maps, core_ids=list(range(8)))
    out = np.empty((8192, D), np.float32)
    for corei in range(8):
        out[corei * T_OWN:(corei + 1) * T_OWN, :] = res.results[corei]["y"].T
    return out.reshape(1, 8192, D)


# version 15 (final)
# speedup vs baseline: 1.4244x; 1.0461x over previous
import numpy as np
import concourse.bass as bass
import concourse.mybir as mybir
from concourse.bass_utils import run_bass_kernel_spmd

F32 = mybir.dt.float32
BF16 = mybir.dt.bfloat16
AF = mybir.ActivationFunctionType
ALU = mybir.AluOpType
AX = mybir.AxisListType

T_OWN, HALO = 1024, 256
T_ALL = T_OWN + HALO
D, KD, F, CF, E, R, NH, DH = 2048, 16, 5632, 44, 8, 16, 32, 64
NT, NQ, NSLOT = T_ALL // 128, T_OWN // 128, 3
M_AL = 2.0 ** (-0.25)
BLK1 = [(0, 512), (512, 512), (1024, 256)]
BLK2 = [(256, 512), (768, 512)]
NEGM = -30000.0




class _InstRec:
    __slots__ = ("chain",)

    def __init__(self):
        self.chain = []

    def then_inc(self, sem, val):
        self.chain.append(("then_inc", (sem, val)))
        return self


class _EngRec:
    __slots__ = ("calls",)

    def __init__(self):
        self.calls = []

    def __getattr__(self, name):
        def _call(*a, **k):
            inst = _InstRec()
            self.calls.append((name, a, k, inst))
            return inst
        return _call

    def replay(self, eng):
        for name, a, k, inst in self.calls:
            real = getattr(eng, name)(*a, **k)
            for cn, ca in inst.chain:
                real = getattr(real, cn)(*ca)


class Led:
    def __init__(self):
        self.c = {}

    def inc(self, s, n=1):
        self.c[s.name] = self.c.get(s.name, 0) + n
        return self.c[s.name]

    def val(self, s):
        return self.c.get(s.name, 0)


def build_program():
    nc = bass.Bass("TRN2", target_bir_lowering=False, debug=False)
    led = Led()

    def inp(name, shape):
        return nc.dram_tensor(name, shape, F32, kind="ExternalInput").ap()

    x0 = inp("x0", [D, T_ALL])
    w13 = inp("w13", [CF, 128, 4096])
    b13 = inp("b13", [CF, 16, 2048])
    a13 = inp("a13", [128, 4096])
    a2pi = inp("a2pi", [128, CF * 128])
    b2pi = inp("b2pi", [16, KD * E * 128])
    gwt = inp("gwt", [128, KD * E])
    wq = inp("wq", [48, 128, 2048])
    wp = inp("wp", [KD, 128, 2048])
    w2r = inp("w2r", [KD, 128, F])
    battn = inp("battn", [128, 48])
    bproj = inp("bproj", [128, KD])
    stb = inp("stb", [128, NQ * NSLOT])
    qaug = inp("qaug", [2, NQ * 128])
    mkrow = inp("mkrow", [2, T_ALL])
    ident = inp("ident", [128, 128])
    tri = inp("tri", [128, 128])
    trit = inp("trit", [128, 128])
    onesf = inp("onesf", [1, 128])

    y = nc.dram_tensor("y", [D, T_OWN], F32, kind="ExternalOutput").ap()

    xres = nc.dram_tensor("xres", [D, T_ALL], F32).ap()
    qd = nc.dram_tensor("qd", [D, T_ALL], BF16).ap()
    kdr = nc.dram_tensor("kdr", [66, NH * T_ALL], BF16).ap()
    vdr = nc.dram_tensor("vdr", [T_ALL, NH * DH], BF16).ap()
    amx = nc.dram_tensor("amx", [CF, 128, T_ALL], BF16).ap()

    import contextlib
    with contextlib.ExitStack() as _st:
        xn_all = _st.enter_context(nc.sbuf_tensor("xn_all", [128, KD * T_ALL], BF16))
        REGA = _st.enter_context(nc.sbuf_tensor("REGA", [128, 2 * E * T_ALL], BF16))
        REGB = _st.enter_context(nc.sbuf_tensor("REGB", [128, E * T_ALL], BF16))
        REGC = _st.enter_context(nc.sbuf_tensor("REGC", [128, KD * T_OWN], BF16))
        REGD = _st.enter_context(nc.sbuf_tensor("REGD", [128, 17408], BF16))
        REGF = _st.enter_context(nc.sbuf_tensor("REGF", [128, 4096], F32))
        a13b = _st.enter_context(nc.sbuf_tensor("a13b", [128, 4096], BF16))
        a2pb = _st.enter_context(nc.sbuf_tensor("a2pb", [128, CF * 128], BF16))
        gwb = _st.enter_context(nc.sbuf_tensor("gwb", [128, KD * E], BF16))
        wfull = _st.enter_context(nc.sbuf_tensor("wfull", [128, E * NT], F32))
        battn_s = _st.enter_context(nc.sbuf_tensor("battn_s", [128, 48], F32))
        battn_q = _st.enter_context(nc.sbuf_tensor("battn_q", [128, 16], F32))
        bproj_s = _st.enter_context(nc.sbuf_tensor("bproj_s", [128, KD], F32))
        stb_s = _st.enter_context(nc.sbuf_tensor("stb_s", [128, NQ * NSLOT], F32))
        qaug_b = _st.enter_context(nc.sbuf_tensor("qaug_b", [2, NQ * 128], BF16))
        id_b = _st.enter_context(nc.sbuf_tensor("id_b", [128, 128], BF16))
        tri_s = _st.enter_context(nc.sbuf_tensor("tri_s", [128, 128], F32))
        trit_s = _st.enter_context(nc.sbuf_tensor("trit_s", [128, 128], F32))
        ones_f = _st.enter_context(nc.sbuf_tensor("ones_f", [1, 128], F32))
        onescol_b = _st.enter_context(nc.sbuf_tensor("onescol_b", [128, 1], BF16))
        one_b = _st.enter_context(nc.sbuf_tensor("one_b", [1, 1], BF16))
        grow = _st.enter_context(nc.sbuf_tensor("grow", [128, 40], F32))
        wf_b = _st.enter_context(nc.sbuf_tensor("wf_b", [128, 8], BF16))
        nmx_b = _st.enter_context(nc.sbuf_tensor("nmx_b", [128, 1], BF16))
        nmx2 = _st.enter_context(nc.sbuf_tensor("nmx2", [128, 2], BF16))
        eps_s = _st.enter_context(nc.sbuf_tensor("eps_s", [128, 1], F32))
        onesrow_b = _st.enter_context(nc.sbuf_tensor("onesrow_b", [1, 128], BF16))
        qa2 = _st.enter_context(nc.sbuf_tensor("qa2", [65, 128], BF16))
        tm_b = _st.enter_context(nc.sbuf_tensor("tm_b", [128, 4 * 64], BF16))
        ph1A = _st.enter_context(nc.psum_tensor("ph1A", [128, 512], F32))
        ph3A = _st.enter_context(nc.psum_tensor("ph3A", [128, 512], F32))
        ph1B = _st.enter_context(nc.psum_tensor("ph1B", [128, 512], F32))
        ph3B = _st.enter_context(nc.psum_tensor("ph3B", [128, 512], F32))
        pamix = _st.enter_context(nc.psum_tensor("pamix", [128, 512], F32))
        pu2a = _st.enter_context(nc.psum_tensor("pu2a", [128, 512], F32))
        pu2b = _st.enter_context(nc.psum_tensor("pu2b", [128, 512], F32))
        pu2c = _st.enter_context(nc.psum_tensor("pu2c", [128, 512], F32))
        s_ld = _st.enter_context(nc.semaphore("s_ld"))
        s_gd = _st.enter_context(nc.semaphore("s_gd"))
        s_xa = _st.enter_context(nc.semaphore("s_xa"))
        s_xc = _st.enter_context(nc.semaphore("s_xc"))
        s_xb = _st.enter_context(nc.semaphore("s_xb"))
        s_wa = _st.enter_context(nc.semaphore("s_wa"))
        s_wb = _st.enter_context(nc.semaphore("s_wb"))
        s_oa = _st.enter_context(nc.semaphore("s_oa"))
        s_ob = _st.enter_context(nc.semaphore("s_ob"))
        s_wd = _st.enter_context(nc.semaphore("s_wd"))
        s_xd = _st.enter_context(nc.semaphore("s_xd"))
        s_od = _st.enter_context(nc.semaphore("s_od"))
        s_pe = _st.enter_context(nc.semaphore("s_pe"))
        s_pe2 = _st.enter_context(nc.semaphore("s_pe2"))
        s_act = _st.enter_context(nc.semaphore("s_act"))
        s_dve = _st.enter_context(nc.semaphore("s_dve"))
        s_pool = _st.enter_context(nc.semaphore("s_pool"))
        s_cast1 = _st.enter_context(nc.semaphore("s_cast1"))
        s_cast2 = _st.enter_context(nc.semaphore("s_cast2"))
        s_bar = _st.enter_context(nc.semaphore("s_bar"))
        B = _st.enter_context(nc.Block())
        uN = REGA[64:80, :]
        u2T = REGA[32:48, 0:E * T_ALL]
        kring = REGA[0:66, 0:4 * NH * 128]
        wb_sb = REGB[:, :]
        vring = REGB[:, 0:4 * NH * 65]
        s_bf = REGC[:, 0:1536]
        act_bf = REGC[:, 1536:3072]
        actw_bf = REGC[:, 3072:7168]
        amx_st = REGC[:, 7168:8192]
        u2s = REGC[:, 8192:8704]
        silu_sb = REGC[:, 8704:10240]
        tmp_sb = REGC[:, 10240:11264]
        b13A = REGD[64:80, 8192:10240]
        b13B = REGD[64:80, 10240:12288]
        bsREG = [REGC[:, 12800 + i * 512:12800 + (i + 1) * 512] for i in range(4)]
        qkst = REGC[:, 14848:15872]
        xt16 = REGC[:, 0:2048]
        attn_sb = REGC[:, :]
        h1s = REGF[:, 1024:2560]
        h3s = REGF[:, 2560:4096]
        ws = REGD[:, 0:8192]
        w2sl = REGD[:, 0:11264]
        b2dg = REGD[32:48, 11264:13312]
        adg = REGD[:, 13312:17408]
        wtrow = REGD[0:1, 0:E * NT * 128]
        wqs = REGD[:, 0:4096]
        q_sb = REGD[0:64, 4096:12288]
        pt_bf = REGD[:, 12288:12288 + 4 * NSLOT * 128]
        qaug64 = REGD[64:66, 4096:4096 + NQ * 128]
        qr64 = REGD[64:65, 13056:13056 + 256]
        xln = REGF[:, 0:512]
        xln2 = REGF[:, 512:1024]
        xfp = REGF[:, 1024:1536]
        rows = REGF[0:1, 1536:3584]
        xio = REGF[:, 0:2048]
        xout = REGF[:, 2048:4096]
        xpj = REGF[:, 1536:2560]
        amixf = REGF[:, 0:1024]

        PU2 = [pu2a, pu2b, pu2c]
        R_pe, R_sc, R_ve, R_gp, R_sy = (_EngRec() for _ in range(5))
        pe = lambda f: f(R_pe)
        sc = lambda f: f(R_sc)
        ve = lambda f: f(R_ve)
        gp = lambda f: f(R_gp)
        sy = lambda f: f(R_sy)
        nbar = [0]

        def barrier():
            nbar[0] += 1
            tgt = 5 * nbar[0]
            for em in (pe, sc, ve, gp, sy):
                def _f(e, tgt=tgt):
                    e.drain().then_inc(s_bar, 1)
                    e.wait_ge(s_bar, tgt)
                em(_f)
            led.c[s_bar.name] = tgt

        def _p0g(g):
            for t_in, t_out in ((a13, a13b[:, :]), (a2pi, a2pb[:, :]),
                                (gwt, gwb[:, :]), (ident, id_b[:, :]),
                                (qaug, qaug_b[:, :])):
                g.dma_start(out=t_out, in_=t_in).then_inc(s_gd, 16)
                led.inc(s_gd, 16)
            for h in range(NH):
                g.dma_start(out=kdr[64:66, h * T_ALL:(h + 1) * T_ALL],
                            in_=mkrow).then_inc(s_gd, 16)
                led.inc(s_gd, 16)
        gp(_p0g)

        def _p0s(s):
            for t_in, t_out in ((battn, battn_s[:, :]), (bproj, bproj_s[:, :]),
                                (stb, stb_s[:, :]), (tri, tri_s[:, :]),
                                (trit, trit_s[:, :]), (onesf, ones_f[:, :])):
                s.dma_start(out=t_out, in_=t_in).then_inc(s_ld, 16)
                led.inc(s_ld, 16)
        sy(_p0s)
        ldt = led.val(s_ld)
        gdt = led.val(s_gd)

        def _p0v(v, ldt=ldt, gdt=gdt):
            v.wait_ge(s_ld, ldt)
            v.memset(onescol_b[:, :], 1.0)
            v.memset(one_b[:, :], 1.0)
            v.memset(eps_s[:, :], 1e-5)
            v.memset(onesrow_b[:, :], 1.0)
        ve(_p0v)

        def _p0sc(s, ldt=ldt, gdt=gdt):
            s.wait_ge(s_ld, ldt)
            s.activation(battn_q[:, :], battn_s[:, 0:16], AF.Identity, scale=0.125)
        sc(_p0sc)
        barrier()

        def ln_phase(src_ap, blocks):
            for (boff, w) in blocks:
                dve0 = led.val(s_dve)
                pe0 = led.val(s_pe)
                for k in range(KD):
                    sem = s_xa if k % 2 == 0 else s_xb

                    def _ld(s2, k=k, boff=boff, w=w, dve0=dve0, sem=sem):
                        s2.wait_ge(s_dve, dve0 + k - 1 if k >= 2 else dve0)
                        dst = xln if k % 2 == 0 else xln2
                        s2.dma_start(out=dst[:, 0:w],
                                     in_=src_ap[k * 128:(k + 1) * 128,
                                                boff:boff + w]).then_inc(sem, 16)
                        led.inc(sem, 16)
                    sy(_ld)
                    xdt = led.val(sem)

                    def _cast(v, k=k, w=w, xdt=xdt, sem=sem, pe0=pe0):
                        v.wait_ge(sem, xdt)
                        if k >= 2:
                            v.wait_ge(s_pe, pe0 + k - 1)
                        src = xln if k % 2 == 0 else xln2
                        xb = xt16[:, (k % 2) * 1024:(k % 2) * 1024 + w]
                        xq = xt16[:, (k % 2) * 1024 + 512:(k % 2) * 1024 + 512 + w]
                        v.tensor_copy(xb, src[:, 0:w])
                        v.tensor_mul(xq, src[:, 0:w], src[:, 0:w]).then_inc(s_dve, 1)
                        led.inc(s_dve, 1)
                    ve(_cast)
                    dvt = led.val(s_dve)

                    def _mm(t, k=k, w=w, dvt=dvt):
                        t.wait_ge(s_dve, dvt)
                        xb = xt16[:, (k % 2) * 1024:(k % 2) * 1024 + w]
                        xq = xt16[:, (k % 2) * 1024 + 512:(k % 2) * 1024 + 512 + w]
                        t.matmul(pamix[0:1, 0:w], onescol_b[:, :], xb,
                                 start=(k == 0), stop=(k == KD - 1))
                        t.matmul(ph1B[0:1, 0:w], onescol_b[:, :], xq,
                                 start=(k == 0),
                                 stop=(k == KD - 1)).then_inc(s_pe, 1)
                        led.inc(s_pe, 1)
                    pe(_mm)
                pet = led.val(s_pe)

                def _st1(s, w=w, pet=pet):
                    s.wait_ge(s_pe, pet)
                    s.activation(rows[:, 0:w], pamix[0:1, 0:w], AF.Identity,
                                 scale=1.0 / D)
                    s.activation(rows[:, 512:512 + w], ph1B[0:1, 0:w],
                                 AF.Identity, scale=1.0 / D).then_inc(s_act, 1)
                    led.inc(s_act, 1)
                sc(_st1)
                at = led.val(s_act)

                def _st2(v, w=w, at=at):
                    v.wait_ge(s_act, at)
                    v.tensor_mul(rows[:, 1024:1024 + w], rows[:, 0:w],
                                 rows[:, 0:w]).then_inc(s_dve, 1)
                    v.wait_ge(s_dve, led.inc(s_dve, 1))
                    v.tensor_sub(rows[:, 512:512 + w], rows[:, 512:512 + w],
                                 rows[:, 1024:1024 + w]).then_inc(s_dve, 1)
                    led.inc(s_dve, 1)
                ve(_st2)
                dvt2 = led.val(s_dve)

                def _st3(s, w=w, dvt2=dvt2):
                    s.wait_ge(s_dve, dvt2)
                    s.activation(rows[:, 512:512 + w], rows[:, 512:512 + w],
                                 AF.Ln, bias=eps_s[0:1, 0:1]).then_inc(s_act, 1)
                    s.wait_ge(s_act, led.inc(s_act, 1))
                    s.activation(rows[:, 512:512 + w], rows[:, 512:512 + w],
                                 AF.Exp, scale=-0.5).then_inc(s_act, 1)
                    led.inc(s_act, 1)
                sc(_st3)
                at2 = led.val(s_act)

                def _st4(v, w=w, at2=at2):
                    v.wait_ge(s_act, at2)
                    v.tensor_mul(rows[:, 1536:1536 + w], rows[:, 0:w],
                                 rows[:, 512:512 + w]).then_inc(s_dve, 1)
                    led.inc(s_dve, 1)
                ve(_st4)
                dvt3 = led.val(s_dve)

                def _bc(t, w=w, dvt3=dvt3):
                    t.wait_ge(s_dve, dvt3)
                    t.matmul(ph1A[:, 0:w], ones_f[:, :], rows[:, 512:512 + w],
                             start=True, stop=True)
                    t.matmul(ph3A[:, 0:w], ones_f[:, :], rows[:, 1536:1536 + w],
                             start=True, stop=True).then_inc(s_pe, 1)
                    led.inc(s_pe, 1)
                pe(_bc)
                pet2 = led.val(s_pe)
                dve1 = led.val(s_dve)
                for k in range(KD):
                    sem = s_xa if k % 2 == 0 else s_xb

                    def _ld2(s2, k=k, boff=boff, w=w, dve1=dve1, sem=sem):
                        s2.wait_ge(s_dve, dve1 + 2 * (k - 1) if k >= 2 else dve1)
                        dst = xln if k % 2 == 0 else xln2
                        s2.dma_start(out=dst[:, 0:w],
                                     in_=src_ap[k * 128:(k + 1) * 128,
                                                boff:boff + w]).then_inc(sem, 16)
                        led.inc(sem, 16)
                    sy(_ld2)
                    xdt2 = led.val(sem)

                    dvp = led.val(s_dve)

                    def _nm(v, k=k, boff=boff, w=w, xdt2=xdt2, pet2=pet2, sem=sem,
                            dvp=dvp):
                        v.wait_ge(sem, xdt2)
                        v.wait_ge(s_dve, dvp)
                        if k == 0:
                            v.wait_ge(s_pe, pet2)
                        src = xln if k % 2 == 0 else xln2
                        v.tensor_mul(xfp[:, 0:w], src[:, 0:w],
                                     ph1A[:, 0:w]).then_inc(s_dve, 1)
                        v.wait_ge(s_dve, led.inc(s_dve, 1))
                        v.tensor_sub(xn_all[:, k * T_ALL + boff:
                                            k * T_ALL + boff + w],
                                     xfp[:, 0:w], ph3A[:, 0:w]).then_inc(s_dve, 1)
                        led.inc(s_dve, 1)
                    ve(_nm)
            barrier()

        def gate_phase(tiles):
            for gi in tiles:
                def _gmm(t, gi=gi):
                    for k in range(KD):
                        mm = t.matmul(pamix[:, 0:8],
                                      xn_all[:, k * T_ALL + gi * 128:
                                             k * T_ALL + gi * 128 + 128],
                                      gwb[:, k * 8:(k + 1) * 8],
                                      start=(k == 0), stop=(k == KD - 1))
                    mm.then_inc(s_pe, 1)
                    led.inc(s_pe, 1)
                pe(_gmm)
                pt = led.val(s_pe)

                def _g1(v, pt=pt):
                    v.wait_ge(s_pe, pt)
                    v.tensor_reduce(grow[:, 0:1], pamix[:, 0:8], AX.X,
                                    ALU.max).then_inc(s_dve, 1)
                    v.tensor_reduce(grow[:, 1:2], pamix[:, 0:8], AX.X, ALU.max,
                                    negate=True)
                    v.wait_ge(s_dve, led.inc(s_dve, 1))
                    v.tensor_single_scalar(grow[:, 8:16], pamix[:, 0:8],
                                           grow[:, 0:1], ALU.is_ge).then_inc(s_dve, 1)
                    v.wait_ge(s_dve, led.inc(s_dve, 1))
                    v.scalar_tensor_tensor(grow[:, 16:24], grow[:, 8:16], -1e30,
                                           pamix[:, 0:8], ALU.mult,
                                           ALU.add).then_inc(s_dve, 1)
                    v.wait_ge(s_dve, led.inc(s_dve, 1))
                    v.tensor_reduce(grow[:, 2:3], grow[:, 16:24], AX.X,
                                    ALU.max).then_inc(s_dve, 1)
                    led.inc(s_dve, 1)
                ve(_g1)
                dt = led.val(s_dve)

                def _g2(s, dt=dt):
                    s.wait_ge(s_dve, dt)
                    s.activation(grow[:, 24:32], pamix[:, 0:8], AF.Exp,
                                 bias=grow[:, 1:2])
                    s.activation(grow[:, 3:4], grow[:, 2:3], AF.Exp,
                                 bias=grow[:, 1:2]).then_inc(s_act, 1)
                    led.inc(s_act, 1)
                sc(_g2)
                at = led.val(s_act)

                def _g3(v, gi=gi, at=at):
                    v.wait_ge(s_act, at)
                    v.tensor_single_scalar(grow[:, 4:5], grow[:, 3:4], 1.0,
                                           ALU.add).then_inc(s_dve, 1)
                    v.wait_ge(s_dve, led.inc(s_dve, 1))
                    v.reciprocal(grow[:, 5:6], grow[:, 4:5])
                    v.tensor_single_scalar(grow[:, 16:24], pamix[:, 0:8],
                                           grow[:, 2:3], ALU.is_ge).then_inc(s_dve, 1)
                    v.wait_ge(s_dve, led.inc(s_dve, 1))
                    v.scalar_tensor_tensor(wfull[:, gi * 8:gi * 8 + 8],
                                           grow[:, 24:32], grow[:, 5:6],
                                           grow[:, 16:24], ALU.mult,
                                           ALU.mult).then_inc(s_dve, 1)
                    v.wait_ge(s_dve, led.inc(s_dve, 1))
                    v.tensor_copy(wf_b[:, :],
                                  wfull[:, gi * 8:gi * 8 + 8]).then_inc(s_dve, 1)
                    led.inc(s_dve, 1)
                ve(_g3)
                dt2 = led.val(s_dve)

                for e in range(E):
                    def _g4b(t, e=e, dt2=dt2):
                        t.wait_ge(s_dve, dt2)
                        if e > 0:
                            t.wait_ge(s_act, led.val(s_act))
                        t.matmul(ph1B[0:1, 0:128], wf_b[:, e:e + 1], id_b[:, :],
                                 start=True, stop=True).then_inc(s_pe, 1)
                        led.inc(s_pe, 1)
                    pe(_g4b)
                    pt2 = led.val(s_pe)

                    def _g5(s, gi=gi, e=e, pt2=pt2):
                        s.wait_ge(s_pe, pt2)
                        s.activation(wtrow[:, (e * NT + gi) * 128:
                                           (e * NT + gi) * 128 + 128],
                                     ph1B[0:1, 0:128],
                                     AF.Identity).then_inc(s_act, 1)
                        led.inc(s_act, 1)
                    sc(_g5)
            att = led.val(s_act)
            for e in range(E):
                for gi in tiles:
                    def _g6(t, e=e, gi=gi, att=att):
                        t.wait_ge(s_act, att)
                        if not (e == 0 and gi == tiles[0]):
                            t.wait_ge(s_act, led.val(s_act))
                        t.matmul(ph3B[:, 0:128], onesrow_b[:, :],
                                 wtrow[:, (e * NT + gi) * 128:
                                       (e * NT + gi) * 128 + 128],
                                 start=True, stop=True).then_inc(s_pe, 1)
                        led.inc(s_pe, 1)
                    pe(_g6)
                    pt3 = led.val(s_pe)

                    def _g7(s, e=e, gi=gi, pt3=pt3):
                        s.wait_ge(s_pe, pt3)
                        s.activation(wb_sb[:, e * T_ALL + gi * 128:
                                           e * T_ALL + gi * 128 + 128],
                                     ph3B[:, 0:128], AF.Identity).then_inc(s_act, 1)
                        led.inc(s_act, 1)
                    sc(_g7)
            barrier()

        def uprod(blocks):
            prev_at = [led.val(s_act)]
            prev_xd = [led.val(s_gd)]
            for p in range(2):
                for (boff, w) in blocks:
                    def _u1(t, p=p, boff=boff, w=w, pa=prev_at[0]):
                        t.wait_ge(s_act, pa)
                        for k in range(KD):
                            mm = t.matmul(ph1A[:, 0:w],
                                          a13b[:, (p * KD + k) * 128:
                                               (p * KD + k + 1) * 128],
                                          xn_all[:, k * T_ALL + boff:
                                                 k * T_ALL + boff + w],
                                          start=(k == 0), stop=(k == KD - 1))
                            if k == KD - 1:
                                mm.then_inc(s_pe, 1)
                        led.inc(s_pe, 1)
                    pe(_u1)
                    pt = led.val(s_pe)

                    def _u2(s, w=w, pt=pt, px=prev_xd[0]):
                        s.wait_ge(s_pe, pt)
                        s.wait_ge(s_gd, px)
                        s.activation(u2s[:, 0:w], ph1A[:, 0:w], AF.Identity,
                                     scale=2.0).then_inc(s_act, 1)
                        led.inc(s_act, 1)
                    sc(_u2)
                    at = led.val(s_act)
                    prev_at[0] = at

                    def _u3(g, p=p, boff=boff, w=w, at=at):
                        g.wait_ge(s_act, at)
                        for e in range(E):
                            col = (p * E + e) * T_ALL + boff
                            g.dma_start(out=uN[:, col:col + w],
                                        in_=u2s[16 * e:16 * e + 16, 0:w]
                                        ).then_inc(s_gd, 16)
                            led.inc(s_gd, 16)
                    gp(_u3)
                    prev_xd[0] = led.val(s_gd)
            gfin = led.val(s_gd)

            def _udrain(t, gfin=gfin):
                t.wait_ge(s_gd, gfin)
            pe(_udrain)
            barrier()

        def floop(blocks):
            H_LORA = [ph1B, ph3B, pamix]
            w13t, chunk_pe_done, o_fin = {}, {}, {}
            P0_ = led.val(s_pe)
            A0_ = led.val(s_act)
            D0_ = led.val(s_dve)
            PL_ = led.val(s_pool)
            P2_ = led.val(s_pe2)
            actwP = [[actw_bf[:, e * 512:(e + 1) * 512] for e in range(E)],
                     [REGC[:, e * 512:(e + 1) * 512] for e in range(6)]
                     + [REGC[:, 11264 + e * 512:11264 + (e + 1) * 512]
                        for e in range(2)]]

            def h_lora(G):
                return P0_ + (G // 16) * 18 + 2 + (G % 16) + 1

            def h_pe_done(u):
                return P0_ + (u + 1) * 18

            def h_bsev(u):
                return A0_ + u * 11 + 2

            def h_silu(g):
                return A0_ + (g // 8) * 11 + 2 + (g % 8) + 1

            def h_amxev(u):
                return A0_ + (u + 1) * 11

            def h_add(G):
                return D0_ + (G // 16) * 24 + (G % 16) + 1

            def h_amixd(u):
                return D0_ + (u + 1) * 24

            def h_tmp1(g):
                return PL_ + (g // 8) * 16 + 2 * (g % 8) + 1

            def h_actw(g):
                return PL_ + (g // 8) * 16 + 2 * (g % 8) + 2

            def h_u2(u):
                return P2_ + u + 1

            unit_meta = []
            NB = len(blocks)

            def emit_u2(t, up):
                cu, biu, boffu, wu = unit_meta[up]
                t.wait_ge(s_pool, h_actw(up * 8 + E - 1))
                aw = actwP[up % 2]
                mm = None
                for ep in range(E):
                    for sl in range(wu // 128):
                        mm = t.matmul(
                            PU2[biu][:, (sl * E + ep) * 16:
                                     (sl * E + ep) * 16 + 16],
                            aw[ep][:, sl * 128:(sl + 1) * 128],
                            a2pb[:, (cu * E + ep) * 16:(cu * E + ep) * 16 + 16],
                            start=(cu == 0 and ep == 0 and sl == 0),
                            stop=(cu == CF - 1 and ep == E - 1
                                  and sl == wu // 128 - 1))
                mm.then_inc(s_pe2, 1)
                led.inc(s_pe2, 1)

            def load_chunk(c, wait_pe):
                wsem = s_wa if c % 2 == 0 else s_wb
                led.inc(wsem, 32)

                def _w(g, c=c, wait_pe=wait_pe, wsem=wsem):
                    if wait_pe is not None:
                        g.wait_ge(s_pe, wait_pe)
                    par = c % 2
                    g.dma_start(out=ws[:, par * 4096:(par + 1) * 4096],
                                 in_=w13[c]).then_inc(wsem, 16)
                    g.dma_start(out=(b13A if par == 0 else b13B),
                                 in_=b13[c]).then_inc(wsem, 16)
                gp(_w)
                w13t[c] = (wsem, led.val(wsem))

            load_chunk(0, None)
            u = -1
            for c in range(CF):
                if c + 1 < CF:
                    load_chunk(c + 1, chunk_pe_done.get(c - 1))
                par = c % 2
                wtile = ws[:, par * 4096:(par + 1) * 4096]
                bt = b13A if par == 0 else b13B
                for bi, (boff, w) in enumerate(blocks):
                    u += 1
                    g0 = u * E
                    G0 = u * 16
                    osem = s_oa if u % 2 == 0 else s_ob
                    unit_meta.append((c, bi, boff, w))

                    def _pe(t, c=c, bi=bi, boff=boff, w=w, u=u,
                            wtile=wtile, bt=bt, g0=g0, G0=G0):
                        t.wait_ge(w13t[c][0], w13t[c][1])
                        if u >= 1:
                            t.wait_ge(s_act, h_bsev(u - 1))
                        for k in range(KD):
                            mm = t.matmul(ph1A[:, 0:w],
                                          wtile[:, k * 128:(k + 1) * 128],
                                          xn_all[:, k * T_ALL + boff:
                                                 k * T_ALL + boff + w],
                                          start=(k == 0), stop=(k == KD - 1))
                        mm.then_inc(s_pe, 1)
                        led.inc(s_pe, 1)
                        for k in range(KD):
                            mm = t.matmul(ph3A[:, 0:w],
                                          wtile[:, (KD + k) * 128:
                                                (KD + k + 1) * 128],
                                          xn_all[:, k * T_ALL + boff:
                                                 k * T_ALL + boff + w],
                                          start=(k == 0), stop=(k == KD - 1))
                        mm.then_inc(s_pe, 1)
                        led.inc(s_pe, 1)
                        for j in range(16):
                            e, half = j // 2, j % 2
                            L = H_LORA[(G0 + j) % 3]
                            if G0 + j >= 3:
                                t.wait_ge(s_dve, h_add(G0 + j - 3))
                            col = e * 128 if half == 0 else 1024 + e * 128
                            roff = (e if half == 0 else E + e) * T_ALL + boff
                            t.matmul(L[:, 0:w], bt[:, col:col + 128],
                                     uN[:, roff:roff + w],
                                     start=True, stop=True).then_inc(s_pe, 1)
                            led.inc(s_pe, 1)
                        if u >= 1:
                            emit_u2(t, u - 1)
                    pe(_pe)
                    assert led.val(s_pe) == h_pe_done(u)

                    def _sc(s, u=u, w=w, g0=g0, G0=G0):
                        s.wait_ge(s_pe, P0_ + u * 18 + 2)
                        if u >= 2:
                            s.wait_ge(s_dve, h_add(G0 - 17))
                        s.activation(bsREG[(u % 2) * 2][:, 0:w], ph1A[:, 0:w],
                                     AF.Identity).then_inc(s_act, 1)
                        led.inc(s_act, 1)
                        s.activation(bsREG[(u % 2) * 2 + 1][:, 0:w],
                                     ph3A[:, 0:w],
                                     AF.Identity).then_inc(s_act, 1)
                        led.inc(s_act, 1)
                        for e in range(E):
                            g = g0 + e
                            s.wait_ge(s_dve, h_add(G0 + 2 * e))
                            if g >= 3:
                                s.wait_ge(s_pool, h_tmp1(g - 3))
                            s.activation(silu_sb[:, (g % 3) * 512:
                                                 (g % 3) * 512 + w],
                                         h1s[:, (g % 3) * 512:(g % 3) * 512 + w],
                                         AF.Silu).then_inc(s_act, 1)
                            led.inc(s_act, 1)
                        s.wait_ge(s_dve, h_amixd(u))
                        if u >= 2:
                            s.wait_ge(o_fin[u - 2][0], o_fin[u - 2][1])
                        s.activation(amx_st[:, (u % 2) * 512:(u % 2) * 512 + w],
                                     amixf[:, (u % 2) * 512:(u % 2) * 512 + w],
                                     AF.Identity).then_inc(s_act, 1)
                        led.inc(s_act, 1)
                    sc(_sc)
                    assert led.val(s_act) == h_amxev(u)

                    def _ve(v, u=u, w=w, g0=g0, G0=G0):
                        aw = actwP[u % 2]
                        for j in range(16):
                            e, half = j // 2, j % 2
                            g = g0 + e
                            G = G0 + j
                            L = H_LORA[G % 3]
                            v.wait_ge(s_pe, h_lora(G))
                            if j == 0:
                                v.wait_ge(s_act, h_bsev(u))
                            if half == 0:
                                if g >= 3:
                                    v.wait_ge(s_act, h_silu(g - 3))
                                dst = h1s[:, (g % 3) * 512:(g % 3) * 512 + w]
                                bs = bsREG[(u % 2) * 2]
                            else:
                                if g >= 3:
                                    v.wait_ge(s_pool, h_actw(g - 3))
                                dst = h3s[:, (g % 3) * 512:(g % 3) * 512 + w]
                                bs = bsREG[(u % 2) * 2 + 1]
                            v.tensor_add(dst, L[:, 0:w],
                                         bs[:, 0:w]).then_inc(s_dve, 1)
                            led.inc(s_dve, 1)
                        am = amixf[:, (u % 2) * 512:(u % 2) * 512 + w]
                        for e in range(E):
                            g = g0 + e
                            v.wait_ge(s_pool, h_actw(g))
                            if e == 0:
                                if u >= 2:
                                    v.wait_ge(s_act, h_amxev(u - 2))
                                v.tensor_copy(am, aw[0][:, 0:w]
                                              ).then_inc(s_dve, 1)
                            else:
                                v.tensor_add(am, am, aw[e][:, 0:w]
                                             ).then_inc(s_dve, 1)
                            led.inc(s_dve, 1)
                    ve(_ve)
                    assert led.val(s_dve) == h_amixd(u)

                    def _gp(g2, u=u, boff=boff, w=w, g0=g0, G0=G0):
                        aw = actwP[u % 2]
                        for e in range(E):
                            g = g0 + e
                            g2.wait_ge(s_act, h_silu(g))
                            if e == 0 and u >= 2:
                                g2.wait_ge(s_pe2, h_u2(u - 2))
                                g2.wait_ge(s_dve, h_amixd(u - 2))
                            g2.tensor_mul(
                                tmp_sb[:, (g % 2) * 512:(g % 2) * 512 + w],
                                silu_sb[:, (g % 3) * 512:(g % 3) * 512 + w],
                                wb_sb[:, e * T_ALL + boff:e * T_ALL + boff + w]
                            ).then_inc(s_pool, 1)
                            led.inc(s_pool, 1)
                            g2.wait_ge(s_dve, h_add(G0 + 2 * e + 1))
                            g2.tensor_mul(
                                aw[e][:, 0:w],
                                tmp_sb[:, (g % 2) * 512:(g % 2) * 512 + w],
                                h3s[:, (g % 3) * 512:(g % 3) * 512 + w]
                            ).then_inc(s_pool, 1)
                            led.inc(s_pool, 1)
                    gp(_gp)
                    assert led.val(s_pool) == h_actw(g0 + 7)

                    def _amxd(s2, c=c, boff=boff, w=w, u=u, osem=osem):
                        s2.wait_ge(s_act, h_amxev(u))
                        s2.dma_start(out=amx[c][:, boff:boff + w],
                                     in_=amx_st[:, (u % 2) * 512:
                                                (u % 2) * 512 + w]
                                     ).then_inc(osem, 16)
                        led.inc(osem, 16)
                    sy(_amxd)
                    o_fin[u] = (osem, led.val(osem))
                chunk_pe_done[c] = led.val(s_pe)

            ulast = u

            def _u2fl(t, ulast=ulast):
                emit_u2(t, ulast)
            pe(_u2fl)
            oa_f, ob_f = led.val(s_oa), led.val(s_ob)

            def _odrain(t, oa_f=oa_f, ob_f=ob_f):
                t.wait_ge(s_oa, oa_f)
                t.wait_ge(s_ob, ob_f)
            pe(_odrain)
            barrier()

        def u2fin(blocks):
            prev_tr = [None]
            gd_prev = [None]
            for bi, (boff, w) in enumerate(blocks):
                nsl = w // 128

                def _sc0(v, bi=bi, nsl=nsl, ptr=prev_tr[0]):
                    if ptr is not None:
                        v.wait_ge(s_pe, ptr)
                    v.tensor_scalar(u2s[:, 0:nsl * E * 16],
                                    PU2[bi][:, 0:nsl * E * 16],
                                    2.0, None, ALU.mult).then_inc(s_dve, 1)
                    led.inc(s_dve, 1)
                ve(_sc0)
                dvt = led.val(s_dve)
                for sl in range(nsl):
                    atp = led.val(s_act)

                    def _tr(t, sl=sl, dvt=dvt, gdp=gd_prev[0], atp=atp):
                        t.wait_ge(s_dve, dvt)
                        t.wait_ge(s_act, atp)
                        if gdp is not None:
                            t.wait_ge(s_gd, gdp)
                        t.matmul(pamix[:, 0:128],
                                 u2s[:, sl * 128:(sl + 1) * 128], id_b[:, :],
                                 start=True, stop=True).then_inc(s_pe, 1)
                        led.inc(s_pe, 1)
                    pe(_tr)
                    pt = led.val(s_pe)

                    def _ev(s, pt=pt):
                        s.wait_ge(s_pe, pt)
                        s.activation(s_bf[:, 0:128], pamix[:, 0:128],
                                     AF.Identity).then_inc(s_act, 1)
                        led.inc(s_act, 1)
                    sc(_ev)
                    at = led.val(s_act)

                    def _rp(g, sl=sl, boff=boff, at=at):
                        g.wait_ge(s_act, at)
                        for e in range(E):
                            g.dma_start(
                                out=u2T[:, e * T_ALL + boff + sl * 128:
                                        e * T_ALL + boff + sl * 128 + 128],
                                in_=s_bf[16 * e:16 * e + 16, 0:128]
                            ).then_inc(s_gd, 16)
                            led.inc(s_gd, 16)
                    gp(_rp)
                    gd_prev[0] = led.val(s_gd)
                prev_tr[0] = led.val(s_pe)
            gdf = led.val(s_gd)

            def _gdrain(t, gdf=gdf):
                t.wait_ge(s_gd, gdf)
            pe(_gdrain)
            barrier()

        def dloop(xsrc_ap, dst_ap, dst_shift, blocks):
            W2SL = [REGD[:, 0:11264], REGC[:, 0:11264]]
            B2DG = [REGD[32:48, 11264:13312], REGC[32:48, 11264:13312]]
            ADG = [REGD[:, 13312:15360], REGD[:, 15360:17408],
                   REGA[:, 10240:12288], REGA[:, 12288:14336]]
            XIO = [REGF[:, 0:1024], REGF[:, 1024:2048]]
            XOUT = [REGF[:, 2048:3072], REGF[:, 3072:4096]]
            PS1 = [ph1A, ph1B]
            PS3 = [ph3A, ph3B]
            ASEM = [s_xa, s_xb, s_xc, s_xd]
            NGR = CF // 4
            h_w2t, h_xt, h_mmlast, h_res, h_st = {}, {}, {}, {}, {}
            mm_all = []
            it = -1
            for bi, (boff, w) in enumerate(blocks):
                for dg in range(KD // 2):
                    it += 1
                    d0 = dg * 2
                    par = it % 2

                    def _wlg(g, d0=d0, par=par, it=it):
                        if it >= 2:
                            g.wait_ge(s_pe, h_mmlast[it - 2])
                        g.dma_start(out=W2SL[par][:, 0:F],
                                    in_=w2r[d0]).then_inc(s_wd, 16)
                        g.dma_start(out=W2SL[par][:, F:2 * F],
                                    in_=w2r[d0 + 1]).then_inc(s_wd, 16)
                        g.dma_start(out=B2DG[par],
                                    in_=b2pi[:, d0 * E * 128:(d0 + 2) * E * 128]
                                    ).then_inc(s_wd, 16)
                        led.inc(s_wd, 48)
                    gp(_wlg)
                    h_w2t[it] = led.val(s_wd)

                    def _wl(s2, d0=d0, boff=boff, w=w, par=par, it=it):
                        if it >= 2:
                            s2.wait_ge(s_dve, h_res[it - 2])
                        s2.dma_start(
                            out=XIO[par].rearrange("p (g t) -> p g t", g=2)
                            [:, :, 0:w],
                            in_=xsrc_ap[d0 * 128:(d0 + 2) * 128, boff:boff + w]
                            .rearrange("(g p) t -> p g t", p=128)
                        ).then_inc(s_wd, 16)
                        led.inc(s_wd, 16)
                    sy(_wl)
                    h_xt[it] = led.val(s_wd)

                    for gidx in range(NGR):
                        ci = gidx * 4
                        gg = it * NGR + gidx
                        slot = gg % 4
                        asem = ASEM[slot]

                        def _al(s2, ci=ci, boff=boff, w=w, asem=asem,
                                slot=slot, gg=gg):
                            if gg >= 4:
                                s2.wait_ge(s_pe, mm_all[gg - 4])
                            s2.dma_start(
                                out=ADG[slot]
                                .rearrange("p (cc t) -> p cc t", cc=4)
                                [:, :, 0:w],
                                in_=amx[ci:ci + 4, :, boff:boff + w]
                                .rearrange("cc p t -> p cc t")
                            ).then_inc(asem, 16)
                            led.inc(asem, 16)
                        sy(_al)
                        xdt = led.val(asem)

                        def _mm(t, ci=ci, gidx=gidx, w=w, xdt=xdt, asem=asem,
                                par=par, it=it, slot=slot):
                            t.wait_ge(asem, xdt)
                            if gidx == 0:
                                t.wait_ge(s_wd, h_w2t[it])
                                if it >= 2:
                                    t.wait_ge(s_dve, h_res[it - 2])
                            for cj in range(4):
                                cc = ci + cj
                                rhs = ADG[slot][:, cj * 512:cj * 512 + w]
                                t.matmul(PS1[par][:, 0:w],
                                         W2SL[par][:, cc * 128:(cc + 1) * 128],
                                         rhs, start=(cc == 0), stop=False)
                                mm2 = t.matmul(
                                    PS3[par][:, 0:w],
                                    W2SL[par][:, F + cc * 128:F + (cc + 1) * 128],
                                    rhs, start=(cc == 0), stop=False)
                            mm2.then_inc(s_pe, 1)
                            mm_all.append(led.inc(s_pe, 1))
                        pe(_mm)

                    def _lora(t, boff=boff, w=w, par=par):
                        for e in range(E):
                            t.matmul(PS1[par][:, 0:w],
                                     B2DG[par][:, e * 128:(e + 1) * 128],
                                     u2T[:, e * T_ALL + boff:e * T_ALL + boff + w],
                                     start=False, stop=(e == E - 1))
                            mm2 = t.matmul(
                                PS3[par][:, 0:w],
                                B2DG[par][:, (E + e) * 128:(E + e + 1) * 128],
                                u2T[:, e * T_ALL + boff:e * T_ALL + boff + w],
                                start=False, stop=(e == E - 1))
                        mm2.then_inc(s_pe, 1)
                        led.inc(s_pe, 1)
                    pe(_lora)
                    h_mmlast[it] = led.val(s_pe)

                    def _res(v, w=w, par=par, it=it):
                        v.wait_ge(s_pe, h_mmlast[it])
                        v.wait_ge(s_wd, h_xt[it])
                        if it >= 2:
                            v.wait_ge(s_od, h_st[it - 2])
                        v.scalar_tensor_tensor(XOUT[par][:, 0:w],
                                               PS1[par][:, 0:w], 0.5,
                                               XIO[par][:, 0:w],
                                               ALU.mult, ALU.add)
                        v.scalar_tensor_tensor(XOUT[par][:, 512:512 + w],
                                               PS3[par][:, 0:w], 0.5,
                                               XIO[par][:, 512:512 + w],
                                               ALU.mult, ALU.add
                                               ).then_inc(s_dve, 1)
                        led.inc(s_dve, 1)
                    ve(_res)
                    h_res[it] = led.val(s_dve)

                    def _st(s2, d0=d0, boff=boff, w=w, par=par, it=it,
                            dst_shift=dst_shift):
                        s2.wait_ge(s_dve, h_res[it])
                        s2.dma_start(
                            out=dst_ap[d0 * 128:(d0 + 2) * 128,
                                       boff - dst_shift:boff - dst_shift + w]
                            .rearrange("(g p) t -> p g t", p=128),
                            in_=XOUT[par].rearrange("p (g t) -> p g t", g=2)
                            [:, :, 0:w]
                        ).then_inc(s_od, 16)
                        led.inc(s_od, 16)
                    sy(_st)
                    h_st[it] = led.val(s_od)
            barrier()

        def qkv_phase():
            wdts = {}
            pe_done = {}
            ev_hist = {}

            def loadw(j, wait_pe):
                wsem = s_wa if j % 2 == 0 else s_wb

                def _l(g, j=j, wait_pe=wait_pe, wsem=wsem):
                    if wait_pe is not None:
                        g.wait_ge(s_pe, wait_pe)
                    g.dma_start(out=wqs[:, (j % 2) * 2048:(j % 2) * 2048 + 2048],
                                 in_=wq[j]).then_inc(wsem, 16)
                    led.inc(wsem, 16)
                gp(_l)
                wdts[j] = (wsem, led.val(wsem))

            loadw(0, None)
            for j in range(48):
                if j + 1 < 48:
                    loadw(j + 1, pe_done.get(j - 1))
                for sbi, (boff, w) in enumerate(BLK1):
                    gidx = j * 3 + sbi
                    ps = ph1A if gidx % 2 == 0 else ph1B
                    atprev = ev_hist.get(gidx - 2, 0)

                    def _mm(t, j=j, boff=boff, w=w, ps=ps, atprev=atprev):
                        t.wait_ge(wdts[j][0], wdts[j][1])
                        t.wait_ge(s_act, atprev)
                        for k in range(KD):
                            mm = t.matmul(ps[:, 0:w],
                                          wqs[:, (j % 2) * 2048 + k * 128:
                                              (j % 2) * 2048 + (k + 1) * 128],
                                          xn_all[:, k * T_ALL + boff:
                                                 k * T_ALL + boff + w],
                                          start=(k == 0), stop=(k == KD - 1))
                            if k == KD - 1:
                                mm.then_inc(s_pe, 1)
                        led.inc(s_pe, 1)
                    pe(_mm)
                    pt = led.val(s_pe)

                    odp = led.val(s_od)
                    if j < 16:
                        def _ev(s, j=j, w=w, pt=pt, ps=ps, odp=odp):
                            s.wait_ge(s_pe, pt)
                            s.wait_ge(s_od, odp)
                            s.activation(qkst[:, 0:w], ps[:, 0:w], AF.Identity,
                                         bias=battn_q[:, j:j + 1],
                                         scale=0.125).then_inc(s_act, 1)
                            led.inc(s_act, 1)
                        sc(_ev)
                        at = led.val(s_act)
                        ev_hist[gidx] = at

                        def _qst(s2, j=j, boff=boff, w=w, at=at):
                            s2.wait_ge(s_act, at)
                            s2.dma_start(out=qd[j * 128:(j + 1) * 128,
                                                boff:boff + w],
                                         in_=qkst[:, 0:w]).then_inc(s_od, 16)
                            led.inc(s_od, 16)
                        sy(_qst)
                    elif j < 32:
                        def _ev(s, j=j, w=w, pt=pt, ps=ps, odp=odp):
                            s.wait_ge(s_pe, pt)
                            s.wait_ge(s_od, odp)
                            s.activation(qkst[:, 0:w], ps[:, 0:w], AF.Identity,
                                         bias=battn_s[:, j:j + 1]
                                         ).then_inc(s_act, 1)
                            led.inc(s_act, 1)
                        sc(_ev)
                        at = led.val(s_act)
                        ev_hist[gidx] = at

                        def _kst(s2, j=j, boff=boff, w=w, at=at):
                            s2.wait_ge(s_act, at)
                            ha = 2 * (j - 16)
                            s2.dma_start(out=kdr[0:64, ha * T_ALL + boff:
                                                ha * T_ALL + boff + w],
                                         in_=qkst[0:64, 0:w]).then_inc(s_od, 16)
                            s2.dma_start(out=kdr[0:64, (ha + 1) * T_ALL + boff:
                                                (ha + 1) * T_ALL + boff + w],
                                         in_=qkst[64:128, 0:w]).then_inc(s_od, 16)
                            led.inc(s_od, 32)
                        sy(_kst)
                    else:
                        def _ev(s, j=j, w=w, pt=pt, ps=ps, odp=odp):
                            s.wait_ge(s_pe, pt)
                            s.wait_ge(s_od, odp)
                            s.activation(qkst[:, 0:w], ps[:, 0:w], AF.Identity,
                                         bias=battn_s[:, j:j + 1]
                                         ).then_inc(s_act, 1)
                            led.inc(s_act, 1)
                        sc(_ev)
                        at0 = led.val(s_act)
                        ev_hist[gidx] = at0
                        for ts in range(w // 128):
                            atprev2 = led.val(s_act)

                            def _tr(t, ts=ts, at0=at0, atprev2=atprev2):
                                t.wait_ge(s_act, at0 if ts == 0 else atprev2)
                                t.matmul(pamix[:, 0:128],
                                         qkst[:, ts * 128:(ts + 1) * 128],
                                         id_b[:, :], start=True,
                                         stop=True).then_inc(s_pe, 1)
                                led.inc(s_pe, 1)
                            pe(_tr)
                            pt2 = led.val(s_pe)

                            odp2 = led.val(s_od)

                            def _e2(s, pt2=pt2, odp2=odp2):
                                s.wait_ge(s_pe, pt2)
                                s.wait_ge(s_od, odp2)
                                s.activation(qkst[:, 512:640], pamix[:, 0:128],
                                             AF.Identity).then_inc(s_act, 1)
                                led.inc(s_act, 1)
                            sc(_e2)
                            at2 = led.val(s_act)

                            def _vst(s2, j=j, boff=boff, ts=ts, at2=at2):
                                s2.wait_ge(s_act, at2)
                                s2.dma_start(
                                    out=vdr[boff + ts * 128:boff + ts * 128 + 128,
                                            (j - 32) * 128:(j - 31) * 128],
                                    in_=qkst[:, 512:640]).then_inc(s_od, 16)
                                led.inc(s_od, 16)
                            sy(_vst)
                pe_done[j] = led.val(s_pe)
            barrier()

        def attn_phase():
            def _qa64(g):
                g.dma_start(out=qaug64[:, :], in_=qaug).then_inc(s_gd, 16)
                led.inc(s_gd, 16)
            gp(_qa64)
            qa64_t = led.val(s_gd)
            odt0 = led.val(s_od)

            def _ms(v, odt0=odt0):
                v.wait_ge(s_od, odt0)
                vr = vring.rearrange("p (s h x) -> p s h x", s=4, h=NH)
                v.memset(vr[:, :, :, 64:65], 1.0).then_inc(s_dve, 1)
                led.inc(s_dve, 1)
            ve(_ms)
            ms_t = led.val(s_dve)
            ring_t, q_t = {}, {}
            ST_PS = [ph1A, ph3A, ph1B, ph3B]
            PV_PS = [pamix, pu2a]
            h_st, h_mask, h_ex, h_pv, h_rc, h_sc, h_tb, h_fin = \
                {}, {}, {}, {}, {}, {}, {}, {}

            def load_ring(cl, wait_pe):
                rsem = [s_xa, s_xb, s_xc, s_xd][cl % 4]

                def _l(s2, cl=cl, wait_pe=wait_pe, rsem=rsem):
                    if wait_pe is not None:
                        s2.wait_ge(s_pe, wait_pe)
                    else:
                        s2.wait_ge(s_od, odt0)
                    slot = cl % 4
                    s2.dma_start(
                        out=kring[:, slot * NH * 128:(slot + 1) * NH * 128]
                        .rearrange("p (h t) -> p h t", h=NH),
                        in_=kdr.rearrange("p (h t) -> p h t", h=NH)
                        [:, :, cl * 128:(cl + 1) * 128]).then_inc(rsem, 16)
                    s2.dma_start(
                        out=vring[:, slot * NH * 65:(slot + 1) * NH * 65]
                        .rearrange("p (h x) -> p h x", h=NH)[:, :, 0:64],
                        in_=vdr[cl * 128:(cl + 1) * 128, :]
                        .rearrange("p (h x) -> p h x", h=NH)).then_inc(rsem, 16)
                sy(_l)
                led.inc(rsem, 32)
                ring_t[cl] = (rsem, led.val(rsem))

            def load_q(qt, wait_pe):
                qsem = s_oa if qt % 2 == 0 else s_ob

                def _l(s2, qt=qt, wait_pe=wait_pe, qsem=qsem):
                    if wait_pe is not None:
                        s2.wait_ge(s_pe, wait_pe)
                    else:
                        s2.wait_ge(s_od, odt0)
                    cl = qt + 2
                    s2.dma_start(
                        out=q_sb[:, (qt % 2) * 4096:(qt % 2) * 4096 + 4096]
                        .rearrange("p (h t) -> p h t", h=NH),
                        in_=qd[:, cl * 128:(cl + 1) * 128]
                        .rearrange("(h p) t -> p h t", p=64)).then_inc(qsem, 16)
                sy(_l)
                led.inc(qsem, 16)
                q_t[qt] = (qsem, led.val(qsem))

            for cl in range(2):
                load_ring(cl, None)
            load_q(0, None)
            NGQ = NQ * NH

            def emit_st(t, gq):
                qt, h = gq // NH, gq % NH
                cl = qt + 2
                stps = ST_PS[gq % 4]
                qh = q_sb[:, (qt % 2) * 4096 + h * 128:
                          (qt % 2) * 4096 + h * 128 + 128]
                for cc in range(max(0, cl - 2), cl + 1):
                    t.wait_ge(ring_t[cc][0], ring_t[cc][1])
                t.wait_ge(q_t[qt][0], q_t[qt][1])
                if gq == 0:
                    t.wait_ge(s_gd, qa64_t)
                if gq >= 4:
                    t.wait_ge(s_act, h_ex[gq - 4])
                for s in range(NSLOT):
                    slot = (cl - 2 + s) % 4
                    kcol = slot * NH * 128 + h * 128
                    t.matmul(stps[:, s * 128:(s + 1) * 128],
                             kring[0:64, kcol:kcol + 128], qh,
                             start=(s == 0), stop=False)
                    mm = t.matmul(stps[:, s * 128:(s + 1) * 128],
                                  kring[64:65, kcol:kcol + 128],
                                  qaug64[0:1, qt * 128:(qt + 1) * 128],
                                  start=False, stop=(s == NSLOT - 1))
                mm.then_inc(s_pe, 1)
                h_st[gq] = led.inc(s_pe, 1)

            def emit_mask(v, gq):
                stps = ST_PS[gq % 4]
                v.wait_ge(s_pe, h_st[gq])
                v.tensor_add(stps[:, 256:384], stps[:, 256:384],
                             trit_s[:, :]).then_inc(s_dve, 1)
                h_mask[gq] = led.inc(s_dve, 1)

            def emit_exp(s, gq):
                qt = gq // NH
                stps = ST_PS[gq % 4]
                s.wait_ge(s_dve, h_mask[gq])
                if gq >= 4:
                    s.wait_ge(s_pe, h_pv[gq - 4])
                for sl in range(NSLOT):
                    ai = s.activation(
                        pt_bf[:, (gq % 4) * 384 + sl * 128:
                              (gq % 4) * 384 + (sl + 1) * 128],
                        stps[:, sl * 128:(sl + 1) * 128], AF.Exp,
                        bias=stb_s[:, qt * NSLOT + sl:qt * NSLOT + sl + 1])
                ai.then_inc(s_act, 1)
                h_ex[gq] = led.inc(s_act, 1)

            def emit_pv(t, gq):
                qt, h = gq // NH, gq % NH
                cl = qt + 2
                pvps = PV_PS[gq % 2]
                t.wait_ge(s_act, h_ex[gq])
                if gq == 0:
                    t.wait_ge(s_dve, ms_t)
                if gq >= 2:
                    t.wait_ge(s_dve, h_sc[gq - 2])
                for s in range(NSLOT):
                    slot = (cl - 2 + s) % 4
                    mm = t.matmul(pvps[0:128, 0:65],
                                  pt_bf[:, (gq % 4) * 384 + s * 128:
                                        (gq % 4) * 384 + (s + 1) * 128],
                                  vring[:, slot * NH * 65 + h * 65:
                                        slot * NH * 65 + (h + 1) * 65],
                                  start=(s == 0), stop=(s == NSLOT - 1))
                mm.then_inc(s_pe, 1)
                h_pv[gq] = led.inc(s_pe, 1)

            def emit_rc(v, gq):
                pvps = PV_PS[gq % 2]
                v.wait_ge(s_pe, h_pv[gq])
                if gq >= 4:
                    v.wait_ge(s_dve, h_sc[gq - 4])
                v.reciprocal(grow[:, 33 + (gq % 4):34 + (gq % 4)],
                             pvps[:, 64:65]).then_inc(s_dve, 1)
                h_rc[gq] = led.inc(s_dve, 1)

            def emit_scale(v, gq):
                pvps = PV_PS[gq % 2]
                v.wait_ge(s_dve, h_rc[gq])
                if gq >= 4:
                    v.wait_ge(s_pe, h_tb[(gq - 4) | 1])
                v.tensor_single_scalar(
                    tm_b[:, (gq % 4) * 64:(gq % 4) * 64 + 64],
                    pvps[:, 0:64],
                    grow[:, 33 + (gq % 4):34 + (gq % 4)],
                    ALU.mult).then_inc(s_dve, 1)
                h_sc[gq] = led.inc(s_dve, 1)

            def emit_tb(t, gq):
                t.wait_ge(s_dve, h_sc[gq])
                if gq >= 2:
                    t.wait_ge(s_act, h_fin[gq - 2])
                pr = (gq // 2) % 2
                t.matmul(pu2c[:, 0:128],
                         tm_b[:, pr * 128:pr * 128 + 128],
                         id_b[:, :], start=True,
                         stop=True).then_inc(s_pe, 1)
                h_tb[gq] = h_tb[gq - 1] = led.inc(s_pe, 1)

            def emit_fin(s, gq):
                qt, h = gq // NH, gq % NH
                s.wait_ge(s_pe, h_tb[gq])
                hc = h // 2
                s.activation(attn_sb[:, hc * T_OWN + qt * 128:
                                     hc * T_OWN + qt * 128 + 128],
                             pu2c[:, 0:128],
                             AF.Identity).then_inc(s_act, 1)
                h_fin[gq] = h_fin[gq - 1] = led.inc(s_act, 1)

            def tail(gq):
                pe(lambda t, gq=gq: emit_pv(t, gq))
                ve(lambda v, gq=gq: emit_rc(v, gq))
                ve(lambda v, gq=gq: emit_scale(v, gq))
                if gq % 2 == 1:
                    pe(lambda t, gq=gq: emit_tb(t, gq))
                    sc(lambda s, gq=gq: emit_fin(s, gq))

            for gq in range(NGQ):
                if gq % NH == 0:
                    qt = gq // NH
                    load_ring(qt + 2,
                              h_pv[(qt - 2) * NH + NH - 1] if qt >= 2 else None)
                    if qt + 1 < NQ:
                        load_q(qt + 1,
                               h_st[(qt - 1) * NH + NH - 1] if qt >= 1 else None)
                pe(lambda t, gq=gq: emit_st(t, gq))
                ve(lambda v, gq=gq: emit_mask(v, gq))
                sc(lambda s, gq=gq: emit_exp(s, gq))
                if gq >= 1:
                    tail(gq - 1)
            tail(NGQ - 1)
            barrier()

        def proj_phase():
            wdts = {}
            pe_done = {}
            pev_hist = {}

            def loadw(dj, wait_pe):
                wsem = s_wa if dj % 2 == 0 else s_wb

                def _l(g, dj=dj, wait_pe=wait_pe, wsem=wsem):
                    if wait_pe is not None:
                        g.wait_ge(s_pe, wait_pe)
                    g.dma_start(out=wqs[:, (dj % 2) * 2048:(dj % 2) * 2048 + 2048],
                                 in_=wp[dj]).then_inc(wsem, 16)
                    led.inc(wsem, 16)
                gp(_l)
                wdts[dj] = (wsem, led.val(wsem))

            loadw(0, None)
            for dj in range(KD):
                if dj + 1 < KD:
                    loadw(dj + 1, pe_done.get(dj - 1))
                for sbi, (boff, w) in enumerate(BLK2):
                    ps = ph1A if (dj * 2 + sbi) % 2 == 0 else ph1B
                    oprev = led.val(s_od)

                    def _xl(s2, dj=dj, boff=boff, w=w, sbi=sbi, oprev=oprev):
                        s2.wait_ge(s_od, oprev)
                        s2.dma_start(out=xpj[:, (sbi % 2) * 512:
                                             (sbi % 2) * 512 + w],
                                     in_=xres[dj * 128:(dj + 1) * 128,
                                              boff:boff + w]).then_inc(s_xd, 16)
                        led.inc(s_xd, 16)
                    sy(_xl)
                    xdt = led.val(s_xd)

                    def _mm(t, dj=dj, boff=boff, w=w, ps=ps,
                            dvp=pev_hist.get(dj * 2 + sbi - 2, 0)):
                        t.wait_ge(wdts[dj][0], wdts[dj][1])
                        t.wait_ge(s_dve, dvp)
                        for k in range(KD):
                            mm = t.matmul(ps[:, 0:w],
                                          wqs[:, (dj % 2) * 2048 + k * 128:
                                              (dj % 2) * 2048 + (k + 1) * 128],
                                          attn_sb[:, k * T_OWN + boff - HALO:
                                                  k * T_OWN + boff - HALO + w],
                                          start=(k == 0), stop=(k == KD - 1))
                            if k == KD - 1:
                                mm.then_inc(s_pe, 1)
                        led.inc(s_pe, 1)
                    pe(_mm)
                    pt = led.val(s_pe)

                    def _ev(v, dj=dj, w=w, sbi=sbi, pt=pt, ps=ps, xdt=xdt):
                        v.wait_ge(s_pe, pt)
                        v.wait_ge(s_xd, xdt)
                        v.scalar_tensor_tensor(
                            xfp[:, 0:w], ps[:, 0:w], bproj_s[:, dj:dj + 1],
                            xpj[:, (sbi % 2) * 512:(sbi % 2) * 512 + w],
                            ALU.add, ALU.add).then_inc(s_dve, 1)
                        led.inc(s_dve, 1)
                    ve(_ev)
                    dvt = led.val(s_dve)
                    pev_hist[dj * 2 + sbi] = dvt

                    def _st(s2, dj=dj, boff=boff, w=w, dvt=dvt):
                        s2.wait_ge(s_dve, dvt)
                        s2.dma_start(out=xres[dj * 128:(dj + 1) * 128,
                                              boff:boff + w],
                                     in_=xfp[:, 0:w]).then_inc(s_od, 16)
                        led.inc(s_od, 16)
                    sy(_st)
                pe_done[dj] = led.val(s_pe)
            odf = led.val(s_od)

            def _oddrain2(t, odf=odf):
                t.wait_ge(s_od, odf)
            pe(_oddrain2)
            barrier()

        ln_phase(x0, BLK1)

        def _castwait(t, gdt=gdt):
            t.wait_ge(s_gd, gdt)
        pe(_castwait)
        barrier()
        gate_phase(list(range(NT)))
        uprod(BLK1)
        floop(BLK1)
        u2fin(BLK1)
        dloop(x0, xres, 0, BLK1)
        ln_phase(xres, BLK1)
        qkv_phase()
        attn_phase()
        proj_phase()
        ln_phase(xres, BLK2)
        gate_phase(list(range(2, NT)))
        uprod(BLK2)
        floop(BLK2)
        u2fin(BLK2)
        dloop(xres, y, HALO, BLK2)

        fin = led.val(s_od)

        def _fin(s2, fin=fin):
            s2.wait_ge(s_od, fin)
        sy(_fin)

        B.tensor(R_pe.replay)
        B.scalar(R_sc.replay)
        B.vector(R_ve.replay)
        B.gpsimd(R_gp.replay)
        B.sync(R_sy.replay)

    return nc


def _prep_weights(i):
    w1, w3, w2 = i["L0_ffn_w1"], i["L0_ffn_w3"], i["L0_ffn_w2"]
    a1, b1 = i["L0_ffn_a1"], i["L0_ffn_b1"]
    a3, b3 = i["L0_ffn_a3"], i["L0_ffn_b3"]
    a2, b2 = i["L0_ffn_a2"], i["L0_ffn_b2"]
    gw = i["L0_ffn_gate_w"]
    W_attn, b_attn = i["L0_attn_W_attn"], i["L0_attn_b_attn"]
    W_proj, b_proj = i["L0_attn_W_proj"], i["L0_attn_b_proj"]

    c = np.ascontiguousarray
    w1r = np.asarray(w1).reshape(CF, 128, KD, 128).transpose(0, 3, 2, 1)
    w3r = np.asarray(w3).reshape(CF, 128, KD, 128).transpose(0, 3, 2, 1)
    w13v = c(np.stack([w1r, w3r], axis=2).reshape(CF, 128, 4096))
    b1r = np.asarray(b1).reshape(E, CF, 128, R).transpose(1, 3, 0, 2)
    b3r = np.asarray(b3).reshape(E, CF, 128, R).transpose(1, 3, 0, 2)
    b13v = c(np.stack([b1r, b3r], axis=2).reshape(CF, 16, 2048))
    a1r = np.asarray(a1).reshape(E, R, KD, 128).transpose(3, 2, 0, 1)
    a3r = np.asarray(a3).reshape(E, R, KD, 128).transpose(3, 2, 0, 1)
    a13v = c(np.stack([a1r, a3r], axis=1).reshape(128, 4096))
    a2v = c(np.asarray(a2).reshape(E, R, CF, 128)
            .transpose(3, 2, 0, 1).reshape(128, CF * 128))
    b2v = c(np.asarray(b2).reshape(E, KD, 128, R)
            .transpose(3, 1, 0, 2).reshape(16, KD * E * 128))
    gwv = c(np.asarray(gw).reshape(E, KD, 128).transpose(2, 1, 0)
            .reshape(128, KD * E))
    wqv = c(np.asarray(W_attn).reshape(KD, 128, 48, 128)
            .transpose(2, 1, 0, 3).reshape(48, 128, 2048))
    wpv = c(np.asarray(W_proj).reshape(KD, 128, KD, 128)
            .transpose(2, 1, 0, 3).reshape(KD, 128, 2048))
    w2v = c(np.asarray(w2).reshape(KD, 128, CF, 128)
            .transpose(0, 3, 2, 1).reshape(KD, 128, F))
    bav = c(np.asarray(b_attn).reshape(48, 128).T)
    bpv = c(np.asarray(b_proj).reshape(KD, 128).T)
    return dict(w13=w13v, b13=b13v, a13=a13v, a2pi=a2v, b2pi=b2v, gwt=gwv,
                wq=wqv, wp=wpv, w2r=w2v, battn=bav, bproj=bpv)


def _prep_meta(core):
    stbv = np.zeros((NQ * NSLOT, 128), np.float32)
    for qt in range(NQ):
        for s in range(NSLOT):
            cl = qt + s
            col = M_AL * (cl * 128 + np.arange(128, dtype=np.float32))
            if core == 0 and cl < 2:
                col = col + NEGM
            stbv[qt * NSLOT + s] = col
    qau = np.zeros((2, NQ * 128), np.float32)
    for qt in range(NQ):
        qau[0, qt * 128:(qt + 1) * 128] = -M_AL * (
            (qt + 2) * 128 + np.arange(128, dtype=np.float32)) - 12.0
    qau[1] = 1.0
    mk = np.zeros((2, T_ALL), np.float32)
    mk[0] = 1.0
    mk[1] = M_AL * np.arange(T_ALL, dtype=np.float32)
    tri_v = np.where(np.arange(128)[None, :] > np.arange(128)[:, None],
                     np.float32(NEGM), np.float32(0.0))
    return dict(stb=np.ascontiguousarray(stbv.T), qaug=qau, mkrow=mk,
                ident=np.eye(128, dtype=np.float32),
                tri=np.ascontiguousarray(tri_v),
                trit=np.ascontiguousarray(tri_v.T),
                onesf=np.ones((1, 128), np.float32))


def kernel(**inputs):
    x = np.asarray(inputs["L0_attn_hidden_states"], np.float32)[0]
    xT = np.ascontiguousarray(x.T)
    wdict = _prep_weights(inputs)
    nc = build_program()
    in_maps = []
    for corei in range(8):
        g0 = corei * T_OWN
        if corei == 0:
            halo = np.zeros((D, HALO), np.float32)
        else:
            halo = xT[:, g0 - HALO:g0]
        x0c = np.ascontiguousarray(
            np.concatenate([halo, xT[:, g0:g0 + T_OWN]], axis=1))
        m = dict(wdict)
        m.update(_prep_meta(corei))
        m["x0"] = x0c
        in_maps.append(m)
    res = run_bass_kernel_spmd(nc, in_maps, core_ids=list(range(8)))
    out = np.empty((8192, D), np.float32)
    for corei in range(8):
        out[corei * T_OWN:(corei + 1) * T_OWN, :] = res.results[corei]["y"].T
    return out.reshape(1, 8192, D)


# version 21
# speedup vs baseline: 1.4244x; 1.0000x over previous
import numpy as np
import concourse.bass as bass
import concourse.mybir as mybir
from concourse.bass_utils import run_bass_kernel_spmd

F32 = mybir.dt.float32
BF16 = mybir.dt.bfloat16
AF = mybir.ActivationFunctionType
ALU = mybir.AluOpType
AX = mybir.AxisListType

T_OWN, HALO = 1024, 256
T_ALL = T_OWN + HALO
D, KD, F, CF, E, R, NH, DH = 2048, 16, 5632, 44, 8, 16, 32, 64
NT, NQ, NSLOT = T_ALL // 128, T_OWN // 128, 3
M_AL = 2.0 ** (-0.25)
BLK1 = [(0, 512), (512, 512), (1024, 256)]
BLK2 = [(256, 512), (768, 512)]
NEGM = -30000.0




class _InstRec:
    __slots__ = ("chain",)

    def __init__(self):
        self.chain = []

    def then_inc(self, sem, val):
        self.chain.append(("then_inc", (sem, val)))
        return self


class _EngRec:
    __slots__ = ("calls",)

    def __init__(self):
        self.calls = []

    def __getattr__(self, name):
        def _call(*a, **k):
            inst = _InstRec()
            self.calls.append((name, a, k, inst))
            return inst
        return _call

    def replay(self, eng):
        for name, a, k, inst in self.calls:
            real = getattr(eng, name)(*a, **k)
            for cn, ca in inst.chain:
                real = getattr(real, cn)(*ca)


class Led:
    def __init__(self):
        self.c = {}

    def inc(self, s, n=1):
        self.c[s.name] = self.c.get(s.name, 0) + n
        return self.c[s.name]

    def val(self, s):
        return self.c.get(s.name, 0)


def build_program():
    nc = bass.Bass("TRN2", target_bir_lowering=False, debug=False)
    led = Led()

    def inp(name, shape):
        return nc.dram_tensor(name, shape, F32, kind="ExternalInput").ap()

    x0 = inp("x0", [D, T_ALL])
    w13 = inp("w13", [CF, 128, 4096])
    b13 = inp("b13", [CF, 16, 2048])
    a13 = inp("a13", [128, 4096])
    a2pi = inp("a2pi", [128, CF * 128])
    b2pi = inp("b2pi", [16, KD * E * 128])
    gwt = inp("gwt", [128, KD * E])
    wq = inp("wq", [48, 128, 2048])
    wp = inp("wp", [KD, 128, 2048])
    w2r = inp("w2r", [KD, 128, F])
    battn = inp("battn", [128, 48])
    bproj = inp("bproj", [128, KD])
    stb = inp("stb", [128, NQ * NSLOT])
    qaug = inp("qaug", [2, NQ * 128])
    mkrow = inp("mkrow", [2, T_ALL])
    ident = inp("ident", [128, 128])
    tri = inp("tri", [128, 128])
    trit = inp("trit", [128, 128])
    onesf = inp("onesf", [1, 128])
    selv = inp("selv", [128, 8])

    y = nc.dram_tensor("y", [D, T_OWN], F32, kind="ExternalOutput").ap()
    ccin = nc.dram_tensor("ccin", [D, HALO], F32)
    ccout = nc.dram_tensor("ccout", [8 * D, HALO], F32, addr_space="Shared")

    xres = nc.dram_tensor("xres", [D, T_ALL], F32).ap()
    wtd = nc.dram_tensor("wtd", [NT, 8, 128], BF16).ap()
    qd = nc.dram_tensor("qd", [D, T_ALL], BF16).ap()
    kdr = nc.dram_tensor("kdr", [66, NH * T_ALL], BF16).ap()
    vdr = nc.dram_tensor("vdr", [T_ALL, NH * DH], BF16).ap()
    amx = nc.dram_tensor("amx", [CF, 128, T_ALL], BF16).ap()

    import contextlib
    with contextlib.ExitStack() as _st:
        xn_all = _st.enter_context(nc.sbuf_tensor("xn_all", [128, KD * T_ALL], BF16))
        REGA = _st.enter_context(nc.sbuf_tensor("REGA", [128, 2 * E * T_ALL], BF16))
        REGB = _st.enter_context(nc.sbuf_tensor("REGB", [128, E * T_ALL], BF16))
        REGC = _st.enter_context(nc.sbuf_tensor("REGC", [128, KD * T_OWN], BF16))
        REGD = _st.enter_context(nc.sbuf_tensor("REGD", [128, 17408], BF16))
        REGF = _st.enter_context(nc.sbuf_tensor("REGF", [128, 4096], F32))
        a13b = _st.enter_context(nc.sbuf_tensor("a13b", [128, 4096], BF16))
        a2pb = _st.enter_context(nc.sbuf_tensor("a2pb", [128, CF * 128], BF16))
        gwb = _st.enter_context(nc.sbuf_tensor("gwb", [128, KD * E], BF16))
        wfull = _st.enter_context(nc.sbuf_tensor("wfull", [128, E * NT], F32))
        battn_s = _st.enter_context(nc.sbuf_tensor("battn_s", [128, 48], F32))
        battn_q = _st.enter_context(nc.sbuf_tensor("battn_q", [128, 16], F32))
        bproj_s = _st.enter_context(nc.sbuf_tensor("bproj_s", [128, KD], F32))
        stb_s = _st.enter_context(nc.sbuf_tensor("stb_s", [128, NQ * NSLOT], F32))
        qaug_b = _st.enter_context(nc.sbuf_tensor("qaug_b", [2, NQ * 128], BF16))
        id_b = _st.enter_context(nc.sbuf_tensor("id_b", [128, 128], BF16))
        tri_s = _st.enter_context(nc.sbuf_tensor("tri_s", [128, 128], F32))
        trit_s = _st.enter_context(nc.sbuf_tensor("trit_s", [128, 128], F32))
        ones_f = _st.enter_context(nc.sbuf_tensor("ones_f", [1, 128], F32))
        selv_s = _st.enter_context(nc.sbuf_tensor("selv_s", [128, 8], F32))
        onescol_b = _st.enter_context(nc.sbuf_tensor("onescol_b", [128, 1], BF16))
        one_b = _st.enter_context(nc.sbuf_tensor("one_b", [1, 1], BF16))
        grow = _st.enter_context(nc.sbuf_tensor("grow", [128, 40], F32))
        wf_b = _st.enter_context(nc.sbuf_tensor("wf_b", [128, 8], BF16))
        nmx_b = _st.enter_context(nc.sbuf_tensor("nmx_b", [128, 1], BF16))
        nmx2 = _st.enter_context(nc.sbuf_tensor("nmx2", [128, 2], BF16))
        eps_s = _st.enter_context(nc.sbuf_tensor("eps_s", [128, 1], F32))
        onesrow_b = _st.enter_context(nc.sbuf_tensor("onesrow_b", [1, 128], BF16))
        qa2 = _st.enter_context(nc.sbuf_tensor("qa2", [65, 128], BF16))
        tm_b = _st.enter_context(nc.sbuf_tensor("tm_b", [128, 4 * 64], BF16))
        ph1A = _st.enter_context(nc.psum_tensor("ph1A", [128, 512], F32))
        ph3A = _st.enter_context(nc.psum_tensor("ph3A", [128, 512], F32))
        ph1B = _st.enter_context(nc.psum_tensor("ph1B", [128, 512], F32))
        ph3B = _st.enter_context(nc.psum_tensor("ph3B", [128, 512], F32))
        pamix = _st.enter_context(nc.psum_tensor("pamix", [128, 512], F32))
        pu2a = _st.enter_context(nc.psum_tensor("pu2a", [128, 512], F32))
        pu2b = _st.enter_context(nc.psum_tensor("pu2b", [128, 512], F32))
        pu2c = _st.enter_context(nc.psum_tensor("pu2c", [128, 512], F32))
        s_ld = _st.enter_context(nc.semaphore("s_ld"))
        s_gd = _st.enter_context(nc.semaphore("s_gd"))
        s_xa = _st.enter_context(nc.semaphore("s_xa"))
        s_xc = _st.enter_context(nc.semaphore("s_xc"))
        s_xb = _st.enter_context(nc.semaphore("s_xb"))
        s_wa = _st.enter_context(nc.semaphore("s_wa"))
        s_wb = _st.enter_context(nc.semaphore("s_wb"))
        s_oa = _st.enter_context(nc.semaphore("s_oa"))
        s_ob = _st.enter_context(nc.semaphore("s_ob"))
        s_wd = _st.enter_context(nc.semaphore("s_wd"))
        s_xd = _st.enter_context(nc.semaphore("s_xd"))
        s_od = _st.enter_context(nc.semaphore("s_od"))
        s_pe = _st.enter_context(nc.semaphore("s_pe"))
        s_pe2 = _st.enter_context(nc.semaphore("s_pe2"))
        s_act = _st.enter_context(nc.semaphore("s_act"))
        s_dve = _st.enter_context(nc.semaphore("s_dve"))
        s_pool = _st.enter_context(nc.semaphore("s_pool"))
        s_cast1 = _st.enter_context(nc.semaphore("s_cast1"))
        s_cast2 = _st.enter_context(nc.semaphore("s_cast2"))
        s_bar = _st.enter_context(nc.semaphore("s_bar"))
        B = _st.enter_context(nc.Block())
        uN = REGA[64:80, :]
        u2T = REGA[32:48, 0:E * T_ALL]
        kring = REGA[0:66, 0:4 * NH * 128]
        wb_sb = REGB[:, :]
        vring = REGB[:, 0:4 * NH * 65]
        s_bf = REGC[:, 0:1536]
        act_bf = REGC[:, 1536:3072]
        actw_bf = REGC[:, 3072:7168]
        amx_st = REGC[:, 7168:8192]
        u2s = REGC[:, 8192:8704]
        silu_sb = REGC[:, 8704:10240]
        tmp_sb = REGC[:, 10240:11264]
        b13A = REGD[64:80, 8192:10240]
        b13B = REGD[64:80, 10240:12288]
        bsREG = [REGC[:, 12800 + i * 512:12800 + (i + 1) * 512] for i in range(4)]
        qkst = REGC[:, 14848:15872]
        xt16 = REGC[:, 0:2048]
        attn_sb = REGC[:, :]
        h1s = REGF[:, 1024:2560]
        h3s = REGF[:, 2560:4096]
        ws = REGD[:, 0:8192]
        w2sl = REGD[:, 0:11264]
        b2dg = REGD[32:48, 11264:13312]
        adg = REGD[:, 13312:17408]
        wtrow = REGD[0:1, 0:E * NT * 128]
        wqs = REGD[:, 0:4096]
        q_sb = REGD[0:64, 4096:12288]
        pt_bf = REGD[:, 12288:12288 + 4 * NSLOT * 128]
        qaug64 = REGD[64:66, 4096:4096 + NQ * 128]
        qr64 = REGD[64:65, 13056:13056 + 256]
        xln = REGF[:, 0:512]
        xln2 = REGF[:, 512:1024]
        xfp = REGF[:, 1024:1536]
        rows = REGF[0:1, 1536:3584]
        xio = REGF[:, 0:2048]
        xout = REGF[:, 2048:4096]
        xpj = REGF[:, 1536:2560]
        amixf = REGF[:, 0:1024]

        PU2 = [pu2a, pu2b, pu2c]
        R_pe, R_sc, R_ve, R_gp, R_sy = (_EngRec() for _ in range(5))
        pe = lambda f: f(R_pe)
        sc = lambda f: f(R_sc)
        ve = lambda f: f(R_ve)
        gp = lambda f: f(R_gp)
        sy = lambda f: f(R_sy)
        nbar = [0]

        def barrier():
            nbar[0] += 1
            tgt = 5 * nbar[0]
            for em in (pe, sc, ve, gp, sy):
                def _f(e, tgt=tgt):
                    e.drain().then_inc(s_bar, 1)
                    e.wait_ge(s_bar, tgt)
                em(_f)
            led.c[s_bar.name] = tgt

        def _p0g(g):
            for t_in, t_out in ((a13, a13b[:, :]), (a2pi, a2pb[:, :]),
                                (gwt, gwb[:, :]), (ident, id_b[:, :]),
                                (qaug, qaug_b[:, :])):
                g.dma_start(out=t_out, in_=t_in).then_inc(s_gd, 16)
                led.inc(s_gd, 16)
            for h in range(NH):
                g.dma_start(out=kdr[64:66, h * T_ALL:(h + 1) * T_ALL],
                            in_=mkrow).then_inc(s_gd, 16)
                led.inc(s_gd, 16)
        gp(_p0g)

        def _p0s(s):
            for t_in, t_out in ((battn, battn_s[:, :]), (bproj, bproj_s[:, :]),
                                (stb, stb_s[:, :]), (tri, tri_s[:, :]),
                                (trit, trit_s[:, :]), (onesf, ones_f[:, :]),
                                (selv, selv_s[:, :])):
                s.dma_start(out=t_out, in_=t_in).then_inc(s_ld, 16)
                led.inc(s_ld, 16)
        sy(_p0s)
        ldt = led.val(s_ld)
        gdt = led.val(s_gd)

        def _p0v(v, ldt=ldt, gdt=gdt):
            v.wait_ge(s_ld, ldt)
            v.memset(onescol_b[:, :], 1.0)
            v.memset(one_b[:, :], 1.0)
            v.memset(eps_s[:, :], 1e-5)
            v.memset(onesrow_b[:, :], 1.0)
        ve(_p0v)

        def _p0sc(s, ldt=ldt, gdt=gdt):
            s.wait_ge(s_ld, ldt)
            s.activation(battn_q[:, :], battn_s[:, 0:16], AF.Identity, scale=0.125)
        sc(_p0sc)
        barrier()

        def ln_phase(src_ap, blocks):
            for (boff, w) in blocks:
                dve0 = led.val(s_dve)
                pl0 = led.val(s_pool)
                ac0 = led.val(s_act)
                pe0 = led.val(s_pe)
                for k in range(KD):
                    sem = s_xa if k % 2 == 0 else s_xb

                    def _ld(s2, k=k, boff=boff, w=w, dve0=dve0, pl0=pl0,
                            ac0=ac0, sem=sem):
                        if k >= 2:
                            s2.wait_ge(s_pool, pl0 + k - 1)
                            s2.wait_ge(s_act, ac0 + k - 1)
                        else:
                            s2.wait_ge(s_dve, dve0)
                        dst = xln if k % 2 == 0 else xln2
                        s2.dma_start(out=dst[:, 0:w],
                                     in_=src_ap[k * 128:(k + 1) * 128,
                                                boff:boff + w]).then_inc(sem, 16)
                        led.inc(sem, 16)
                    sy(_ld)
                    xdt = led.val(sem)

                    def _cpy(g2, k=k, w=w, xdt=xdt, sem=sem, pe0=pe0):
                        g2.wait_ge(sem, xdt)
                        if k >= 2:
                            g2.wait_ge(s_pe, pe0 + k - 1)
                        src = xln if k % 2 == 0 else xln2
                        xb = xt16[:, (k % 2) * 1024:(k % 2) * 1024 + w]
                        g2.tensor_copy(xb, src[:, 0:w]).then_inc(s_pool, 1)
                        led.inc(s_pool, 1)
                    gp(_cpy)

                    def _sq(s, k=k, w=w, xdt=xdt, sem=sem, pe0=pe0):
                        s.wait_ge(sem, xdt)
                        if k >= 2:
                            s.wait_ge(s_pe, pe0 + k - 1)
                        src = xln if k % 2 == 0 else xln2
                        xq = xt16[:, (k % 2) * 1024 + 512:(k % 2) * 1024 + 512 + w]
                        s.activation(xq, src[:, 0:w],
                                     AF.Square).then_inc(s_act, 1)
                        led.inc(s_act, 1)
                    sc(_sq)
                    plt = led.val(s_pool)
                    act_t = led.val(s_act)

                    def _mm(t, k=k, w=w, plt=plt, act_t=act_t):
                        t.wait_ge(s_pool, plt)
                        t.wait_ge(s_act, act_t)
                        xb = xt16[:, (k % 2) * 1024:(k % 2) * 1024 + w]
                        xq = xt16[:, (k % 2) * 1024 + 512:(k % 2) * 1024 + 512 + w]
                        t.matmul(pamix[0:1, 0:w], onescol_b[:, :], xb,
                                 start=(k == 0), stop=(k == KD - 1))
                        t.matmul(ph1B[0:1, 0:w], onescol_b[:, :], xq,
                                 start=(k == 0),
                                 stop=(k == KD - 1)).then_inc(s_pe, 1)
                        led.inc(s_pe, 1)
                    pe(_mm)
                pet = led.val(s_pe)

                def _st1(s, w=w, pet=pet):
                    s.wait_ge(s_pe, pet)
                    s.activation(rows[:, 0:w], pamix[0:1, 0:w], AF.Identity,
                                 scale=1.0 / D)
                    s.activation(rows[:, 512:512 + w], ph1B[0:1, 0:w],
                                 AF.Identity, scale=1.0 / D).then_inc(s_act, 1)
                    led.inc(s_act, 1)
                sc(_st1)
                at = led.val(s_act)

                def _st2(v, w=w, at=at):
                    v.wait_ge(s_act, at)
                    v.tensor_mul(rows[:, 1024:1024 + w], rows[:, 0:w],
                                 rows[:, 0:w]).then_inc(s_dve, 1)
                    v.wait_ge(s_dve, led.inc(s_dve, 1))
                    v.tensor_sub(rows[:, 512:512 + w], rows[:, 512:512 + w],
                                 rows[:, 1024:1024 + w]).then_inc(s_dve, 1)
                    led.inc(s_dve, 1)
                ve(_st2)
                dvt2 = led.val(s_dve)

                def _st3(s, w=w, dvt2=dvt2):
                    s.wait_ge(s_dve, dvt2)
                    s.activation(rows[:, 512:512 + w], rows[:, 512:512 + w],
                                 AF.Ln, bias=eps_s[0:1, 0:1]).then_inc(s_act, 1)
                    s.wait_ge(s_act, led.inc(s_act, 1))
                    s.activation(rows[:, 512:512 + w], rows[:, 512:512 + w],
                                 AF.Exp, scale=-0.5).then_inc(s_act, 1)
                    led.inc(s_act, 1)
                sc(_st3)
                at2 = led.val(s_act)

                def _st4(v, w=w, at2=at2):
                    v.wait_ge(s_act, at2)
                    v.tensor_mul(rows[:, 1536:1536 + w], rows[:, 0:w],
                                 rows[:, 512:512 + w]).then_inc(s_dve, 1)
                    led.inc(s_dve, 1)
                ve(_st4)
                dvt3 = led.val(s_dve)

                def _bc(t, w=w, dvt3=dvt3):
                    t.wait_ge(s_dve, dvt3)
                    t.matmul(ph1A[:, 0:w], ones_f[:, :], rows[:, 512:512 + w],
                             start=True, stop=True)
                    t.matmul(ph3A[:, 0:w], ones_f[:, :], rows[:, 1536:1536 + w],
                             start=True, stop=True).then_inc(s_pe, 1)
                    led.inc(s_pe, 1)
                pe(_bc)
                pet2 = led.val(s_pe)
                dve1 = led.val(s_dve)
                plE = led.val(s_pool)
                acE = led.val(s_act)
                for k in range(KD):
                    sem = s_xa if k % 2 == 0 else s_xb

                    def _ld2(s2, k=k, boff=boff, w=w, dve1=dve1, sem=sem,
                             plE=plE, acE=acE):
                        if k < 2:
                            s2.wait_ge(s_pool, plE)
                            s2.wait_ge(s_act, acE)
                        s2.wait_ge(s_dve, dve1 + 2 * (k - 1) if k >= 2 else dve1)
                        dst = xln if k % 2 == 0 else xln2
                        s2.dma_start(out=dst[:, 0:w],
                                     in_=src_ap[k * 128:(k + 1) * 128,
                                                boff:boff + w]).then_inc(sem, 16)
                        led.inc(sem, 16)
                    sy(_ld2)
                    xdt2 = led.val(sem)

                    dvp = led.val(s_dve)

                    def _nm(v, k=k, boff=boff, w=w, xdt2=xdt2, pet2=pet2, sem=sem,
                            dvp=dvp):
                        v.wait_ge(sem, xdt2)
                        v.wait_ge(s_dve, dvp)
                        if k == 0:
                            v.wait_ge(s_pe, pet2)
                        src = xln if k % 2 == 0 else xln2
                        v.tensor_mul(xfp[:, 0:w], src[:, 0:w],
                                     ph1A[:, 0:w]).then_inc(s_dve, 1)
                        v.wait_ge(s_dve, led.inc(s_dve, 1))
                        v.tensor_sub(xn_all[:, k * T_ALL + boff:
                                            k * T_ALL + boff + w],
                                     xfp[:, 0:w], ph3A[:, 0:w]).then_inc(s_dve, 1)
                        led.inc(s_dve, 1)
                    ve(_nm)
            barrier()

        def gate_phase(tiles):
            h_t2, h_t3, h_t6 = {}, {}, {}
            wt8 = REGC[0:8, 0:256]
            for gi in tiles:
                def _gmm(t, gi=gi):
                    for k in range(KD):
                        mm = t.matmul(pamix[:, 0:8],
                                      xn_all[:, k * T_ALL + gi * 128:
                                             k * T_ALL + gi * 128 + 128],
                                      gwb[:, k * 8:(k + 1) * 8],
                                      start=(k == 0), stop=(k == KD - 1))
                    mm.then_inc(s_pe, 1)
                    led.inc(s_pe, 1)
                pe(_gmm)
                pt = led.val(s_pe)

                def _g1(v, pt=pt):
                    v.wait_ge(s_pe, pt)
                    v.tensor_reduce(grow[:, 0:1], pamix[:, 0:8], AX.X,
                                    ALU.max).then_inc(s_dve, 1)
                    v.tensor_reduce(grow[:, 1:2], pamix[:, 0:8], AX.X, ALU.max,
                                    negate=True)
                    v.wait_ge(s_dve, led.inc(s_dve, 1))
                    v.tensor_single_scalar(grow[:, 8:16], pamix[:, 0:8],
                                           grow[:, 0:1], ALU.is_ge).then_inc(s_dve, 1)
                    v.wait_ge(s_dve, led.inc(s_dve, 1))
                    v.scalar_tensor_tensor(grow[:, 16:24], grow[:, 8:16], -1e30,
                                           pamix[:, 0:8], ALU.mult,
                                           ALU.add).then_inc(s_dve, 1)
                    v.wait_ge(s_dve, led.inc(s_dve, 1))
                    v.tensor_reduce(grow[:, 2:3], grow[:, 16:24], AX.X,
                                    ALU.max).then_inc(s_dve, 1)
                    led.inc(s_dve, 1)
                ve(_g1)
                dt = led.val(s_dve)

                def _g2(s, dt=dt):
                    s.wait_ge(s_dve, dt)
                    s.activation(grow[:, 24:32], pamix[:, 0:8], AF.Exp,
                                 bias=grow[:, 1:2])
                    s.activation(grow[:, 3:4], grow[:, 2:3], AF.Exp,
                                 bias=grow[:, 1:2]).then_inc(s_act, 1)
                    led.inc(s_act, 1)
                sc(_g2)
                at = led.val(s_act)

                def _g3(v, gi=gi, at=at):
                    v.wait_ge(s_act, at)
                    v.tensor_single_scalar(grow[:, 4:5], grow[:, 3:4], 1.0,
                                           ALU.add).then_inc(s_dve, 1)
                    v.wait_ge(s_dve, led.inc(s_dve, 1))
                    v.reciprocal(grow[:, 5:6], grow[:, 4:5])
                    v.tensor_single_scalar(grow[:, 16:24], pamix[:, 0:8],
                                           grow[:, 2:3], ALU.is_ge).then_inc(s_dve, 1)
                    v.wait_ge(s_dve, led.inc(s_dve, 1))
                    v.scalar_tensor_tensor(wfull[:, gi * 8:gi * 8 + 8],
                                           grow[:, 24:32], grow[:, 5:6],
                                           grow[:, 16:24], ALU.mult,
                                           ALU.mult).then_inc(s_dve, 1)
                    v.wait_ge(s_dve, led.inc(s_dve, 1))
                    v.tensor_copy(wf_b[:, :],
                                  wfull[:, gi * 8:gi * 8 + 8]).then_inc(s_dve, 1)
                    led.inc(s_dve, 1)
                ve(_g3)
                dt2 = led.val(s_dve)
                gidx = tiles.index(gi)

                def _t1(t, dt2=dt2, gidx=gidx):
                    t.wait_ge(s_dve, dt2)
                    if gidx >= 1:
                        t.wait_ge(s_act, h_t2[gidx - 1])
                    t.matmul(ph1B[0:8, 0:128], wf_b[:, 0:8], id_b[:, :],
                             start=True, stop=True).then_inc(s_pe, 1)
                    led.inc(s_pe, 1)
                pe(_t1)
                pt2 = led.val(s_pe)

                def _t2(s, pt2=pt2, gidx=gidx):
                    s.wait_ge(s_pe, pt2)
                    if gidx >= 2:
                        s.wait_ge(s_od, h_t3[gidx - 2])
                    s.activation(wt8[0:8, (gidx % 2) * 128:(gidx % 2) * 128 + 128],
                                 ph1B[0:8, 0:128],
                                 AF.Identity).then_inc(s_act, 1)
                    h_t2[gidx] = led.inc(s_act, 1)
                sc(_t2)

                def _t3(s2, gi=gi, gidx=gidx):
                    s2.wait_ge(s_act, h_t2[gidx])
                    s2.dma_start(out=wtd[gi],
                                 in_=wt8[0:8, (gidx % 2) * 128:
                                         (gidx % 2) * 128 + 128]
                                 ).then_inc(s_od, 16)
                    s2.wait_ge(s_od, led.inc(s_od, 16))
                    s2.dma_start(
                        out=wtrow[:, gi * 1024:(gi + 1) * 1024],
                        in_=wtd.rearrange("g e t -> g (e t)")[gi:gi + 1, :]
                    ).then_inc(s_od, 16)
                    h_t3[gidx] = led.inc(s_od, 16)
                sy(_t3)

                def _t5(t, gi=gi, gidx=gidx):
                    t.wait_ge(s_od, h_t3[gidx])
                    if gidx >= 1:
                        t.wait_ge(s_act, h_t6[gidx - 1])
                    for e in range(E):
                        ps = ph3A if e < 4 else ph3B
                        mm = t.matmul(ps[:, (e % 4) * 128:(e % 4) * 128 + 128],
                                      onesrow_b[:, :],
                                      wtrow[:, gi * 1024 + e * 128:
                                            gi * 1024 + (e + 1) * 128],
                                      start=True, stop=True)
                    mm.then_inc(s_pe, 1)
                    led.inc(s_pe, 1)
                pe(_t5)
                pt5 = led.val(s_pe)

                def _t6(s, gi=gi, pt5=pt5, gidx=gidx):
                    s.wait_ge(s_pe, pt5)
                    for e in range(E):
                        ps = ph3A if e < 4 else ph3B
                        s.activation(wb_sb[:, e * T_ALL + gi * 128:
                                           e * T_ALL + gi * 128 + 128],
                                     ps[:, (e % 4) * 128:(e % 4) * 128 + 128],
                                     AF.Identity).then_inc(s_act, 1)
                        led.inc(s_act, 1)
                    h_t6[gidx] = led.val(s_act)
                sc(_t6)
            barrier()

        def uprod(blocks):
            prev_at = [led.val(s_act)]
            prev_xd = [led.val(s_gd)]
            for p in range(2):
                for (boff, w) in blocks:
                    def _u1(t, p=p, boff=boff, w=w, pa=prev_at[0]):
                        t.wait_ge(s_act, pa)
                        for k in range(KD):
                            mm = t.matmul(ph1A[:, 0:w],
                                          a13b[:, (p * KD + k) * 128:
                                               (p * KD + k + 1) * 128],
                                          xn_all[:, k * T_ALL + boff:
                                                 k * T_ALL + boff + w],
                                          start=(k == 0), stop=(k == KD - 1))
                            if k == KD - 1:
                                mm.then_inc(s_pe, 1)
                        led.inc(s_pe, 1)
                    pe(_u1)
                    pt = led.val(s_pe)

                    def _u2(s, w=w, pt=pt, px=prev_xd[0]):
                        s.wait_ge(s_pe, pt)
                        s.wait_ge(s_gd, px)
                        s.activation(u2s[:, 0:w], ph1A[:, 0:w], AF.Identity,
                                     scale=2.0).then_inc(s_act, 1)
                        led.inc(s_act, 1)
                    sc(_u2)
                    at = led.val(s_act)
                    prev_at[0] = at

                    def _u3(g, p=p, boff=boff, w=w, at=at):
                        g.wait_ge(s_act, at)
                        for e in range(E):
                            col = (p * E + e) * T_ALL + boff
                            g.dma_start(out=uN[:, col:col + w],
                                        in_=u2s[16 * e:16 * e + 16, 0:w]
                                        ).then_inc(s_gd, 16)
                            led.inc(s_gd, 16)
                    gp(_u3)
                    prev_xd[0] = led.val(s_gd)
            gfin = led.val(s_gd)

            def _udrain(t, gfin=gfin):
                t.wait_ge(s_gd, gfin)
            pe(_udrain)
            barrier()

        def floop(blocks):
            H_LORA = [ph1B, ph3B, pamix]
            w13t, chunk_pe_done, o_fin = {}, {}, {}
            P0_ = led.val(s_pe)
            A0_ = led.val(s_act)
            D0_ = led.val(s_dve)
            PL_ = led.val(s_pool)
            P2_ = led.val(s_pe2)
            actwP = [[actw_bf[:, e * 512:(e + 1) * 512] for e in range(E)],
                     [REGC[:, e * 512:(e + 1) * 512] for e in range(6)]
                     + [REGC[:, 11264 + e * 512:11264 + (e + 1) * 512]
                        for e in range(2)]]

            def h_lora(G):
                return P0_ + (G // 16) * 18 + 2 + (G % 16) + 1

            def h_pe_done(u):
                return P0_ + (u + 1) * 18

            def h_bsev(u):
                return A0_ + u * 11 + 2

            def h_silu(g):
                return A0_ + (g // 8) * 11 + 2 + (g % 8) + 1

            def h_amxev(u):
                return A0_ + (u + 1) * 11

            def h_add(G):
                return D0_ + (G // 16) * 24 + (G % 16) + 1

            def h_amixd(u):
                return D0_ + (u + 1) * 24

            def h_tmp1(g):
                return PL_ + (g // 8) * 16 + 2 * (g % 8) + 1

            def h_actw(g):
                return PL_ + (g // 8) * 16 + 2 * (g % 8) + 2

            def h_u2(u):
                return P2_ + u + 1

            unit_meta = []
            NB = len(blocks)

            def emit_u2(t, up):
                cu, biu, boffu, wu = unit_meta[up]
                t.wait_ge(s_pool, h_actw(up * 8 + E - 1))
                aw = actwP[up % 2]
                mm = None
                for ep in range(E):
                    for sl in range(wu // 128):
                        mm = t.matmul(
                            PU2[biu][:, (sl * E + ep) * 16:
                                     (sl * E + ep) * 16 + 16],
                            aw[ep][:, sl * 128:(sl + 1) * 128],
                            a2pb[:, (cu * E + ep) * 16:(cu * E + ep) * 16 + 16],
                            start=(cu == 0 and ep == 0 and sl == 0),
                            stop=(cu == CF - 1 and ep == E - 1
                                  and sl == wu // 128 - 1))
                mm.then_inc(s_pe2, 1)
                led.inc(s_pe2, 1)

            def load_chunk(c, wait_pe):
                wsem = s_wa if c % 2 == 0 else s_wb
                led.inc(wsem, 32)

                def _w(g, c=c, wait_pe=wait_pe, wsem=wsem):
                    if wait_pe is not None:
                        g.wait_ge(s_pe, wait_pe)
                    par = c % 2
                    g.dma_start(out=ws[:, par * 4096:(par + 1) * 4096],
                                 in_=w13[c]).then_inc(wsem, 16)
                    g.dma_start(out=(b13A if par == 0 else b13B),
                                 in_=b13[c]).then_inc(wsem, 16)
                gp(_w)
                w13t[c] = (wsem, led.val(wsem))

            load_chunk(0, None)
            u = -1
            for c in range(CF):
                if c + 1 < CF:
                    load_chunk(c + 1, chunk_pe_done.get(c - 1))
                par = c % 2
                wtile = ws[:, par * 4096:(par + 1) * 4096]
                bt = b13A if par == 0 else b13B
                for bi, (boff, w) in enumerate(blocks):
                    u += 1
                    g0 = u * E
                    G0 = u * 16
                    osem = s_oa if u % 2 == 0 else s_ob
                    unit_meta.append((c, bi, boff, w))

                    def _pe(t, c=c, bi=bi, boff=boff, w=w, u=u,
                            wtile=wtile, bt=bt, g0=g0, G0=G0):
                        t.wait_ge(w13t[c][0], w13t[c][1])
                        if u >= 1:
                            t.wait_ge(s_act, h_bsev(u - 1))
                        for k in range(KD):
                            mm = t.matmul(ph1A[:, 0:w],
                                          wtile[:, k * 128:(k + 1) * 128],
                                          xn_all[:, k * T_ALL + boff:
                                                 k * T_ALL + boff + w],
                                          start=(k == 0), stop=(k == KD - 1))
                        mm.then_inc(s_pe, 1)
                        led.inc(s_pe, 1)
                        for k in range(KD):
                            mm = t.matmul(ph3A[:, 0:w],
                                          wtile[:, (KD + k) * 128:
                                                (KD + k + 1) * 128],
                                          xn_all[:, k * T_ALL + boff:
                                                 k * T_ALL + boff + w],
                                          start=(k == 0), stop=(k == KD - 1))
                        mm.then_inc(s_pe, 1)
                        led.inc(s_pe, 1)
                        for j in range(16):
                            e, half = j // 2, j % 2
                            L = H_LORA[(G0 + j) % 3]
                            if G0 + j >= 3:
                                t.wait_ge(s_dve, h_add(G0 + j - 3))
                            col = e * 128 if half == 0 else 1024 + e * 128
                            roff = (e if half == 0 else E + e) * T_ALL + boff
                            t.matmul(L[:, 0:w], bt[:, col:col + 128],
                                     uN[:, roff:roff + w],
                                     start=True, stop=True).then_inc(s_pe, 1)
                            led.inc(s_pe, 1)
                        if u >= 1:
                            emit_u2(t, u - 1)
                    pe(_pe)
                    assert led.val(s_pe) == h_pe_done(u)

                    def _sc(s, u=u, w=w, g0=g0, G0=G0):
                        s.wait_ge(s_pe, P0_ + u * 18 + 2)
                        if u >= 2:
                            s.wait_ge(s_dve, h_add(G0 - 17))
                        s.activation(bsREG[(u % 2) * 2][:, 0:w], ph1A[:, 0:w],
                                     AF.Identity).then_inc(s_act, 1)
                        led.inc(s_act, 1)
                        s.activation(bsREG[(u % 2) * 2 + 1][:, 0:w],
                                     ph3A[:, 0:w],
                                     AF.Identity).then_inc(s_act, 1)
                        led.inc(s_act, 1)
                        for e in range(E):
                            g = g0 + e
                            s.wait_ge(s_dve, h_add(G0 + 2 * e))
                            if g >= 3:
                                s.wait_ge(s_pool, h_tmp1(g - 3))
                            s.activation(silu_sb[:, (g % 3) * 512:
                                                 (g % 3) * 512 + w],
                                         h1s[:, (g % 3) * 512:(g % 3) * 512 + w],
                                         AF.Silu).then_inc(s_act, 1)
                            led.inc(s_act, 1)
                        s.wait_ge(s_dve, h_amixd(u))
                        if u >= 2:
                            s.wait_ge(o_fin[u - 2][0], o_fin[u - 2][1])
                        s.activation(amx_st[:, (u % 2) * 512:(u % 2) * 512 + w],
                                     amixf[:, (u % 2) * 512:(u % 2) * 512 + w],
                                     AF.Identity).then_inc(s_act, 1)
                        led.inc(s_act, 1)
                    sc(_sc)
                    assert led.val(s_act) == h_amxev(u)

                    def _ve(v, u=u, w=w, g0=g0, G0=G0):
                        aw = actwP[u % 2]
                        for j in range(16):
                            e, half = j // 2, j % 2
                            g = g0 + e
                            G = G0 + j
                            L = H_LORA[G % 3]
                            v.wait_ge(s_pe, h_lora(G))
                            if j == 0:
                                v.wait_ge(s_act, h_bsev(u))
                            if half == 0:
                                if g >= 3:
                                    v.wait_ge(s_act, h_silu(g - 3))
                                dst = h1s[:, (g % 3) * 512:(g % 3) * 512 + w]
                                bs = bsREG[(u % 2) * 2]
                            else:
                                if g >= 3:
                                    v.wait_ge(s_pool, h_actw(g - 3))
                                dst = h3s[:, (g % 3) * 512:(g % 3) * 512 + w]
                                bs = bsREG[(u % 2) * 2 + 1]
                            v.tensor_add(dst, L[:, 0:w],
                                         bs[:, 0:w]).then_inc(s_dve, 1)
                            led.inc(s_dve, 1)
                        am = amixf[:, (u % 2) * 512:(u % 2) * 512 + w]
                        for e in range(E):
                            g = g0 + e
                            v.wait_ge(s_pool, h_actw(g))
                            if e == 0:
                                if u >= 2:
                                    v.wait_ge(s_act, h_amxev(u - 2))
                                v.tensor_copy(am, aw[0][:, 0:w]
                                              ).then_inc(s_dve, 1)
                            else:
                                v.tensor_add(am, am, aw[e][:, 0:w]
                                             ).then_inc(s_dve, 1)
                            led.inc(s_dve, 1)
                    ve(_ve)
                    assert led.val(s_dve) == h_amixd(u)

                    def _gp(g2, u=u, boff=boff, w=w, g0=g0, G0=G0):
                        aw = actwP[u % 2]
                        for e in range(E):
                            g = g0 + e
                            g2.wait_ge(s_act, h_silu(g))
                            if e == 0 and u >= 2:
                                g2.wait_ge(s_pe2, h_u2(u - 2))
                                g2.wait_ge(s_dve, h_amixd(u - 2))
                            g2.tensor_mul(
                                tmp_sb[:, (g % 2) * 512:(g % 2) * 512 + w],
                                silu_sb[:, (g % 3) * 512:(g % 3) * 512 + w],
                                wb_sb[:, e * T_ALL + boff:e * T_ALL + boff + w]
                            ).then_inc(s_pool, 1)
                            led.inc(s_pool, 1)
                            g2.wait_ge(s_dve, h_add(G0 + 2 * e + 1))
                            g2.tensor_mul(
                                aw[e][:, 0:w],
                                tmp_sb[:, (g % 2) * 512:(g % 2) * 512 + w],
                                h3s[:, (g % 3) * 512:(g % 3) * 512 + w]
                            ).then_inc(s_pool, 1)
                            led.inc(s_pool, 1)
                    gp(_gp)
                    assert led.val(s_pool) == h_actw(g0 + 7)

                    def _amxd(s2, c=c, boff=boff, w=w, u=u, osem=osem):
                        s2.wait_ge(s_act, h_amxev(u))
                        s2.dma_start(out=amx[c][:, boff:boff + w],
                                     in_=amx_st[:, (u % 2) * 512:
                                                (u % 2) * 512 + w]
                                     ).then_inc(osem, 16)
                        led.inc(osem, 16)
                    sy(_amxd)
                    o_fin[u] = (osem, led.val(osem))
                chunk_pe_done[c] = led.val(s_pe)

            ulast = u

            def _u2fl(t, ulast=ulast):
                emit_u2(t, ulast)
            pe(_u2fl)
            oa_f, ob_f = led.val(s_oa), led.val(s_ob)

            def _odrain(t, oa_f=oa_f, ob_f=ob_f):
                t.wait_ge(s_oa, oa_f)
                t.wait_ge(s_ob, ob_f)
            pe(_odrain)
            barrier()

        def u2fin(blocks):
            prev_tr = [None]
            gd_prev = [None]
            for bi, (boff, w) in enumerate(blocks):
                nsl = w // 128

                def _sc0(v, bi=bi, nsl=nsl, ptr=prev_tr[0]):
                    if ptr is not None:
                        v.wait_ge(s_pe, ptr)
                    v.tensor_scalar(u2s[:, 0:nsl * E * 16],
                                    PU2[bi][:, 0:nsl * E * 16],
                                    2.0, None, ALU.mult).then_inc(s_dve, 1)
                    led.inc(s_dve, 1)
                ve(_sc0)
                dvt = led.val(s_dve)
                for sl in range(nsl):
                    atp = led.val(s_act)

                    def _tr(t, sl=sl, dvt=dvt, gdp=gd_prev[0], atp=atp):
                        t.wait_ge(s_dve, dvt)
                        t.wait_ge(s_act, atp)
                        if gdp is not None:
                            t.wait_ge(s_gd, gdp)
                        t.matmul(pamix[:, 0:128],
                                 u2s[:, sl * 128:(sl + 1) * 128], id_b[:, :],
                                 start=True, stop=True).then_inc(s_pe, 1)
                        led.inc(s_pe, 1)
                    pe(_tr)
                    pt = led.val(s_pe)

                    def _ev(s, pt=pt):
                        s.wait_ge(s_pe, pt)
                        s.activation(s_bf[:, 0:128], pamix[:, 0:128],
                                     AF.Identity).then_inc(s_act, 1)
                        led.inc(s_act, 1)
                    sc(_ev)
                    at = led.val(s_act)

                    def _rp(g, sl=sl, boff=boff, at=at):
                        g.wait_ge(s_act, at)
                        for e in range(E):
                            g.dma_start(
                                out=u2T[:, e * T_ALL + boff + sl * 128:
                                        e * T_ALL + boff + sl * 128 + 128],
                                in_=s_bf[16 * e:16 * e + 16, 0:128]
                            ).then_inc(s_gd, 16)
                            led.inc(s_gd, 16)
                    gp(_rp)
                    gd_prev[0] = led.val(s_gd)
                prev_tr[0] = led.val(s_pe)
            gdf = led.val(s_gd)

            def _gdrain(t, gdf=gdf):
                t.wait_ge(s_gd, gdf)
            pe(_gdrain)
            barrier()

        def dloop(xsrc_ap, dst_ap, dst_shift, blocks):
            W2SL = [REGD[:, 0:11264], REGC[:, 0:11264]]
            B2DG = [REGD[32:48, 11264:13312], REGC[32:48, 11264:13312]]
            ADG = [REGD[:, 13312:15360], REGD[:, 15360:17408],
                   REGA[:, 10240:12288], REGA[:, 12288:14336]]
            XIO = [REGF[:, 0:1024], REGF[:, 1024:2048]]
            XOUT = [REGF[:, 2048:3072], REGF[:, 3072:4096]]
            PS1 = [ph1A, ph1B]
            PS3 = [ph3A, ph3B]
            ASEM = [s_xa, s_xb, s_xc, s_xd]
            NGR = CF // 4
            h_w2t, h_xt, h_mmlast, h_res, h_st = {}, {}, {}, {}, {}
            mm_all = []
            it = -1
            for bi, (boff, w) in enumerate(blocks):
                for dg in range(KD // 2):
                    it += 1
                    d0 = dg * 2
                    par = it % 2

                    def _wlg(g, d0=d0, par=par, it=it):
                        if it >= 2:
                            g.wait_ge(s_pe, h_mmlast[it - 2])
                        g.dma_start(out=W2SL[par][:, 0:F],
                                    in_=w2r[d0]).then_inc(s_wd, 16)
                        g.dma_start(out=W2SL[par][:, F:2 * F],
                                    in_=w2r[d0 + 1]).then_inc(s_wd, 16)
                        g.dma_start(out=B2DG[par],
                                    in_=b2pi[:, d0 * E * 128:(d0 + 2) * E * 128]
                                    ).then_inc(s_wd, 16)
                        led.inc(s_wd, 48)
                    gp(_wlg)
                    h_w2t[it] = led.val(s_wd)

                    def _wl(s2, d0=d0, boff=boff, w=w, par=par, it=it):
                        if it >= 2:
                            s2.wait_ge(s_dve, h_res[it - 2])
                        s2.dma_start(
                            out=XIO[par].rearrange("p (g t) -> p g t", g=2)
                            [:, :, 0:w],
                            in_=xsrc_ap[d0 * 128:(d0 + 2) * 128, boff:boff + w]
                            .rearrange("(g p) t -> p g t", p=128)
                        ).then_inc(s_wd, 16)
                        led.inc(s_wd, 16)
                    sy(_wl)
                    h_xt[it] = led.val(s_wd)

                    for gidx in range(NGR):
                        ci = gidx * 4
                        gg = it * NGR + gidx
                        slot = gg % 4
                        asem = ASEM[slot]

                        def _al(s2, ci=ci, boff=boff, w=w, asem=asem,
                                slot=slot, gg=gg):
                            if gg >= 4:
                                s2.wait_ge(s_pe, mm_all[gg - 4])
                            s2.dma_start(
                                out=ADG[slot]
                                .rearrange("p (cc t) -> p cc t", cc=4)
                                [:, :, 0:w],
                                in_=amx[ci:ci + 4, :, boff:boff + w]
                                .rearrange("cc p t -> p cc t")
                            ).then_inc(asem, 16)
                            led.inc(asem, 16)
                        sy(_al)
                        xdt = led.val(asem)

                        def _mm(t, ci=ci, gidx=gidx, w=w, xdt=xdt, asem=asem,
                                par=par, it=it, slot=slot):
                            t.wait_ge(asem, xdt)
                            if gidx == 0:
                                t.wait_ge(s_wd, h_w2t[it])
                                if it >= 2:
                                    t.wait_ge(s_dve, h_res[it - 2])
                            for cj in range(4):
                                cc = ci + cj
                                rhs = ADG[slot][:, cj * 512:cj * 512 + w]
                                t.matmul(PS1[par][:, 0:w],
                                         W2SL[par][:, cc * 128:(cc + 1) * 128],
                                         rhs, start=(cc == 0), stop=False)
                                mm2 = t.matmul(
                                    PS3[par][:, 0:w],
                                    W2SL[par][:, F + cc * 128:F + (cc + 1) * 128],
                                    rhs, start=(cc == 0), stop=False)
                            mm2.then_inc(s_pe, 1)
                            mm_all.append(led.inc(s_pe, 1))
                        pe(_mm)

                    def _lora(t, boff=boff, w=w, par=par):
                        for e in range(E):
                            t.matmul(PS1[par][:, 0:w],
                                     B2DG[par][:, e * 128:(e + 1) * 128],
                                     u2T[:, e * T_ALL + boff:e * T_ALL + boff + w],
                                     start=False, stop=(e == E - 1))
                            mm2 = t.matmul(
                                PS3[par][:, 0:w],
                                B2DG[par][:, (E + e) * 128:(E + e + 1) * 128],
                                u2T[:, e * T_ALL + boff:e * T_ALL + boff + w],
                                start=False, stop=(e == E - 1))
                        mm2.then_inc(s_pe, 1)
                        led.inc(s_pe, 1)
                    pe(_lora)
                    h_mmlast[it] = led.val(s_pe)

                    def _res(v, w=w, par=par, it=it):
                        v.wait_ge(s_pe, h_mmlast[it])
                        v.wait_ge(s_wd, h_xt[it])
                        if it >= 2:
                            v.wait_ge(s_od, h_st[it - 2])
                        v.scalar_tensor_tensor(XOUT[par][:, 0:w],
                                               PS1[par][:, 0:w], 0.5,
                                               XIO[par][:, 0:w],
                                               ALU.mult, ALU.add)
                        v.scalar_tensor_tensor(XOUT[par][:, 512:512 + w],
                                               PS3[par][:, 0:w], 0.5,
                                               XIO[par][:, 512:512 + w],
                                               ALU.mult, ALU.add
                                               ).then_inc(s_dve, 1)
                        led.inc(s_dve, 1)
                    ve(_res)
                    h_res[it] = led.val(s_dve)

                    def _st(s2, d0=d0, boff=boff, w=w, par=par, it=it,
                            dst_shift=dst_shift):
                        s2.wait_ge(s_dve, h_res[it])
                        s2.dma_start(
                            out=dst_ap[d0 * 128:(d0 + 2) * 128,
                                       boff - dst_shift:boff - dst_shift + w]
                            .rearrange("(g p) t -> p g t", p=128),
                            in_=XOUT[par].rearrange("p (g t) -> p g t", g=2)
                            [:, :, 0:w]
                        ).then_inc(s_od, 16)
                        led.inc(s_od, 16)
                    sy(_st)
                    h_st[it] = led.val(s_od)
            barrier()

        def qkv_phase():
            wdts = {}
            pe_done = {}
            ev_hist = {}

            def loadw(j, wait_pe):
                wsem = s_wa if j % 2 == 0 else s_wb

                def _l(g, j=j, wait_pe=wait_pe, wsem=wsem):
                    if wait_pe is not None:
                        g.wait_ge(s_pe, wait_pe)
                    g.dma_start(out=wqs[:, (j % 2) * 2048:(j % 2) * 2048 + 2048],
                                 in_=wq[j]).then_inc(wsem, 16)
                    led.inc(wsem, 16)
                gp(_l)
                wdts[j] = (wsem, led.val(wsem))

            loadw(0, None)
            for j in range(48):
                if j + 1 < 48:
                    loadw(j + 1, pe_done.get(j - 1))
                for sbi, (boff, w) in enumerate(BLK1):
                    gidx = j * 3 + sbi
                    ps = ph1A if gidx % 2 == 0 else ph1B
                    atprev = ev_hist.get(gidx - 2, 0)

                    def _mm(t, j=j, boff=boff, w=w, ps=ps, atprev=atprev):
                        t.wait_ge(wdts[j][0], wdts[j][1])
                        t.wait_ge(s_act, atprev)
                        for k in range(KD):
                            mm = t.matmul(ps[:, 0:w],
                                          wqs[:, (j % 2) * 2048 + k * 128:
                                              (j % 2) * 2048 + (k + 1) * 128],
                                          xn_all[:, k * T_ALL + boff:
                                                 k * T_ALL + boff + w],
                                          start=(k == 0), stop=(k == KD - 1))
                            if k == KD - 1:
                                mm.then_inc(s_pe, 1)
                        led.inc(s_pe, 1)
                    pe(_mm)
                    pt = led.val(s_pe)

                    odp = led.val(s_od)
                    if j < 16:
                        def _ev(s, j=j, w=w, pt=pt, ps=ps, odp=odp):
                            s.wait_ge(s_pe, pt)
                            s.wait_ge(s_od, odp)
                            s.activation(qkst[:, 0:w], ps[:, 0:w], AF.Identity,
                                         bias=battn_q[:, j:j + 1],
                                         scale=0.125).then_inc(s_act, 1)
                            led.inc(s_act, 1)
                        sc(_ev)
                        at = led.val(s_act)
                        ev_hist[gidx] = at

                        def _qst(s2, j=j, boff=boff, w=w, at=at):
                            s2.wait_ge(s_act, at)
                            s2.dma_start(out=qd[j * 128:(j + 1) * 128,
                                                boff:boff + w],
                                         in_=qkst[:, 0:w]).then_inc(s_od, 16)
                            led.inc(s_od, 16)
                        sy(_qst)
                    elif j < 32:
                        def _ev(s, j=j, w=w, pt=pt, ps=ps, odp=odp):
                            s.wait_ge(s_pe, pt)
                            s.wait_ge(s_od, odp)
                            s.activation(qkst[:, 0:w], ps[:, 0:w], AF.Identity,
                                         bias=battn_s[:, j:j + 1]
                                         ).then_inc(s_act, 1)
                            led.inc(s_act, 1)
                        sc(_ev)
                        at = led.val(s_act)
                        ev_hist[gidx] = at

                        def _kst(s2, j=j, boff=boff, w=w, at=at):
                            s2.wait_ge(s_act, at)
                            ha = 2 * (j - 16)
                            s2.dma_start(out=kdr[0:64, ha * T_ALL + boff:
                                                ha * T_ALL + boff + w],
                                         in_=qkst[0:64, 0:w]).then_inc(s_od, 16)
                            s2.dma_start(out=kdr[0:64, (ha + 1) * T_ALL + boff:
                                                (ha + 1) * T_ALL + boff + w],
                                         in_=qkst[64:128, 0:w]).then_inc(s_od, 16)
                            led.inc(s_od, 32)
                        sy(_kst)
                    else:
                        def _ev(s, j=j, w=w, pt=pt, ps=ps, odp=odp):
                            s.wait_ge(s_pe, pt)
                            s.wait_ge(s_od, odp)
                            s.activation(qkst[:, 0:w], ps[:, 0:w], AF.Identity,
                                         bias=battn_s[:, j:j + 1]
                                         ).then_inc(s_act, 1)
                            led.inc(s_act, 1)
                        sc(_ev)
                        at0 = led.val(s_act)
                        ev_hist[gidx] = at0
                        for ts in range(w // 128):
                            atprev2 = led.val(s_act)

                            def _tr(t, ts=ts, at0=at0, atprev2=atprev2):
                                t.wait_ge(s_act, at0 if ts == 0 else atprev2)
                                t.matmul(pamix[:, 0:128],
                                         qkst[:, ts * 128:(ts + 1) * 128],
                                         id_b[:, :], start=True,
                                         stop=True).then_inc(s_pe, 1)
                                led.inc(s_pe, 1)
                            pe(_tr)
                            pt2 = led.val(s_pe)

                            odp2 = led.val(s_od)

                            def _e2(s, pt2=pt2, odp2=odp2):
                                s.wait_ge(s_pe, pt2)
                                s.wait_ge(s_od, odp2)
                                s.activation(qkst[:, 512:640], pamix[:, 0:128],
                                             AF.Identity).then_inc(s_act, 1)
                                led.inc(s_act, 1)
                            sc(_e2)
                            at2 = led.val(s_act)

                            def _vst(s2, j=j, boff=boff, ts=ts, at2=at2):
                                s2.wait_ge(s_act, at2)
                                s2.dma_start(
                                    out=vdr[boff + ts * 128:boff + ts * 128 + 128,
                                            (j - 32) * 128:(j - 31) * 128],
                                    in_=qkst[:, 512:640]).then_inc(s_od, 16)
                                led.inc(s_od, 16)
                            sy(_vst)
                pe_done[j] = led.val(s_pe)
            barrier()

        def attn_phase():
            def _qa64(g):
                g.dma_start(out=qaug64[:, :], in_=qaug).then_inc(s_gd, 16)
                led.inc(s_gd, 16)
            gp(_qa64)
            qa64_t = led.val(s_gd)
            odt0 = led.val(s_od)

            def _ms(v, odt0=odt0):
                v.wait_ge(s_od, odt0)
                vr = vring.rearrange("p (s h x) -> p s h x", s=4, h=NH)
                v.memset(vr[:, :, :, 64:65], 1.0).then_inc(s_dve, 1)
                led.inc(s_dve, 1)
            ve(_ms)
            ms_t = led.val(s_dve)
            ring_t, q_t = {}, {}
            ST_PS = [ph1A, ph3A, ph1B, ph3B]
            PV_PS = [pamix, pu2a]
            h_st, h_mask, h_ex, h_pv, h_rc, h_sc, h_tb, h_fin = \
                {}, {}, {}, {}, {}, {}, {}, {}

            def load_ring(cl, wait_pe):
                rsem = [s_xa, s_xb, s_xc, s_xd][cl % 4]

                def _l(s2, cl=cl, wait_pe=wait_pe, rsem=rsem):
                    if wait_pe is not None:
                        s2.wait_ge(s_pe, wait_pe)
                    else:
                        s2.wait_ge(s_od, odt0)
                    slot = cl % 4
                    s2.dma_start(
                        out=kring[:, slot * NH * 128:(slot + 1) * NH * 128]
                        .rearrange("p (h t) -> p h t", h=NH),
                        in_=kdr.rearrange("p (h t) -> p h t", h=NH)
                        [:, :, cl * 128:(cl + 1) * 128]).then_inc(rsem, 16)
                    s2.dma_start(
                        out=vring[:, slot * NH * 65:(slot + 1) * NH * 65]
                        .rearrange("p (h x) -> p h x", h=NH)[:, :, 0:64],
                        in_=vdr[cl * 128:(cl + 1) * 128, :]
                        .rearrange("p (h x) -> p h x", h=NH)).then_inc(rsem, 16)
                sy(_l)
                led.inc(rsem, 32)
                ring_t[cl] = (rsem, led.val(rsem))

            def load_q(qt, wait_pe):
                qsem = s_oa if qt % 2 == 0 else s_ob

                def _l(s2, qt=qt, wait_pe=wait_pe, qsem=qsem):
                    if wait_pe is not None:
                        s2.wait_ge(s_pe, wait_pe)
                    else:
                        s2.wait_ge(s_od, odt0)
                    cl = qt + 2
                    s2.dma_start(
                        out=q_sb[:, (qt % 2) * 4096:(qt % 2) * 4096 + 4096]
                        .rearrange("p (h t) -> p h t", h=NH),
                        in_=qd[:, cl * 128:(cl + 1) * 128]
                        .rearrange("(h p) t -> p h t", p=64)).then_inc(qsem, 16)
                sy(_l)
                led.inc(qsem, 16)
                q_t[qt] = (qsem, led.val(qsem))

            for cl in range(2):
                load_ring(cl, None)
            load_q(0, None)
            NGQ = NQ * NH

            def emit_st(t, gq):
                qt, h = gq // NH, gq % NH
                cl = qt + 2
                stps = ST_PS[gq % 4]
                qh = q_sb[:, (qt % 2) * 4096 + h * 128:
                          (qt % 2) * 4096 + h * 128 + 128]
                for cc in range(max(0, cl - 2), cl + 1):
                    t.wait_ge(ring_t[cc][0], ring_t[cc][1])
                t.wait_ge(q_t[qt][0], q_t[qt][1])
                if gq == 0:
                    t.wait_ge(s_gd, qa64_t)
                if gq >= 4:
                    t.wait_ge(s_act, h_ex[gq - 4])
                for s in range(NSLOT):
                    slot = (cl - 2 + s) % 4
                    kcol = slot * NH * 128 + h * 128
                    t.matmul(stps[:, s * 128:(s + 1) * 128],
                             kring[0:64, kcol:kcol + 128], qh,
                             start=(s == 0), stop=False)
                    mm = t.matmul(stps[:, s * 128:(s + 1) * 128],
                                  kring[64:65, kcol:kcol + 128],
                                  qaug64[0:1, qt * 128:(qt + 1) * 128],
                                  start=False, stop=(s == NSLOT - 1))
                mm.then_inc(s_pe, 1)
                h_st[gq] = led.inc(s_pe, 1)

            def emit_mask(v, gq):
                stps = ST_PS[gq % 4]
                v.wait_ge(s_pe, h_st[gq])
                v.tensor_add(stps[:, 256:384], stps[:, 256:384],
                             trit_s[:, :]).then_inc(s_dve, 1)
                h_mask[gq] = led.inc(s_dve, 1)

            def emit_exp(s, gq):
                qt = gq // NH
                stps = ST_PS[gq % 4]
                s.wait_ge(s_dve, h_mask[gq])
                if gq >= 4:
                    s.wait_ge(s_pe, h_pv[gq - 4])
                for sl in range(NSLOT):
                    ai = s.activation(
                        pt_bf[:, (gq % 4) * 384 + sl * 128:
                              (gq % 4) * 384 + (sl + 1) * 128],
                        stps[:, sl * 128:(sl + 1) * 128], AF.Exp,
                        bias=stb_s[:, qt * NSLOT + sl:qt * NSLOT + sl + 1])
                ai.then_inc(s_act, 1)
                h_ex[gq] = led.inc(s_act, 1)

            def emit_pv(t, gq):
                qt, h = gq // NH, gq % NH
                cl = qt + 2
                pvps = PV_PS[gq % 2]
                t.wait_ge(s_act, h_ex[gq])
                if gq == 0:
                    t.wait_ge(s_dve, ms_t)
                if gq >= 2:
                    t.wait_ge(s_dve, h_sc[gq - 2])
                for s in range(NSLOT):
                    slot = (cl - 2 + s) % 4
                    mm = t.matmul(pvps[0:128, 0:65],
                                  pt_bf[:, (gq % 4) * 384 + s * 128:
                                        (gq % 4) * 384 + (s + 1) * 128],
                                  vring[:, slot * NH * 65 + h * 65:
                                        slot * NH * 65 + (h + 1) * 65],
                                  start=(s == 0), stop=(s == NSLOT - 1))
                mm.then_inc(s_pe, 1)
                h_pv[gq] = led.inc(s_pe, 1)

            def emit_rc(v, gq):
                pvps = PV_PS[gq % 2]
                v.wait_ge(s_pe, h_pv[gq])
                if gq >= 4:
                    v.wait_ge(s_dve, h_sc[gq - 4])
                v.reciprocal(grow[:, 33 + (gq % 4):34 + (gq % 4)],
                             pvps[:, 64:65]).then_inc(s_dve, 1)
                h_rc[gq] = led.inc(s_dve, 1)

            def emit_scale(v, gq):
                pvps = PV_PS[gq % 2]
                v.wait_ge(s_dve, h_rc[gq])
                if gq >= 4:
                    v.wait_ge(s_pe, h_tb[(gq - 4) | 1])
                v.tensor_single_scalar(
                    tm_b[:, (gq % 4) * 64:(gq % 4) * 64 + 64],
                    pvps[:, 0:64],
                    grow[:, 33 + (gq % 4):34 + (gq % 4)],
                    ALU.mult).then_inc(s_dve, 1)
                h_sc[gq] = led.inc(s_dve, 1)

            def emit_tb(t, gq):
                t.wait_ge(s_dve, h_sc[gq])
                if gq >= 2:
                    t.wait_ge(s_act, h_fin[gq - 2])
                pr = (gq // 2) % 2
                t.matmul(pu2c[:, 0:128],
                         tm_b[:, pr * 128:pr * 128 + 128],
                         id_b[:, :], start=True,
                         stop=True).then_inc(s_pe, 1)
                h_tb[gq] = h_tb[gq - 1] = led.inc(s_pe, 1)

            def emit_fin(s, gq):
                qt, h = gq // NH, gq % NH
                s.wait_ge(s_pe, h_tb[gq])
                hc = h // 2
                s.activation(attn_sb[:, hc * T_OWN + qt * 128:
                                     hc * T_OWN + qt * 128 + 128],
                             pu2c[:, 0:128],
                             AF.Identity).then_inc(s_act, 1)
                h_fin[gq] = h_fin[gq - 1] = led.inc(s_act, 1)

            def tail(gq):
                pe(lambda t, gq=gq: emit_pv(t, gq))
                ve(lambda v, gq=gq: emit_rc(v, gq))
                ve(lambda v, gq=gq: emit_scale(v, gq))
                if gq % 2 == 1:
                    pe(lambda t, gq=gq: emit_tb(t, gq))
                    sc(lambda s, gq=gq: emit_fin(s, gq))

            for gq in range(NGQ):
                if gq % NH == 0:
                    qt = gq // NH
                    load_ring(qt + 2,
                              h_pv[(qt - 2) * NH + NH - 1] if qt >= 2 else None)
                    if qt + 1 < NQ:
                        load_q(qt + 1,
                               h_st[(qt - 1) * NH + NH - 1] if qt >= 1 else None)
                pe(lambda t, gq=gq: emit_st(t, gq))
                ve(lambda v, gq=gq: emit_mask(v, gq))
                sc(lambda s, gq=gq: emit_exp(s, gq))
                if gq >= 1:
                    tail(gq - 1)
            tail(NGQ - 1)
            barrier()

        def proj_phase():
            wdts = {}
            pe_done = {}
            pev_hist = {}

            def loadw(dj, wait_pe):
                wsem = s_wa if dj % 2 == 0 else s_wb

                def _l(g, dj=dj, wait_pe=wait_pe, wsem=wsem):
                    if wait_pe is not None:
                        g.wait_ge(s_pe, wait_pe)
                    g.dma_start(out=wqs[:, (dj % 2) * 2048:(dj % 2) * 2048 + 2048],
                                 in_=wp[dj]).then_inc(wsem, 16)
                    led.inc(wsem, 16)
                gp(_l)
                wdts[dj] = (wsem, led.val(wsem))

            loadw(0, None)
            for dj in range(KD):
                if dj + 1 < KD:
                    loadw(dj + 1, pe_done.get(dj - 1))
                for sbi, (boff, w) in enumerate(BLK2):
                    ps = ph1A if (dj * 2 + sbi) % 2 == 0 else ph1B
                    oprev = led.val(s_od)

                    def _xl(s2, dj=dj, boff=boff, w=w, sbi=sbi, oprev=oprev):
                        s2.wait_ge(s_od, oprev)
                        s2.dma_start(out=xpj[:, (sbi % 2) * 512:
                                             (sbi % 2) * 512 + w],
                                     in_=xres[dj * 128:(dj + 1) * 128,
                                              boff:boff + w]).then_inc(s_xd, 16)
                        led.inc(s_xd, 16)
                    sy(_xl)
                    xdt = led.val(s_xd)

                    def _mm(t, dj=dj, boff=boff, w=w, ps=ps,
                            dvp=pev_hist.get(dj * 2 + sbi - 2, 0)):
                        t.wait_ge(wdts[dj][0], wdts[dj][1])
                        t.wait_ge(s_dve, dvp)
                        for k in range(KD):
                            mm = t.matmul(ps[:, 0:w],
                                          wqs[:, (dj % 2) * 2048 + k * 128:
                                              (dj % 2) * 2048 + (k + 1) * 128],
                                          attn_sb[:, k * T_OWN + boff - HALO:
                                                  k * T_OWN + boff - HALO + w],
                                          start=(k == 0), stop=(k == KD - 1))
                            if k == KD - 1:
                                mm.then_inc(s_pe, 1)
                        led.inc(s_pe, 1)
                    pe(_mm)
                    pt = led.val(s_pe)

                    def _ev(v, dj=dj, w=w, sbi=sbi, pt=pt, ps=ps, xdt=xdt):
                        v.wait_ge(s_pe, pt)
                        v.wait_ge(s_xd, xdt)
                        v.scalar_tensor_tensor(
                            xfp[:, 0:w], ps[:, 0:w], bproj_s[:, dj:dj + 1],
                            xpj[:, (sbi % 2) * 512:(sbi % 2) * 512 + w],
                            ALU.add, ALU.add).then_inc(s_dve, 1)
                        led.inc(s_dve, 1)
                    ve(_ev)
                    dvt = led.val(s_dve)
                    pev_hist[dj * 2 + sbi] = dvt

                    def _st(s2, dj=dj, boff=boff, w=w, dvt=dvt):
                        s2.wait_ge(s_dve, dvt)
                        s2.dma_start(out=xres[dj * 128:(dj + 1) * 128,
                                              boff:boff + w],
                                     in_=xfp[:, 0:w]).then_inc(s_od, 16)
                        led.inc(s_od, 16)
                    sy(_st)
                pe_done[dj] = led.val(s_pe)
            odf = led.val(s_od)

            def _oddrain2(t, odf=odf):
                t.wait_ge(s_od, odf)
            pe(_oddrain2)
            barrier()

        def halo_exchange():
            def _hx(g):
                g.dma_start(out=ccin.ap()[:, :],
                            in_=xres[:, T_ALL - HALO:T_ALL]).then_inc(s_gd, 16)
                g.wait_ge(s_gd, led.inc(s_gd, 16))
                g.collective_compute(
                    "AllGather", ALU.bypass,
                    replica_groups=[list(range(8))],
                    ins=[ccin.ap().opt()],
                    outs=[ccout.ap().opt()],
                ).then_inc(s_cast1, 16)
                led.inc(s_cast1, 16)
            gp(_hx)
            ccv = ccout.ap().rearrange("(r kk p) t -> p r kk t", p=128, kk=KD)
            hsel = REGD[:, 0:4096]
            h_in, h_bl, h_out = {}, {}, {}
            for k in range(KD):
                def _hi(g, k=k):
                    if k == 0:
                        g.wait_ge(s_cast1, led.val(s_cast1))
                    if k >= 2:
                        g.wait_ge(s_dve, h_bl[k - 2])
                    g.dma_start(out=hsel[:, (k % 2) * 2048:(k % 2) * 2048 + 2048]
                                .rearrange("p (r t) -> p r t", r=8),
                                in_=ccv[:, :, k, :]).then_inc(s_gd, 16)
                    h_in[k] = led.inc(s_gd, 16)
                gp(_hi)

                def _hb(v, k=k):
                    v.wait_ge(s_gd, h_in[k])
                    if k >= 2:
                        v.wait_ge(s_od, h_out[k - 2])
                    acc = xln if k % 2 == 0 else xln2
                    hs = hsel[:, (k % 2) * 2048:(k % 2) * 2048 + 2048]
                    v.tensor_single_scalar(acc[:, 0:256], hs[:, 0:256],
                                           selv_s[:, 0:1], ALU.mult)
                    for r in range(1, 8):
                        ai = v.scalar_tensor_tensor(
                            acc[:, 0:256], hs[:, r * 256:(r + 1) * 256],
                            selv_s[:, r:r + 1], acc[:, 0:256],
                            ALU.mult, ALU.add)
                    ai.then_inc(s_dve, 1)
                    h_bl[k] = led.inc(s_dve, 1)
                ve(_hb)

                def _ho(s2, k=k):
                    s2.wait_ge(s_dve, h_bl[k])
                    acc = xln if k % 2 == 0 else xln2
                    s2.dma_start(out=xres[k * 128:(k + 1) * 128, 0:HALO],
                                 in_=acc[:, 0:256]).then_inc(s_od, 16)
                    h_out[k] = led.inc(s_od, 16)
                sy(_ho)
            odf = led.val(s_od)

            def _hdrain(t, odf=odf):
                t.wait_ge(s_od, odf)
            pe(_hdrain)
            barrier()

        ln_phase(x0, BLK1)

        def _castwait(t, gdt=gdt):
            t.wait_ge(s_gd, gdt)
        pe(_castwait)
        barrier()
        gate_phase(list(range(NT)))
        uprod(BLK1)
        floop(BLK1)
        u2fin(BLK1)
        dloop(x0, xres, 0, BLK1)
        ln_phase(xres, BLK1)
        qkv_phase()
        attn_phase()
        proj_phase()
        ln_phase(xres, BLK2)
        gate_phase(list(range(2, NT)))
        uprod(BLK2)
        floop(BLK2)
        u2fin(BLK2)
        dloop(xres, y, HALO, BLK2)

        fin = led.val(s_od)

        def _fin(s2, fin=fin):
            s2.wait_ge(s_od, fin)
        sy(_fin)

        B.tensor(R_pe.replay)
        B.scalar(R_sc.replay)
        B.vector(R_ve.replay)
        B.gpsimd(R_gp.replay)
        B.sync(R_sy.replay)

    return nc


def _prep_weights(i):
    w1, w3, w2 = i["L0_ffn_w1"], i["L0_ffn_w3"], i["L0_ffn_w2"]
    a1, b1 = i["L0_ffn_a1"], i["L0_ffn_b1"]
    a3, b3 = i["L0_ffn_a3"], i["L0_ffn_b3"]
    a2, b2 = i["L0_ffn_a2"], i["L0_ffn_b2"]
    gw = i["L0_ffn_gate_w"]
    W_attn, b_attn = i["L0_attn_W_attn"], i["L0_attn_b_attn"]
    W_proj, b_proj = i["L0_attn_W_proj"], i["L0_attn_b_proj"]

    c = np.ascontiguousarray
    w1r = np.asarray(w1).reshape(CF, 128, KD, 128).transpose(0, 3, 2, 1)
    w3r = np.asarray(w3).reshape(CF, 128, KD, 128).transpose(0, 3, 2, 1)
    w13v = c(np.stack([w1r, w3r], axis=2).reshape(CF, 128, 4096))
    b1r = np.asarray(b1).reshape(E, CF, 128, R).transpose(1, 3, 0, 2)
    b3r = np.asarray(b3).reshape(E, CF, 128, R).transpose(1, 3, 0, 2)
    b13v = c(np.stack([b1r, b3r], axis=2).reshape(CF, 16, 2048))
    a1r = np.asarray(a1).reshape(E, R, KD, 128).transpose(3, 2, 0, 1)
    a3r = np.asarray(a3).reshape(E, R, KD, 128).transpose(3, 2, 0, 1)
    a13v = c(np.stack([a1r, a3r], axis=1).reshape(128, 4096))
    a2v = c(np.asarray(a2).reshape(E, R, CF, 128)
            .transpose(3, 2, 0, 1).reshape(128, CF * 128))
    b2v = c(np.asarray(b2).reshape(E, KD, 128, R)
            .transpose(3, 1, 0, 2).reshape(16, KD * E * 128))
    gwv = c(np.asarray(gw).reshape(E, KD, 128).transpose(2, 1, 0)
            .reshape(128, KD * E))
    wqv = c(np.asarray(W_attn).reshape(KD, 128, 48, 128)
            .transpose(2, 1, 0, 3).reshape(48, 128, 2048))
    wpv = c(np.asarray(W_proj).reshape(KD, 128, KD, 128)
            .transpose(2, 1, 0, 3).reshape(KD, 128, 2048))
    w2v = c(np.asarray(w2).reshape(KD, 128, CF, 128)
            .transpose(0, 3, 2, 1).reshape(KD, 128, F))
    bav = c(np.asarray(b_attn).reshape(48, 128).T)
    bpv = c(np.asarray(b_proj).reshape(KD, 128).T)
    return dict(w13=w13v, b13=b13v, a13=a13v, a2pi=a2v, b2pi=b2v, gwt=gwv,
                wq=wqv, wp=wpv, w2r=w2v, battn=bav, bproj=bpv)


def _prep_meta(core):
    stbv = np.zeros((NQ * NSLOT, 128), np.float32)
    for qt in range(NQ):
        for s in range(NSLOT):
            cl = qt + s
            col = M_AL * (cl * 128 + np.arange(128, dtype=np.float32))
            if core == 0 and cl < 2:
                col = col + NEGM
            stbv[qt * NSLOT + s] = col
    qau = np.zeros((2, NQ * 128), np.float32)
    for qt in range(NQ):
        qau[0, qt * 128:(qt + 1) * 128] = -M_AL * (
            (qt + 2) * 128 + np.arange(128, dtype=np.float32)) - 12.0
    qau[1] = 1.0
    mk = np.zeros((2, T_ALL), np.float32)
    mk[0] = 1.0
    mk[1] = M_AL * np.arange(T_ALL, dtype=np.float32)
    tri_v = np.where(np.arange(128)[None, :] > np.arange(128)[:, None],
                     np.float32(NEGM), np.float32(0.0))
    sel = np.zeros((128, 8), np.float32)
    if core > 0:
        sel[:, core - 1] = 1.0
    return dict(stb=np.ascontiguousarray(stbv.T), qaug=qau, mkrow=mk,
                ident=np.eye(128, dtype=np.float32),
                tri=np.ascontiguousarray(tri_v),
                trit=np.ascontiguousarray(tri_v.T),
                onesf=np.ones((1, 128), np.float32),
                selv=sel)


def kernel(**inputs):
    x = np.asarray(inputs["L0_attn_hidden_states"], np.float32)[0]
    xT = np.ascontiguousarray(x.T)
    wdict = _prep_weights(inputs)
    nc = build_program()
    in_maps = []
    for corei in range(8):
        g0 = corei * T_OWN
        if corei == 0:
            halo = np.zeros((D, HALO), np.float32)
        else:
            halo = xT[:, g0 - HALO:g0]
        x0c = np.ascontiguousarray(
            np.concatenate([halo, xT[:, g0:g0 + T_OWN]], axis=1))
        m = dict(wdict)
        m.update(_prep_meta(corei))
        m["x0"] = x0c
        in_maps.append(m)
    res = run_bass_kernel_spmd(nc, in_maps, core_ids=list(range(8)))
    out = np.empty((8192, D), np.float32)
    for corei in range(8):
        out[corei * T_OWN:(corei + 1) * T_OWN, :] = res.results[corei]["y"].T
    return out.reshape(1, 8192, D)
